# version 02
# baseline (speedup 1.0000x reference)
import contextlib

import numpy as np
import ml_dtypes

import concourse.bass as bass
import concourse.mybir as mybir
from concourse.bass_utils import run_bass_kernel_spmd

F32 = mybir.dt.float32
BF16 = mybir.dt.bfloat16
I32 = mybir.dt.int32
AF = mybir.ActivationFunctionType
ALU = mybir.AluOpType
AX = mybir.AxisListType
BF = ml_dtypes.bfloat16

NCORE = 8
SEQ = 4096
HID = 2048
HQ = 32
DH = 64
INTER = 5632
NTILE = 36
KVROWS = NTILE * 128
TOWN = 1024
HT = 16
ITL = 44
EPS = 1e-5

ENG = ("sync", "gpsimd", "tensor", "vector", "scalar")

ACOLS = 101888
XNT_OWN = 0
ATTN_OT = 16384
H_OFF = 32768
KTA_O = 32768
KTB_O = 37376
VS_O = 41984
CTOK_O = 51344
MASK_O = 53648
WKV_O = 55696
XN_O = 63888
XNTT_O = 70032
KTOK_O = 74128
XBUF_O = 74640
QTOK_O = 55696
QT_O = 72080
WQC_O = 88464
RTMPA_O = 88464
RTMPB_O = 89488
QSQT_O = 90512
ATT_O = 92560
ZERO_O = 101264
RBFF_O = 95120
RBF2_O = 97168
RBSB_O = 98192
ATMP_O = 100240
KT2A_O = 55696
KT2B_O = 60304
GW_O = 77824
UW_O = 86016
GST_O = 94208
UST_O = 95232
XQ_O = 0
DW_O = 77824
WO_O = 89088
OST_O = 93184


class Prog:
    def __init__(self):
        self.ops = {e: [] for e in ENG}
        self.cnt = {}

    def sem(self, name):
        self.cnt.setdefault(name, 0)

    def emit(self, eng, fn, wait=(), inc=None, n=1):
        wait = tuple((s, v) for (s, v) in wait if s is not None and v > 0)
        ev = None
        if inc is not None:
            self.cnt[inc] += n
            ev = (inc, self.cnt[inc])
        self.ops[eng].append((fn, wait, inc, n))
        return ev

    def replay(self, eng, e, sems):
        last = {s: 0 for s in self.cnt}
        for fn, wait, inc, n in self.ops[eng]:
            for s, v in wait:
                if last[s] < v:
                    e.wait_ge(sems[s], v)
                    last[s] = v
            ins = fn(e) if fn is not None else None
            if inc is not None and ins is not None:
                ins.then_inc(sems[inc], n)


def emax(*evs):
    best = {}
    for ev in evs:
        if not ev:
            continue
        if isinstance(ev, list):
            for s, v in ev:
                if best.get(s, 0) < v:
                    best[s] = v
        else:
            s, v = ev
            if best.get(s, 0) < v:
                best[s] = v
    return list(best.items())


def build_program(dbg=()):
    nc = bass.Bass("TRN2", target_bir_lowering=False)

    xkv = nc.dram_tensor("xkv", [KVROWS, HID], F32, kind="ExternalInput").ap()
    wkv_h = nc.dram_tensor("wkv_h", [128, 8192], F32, kind="ExternalInput").ap()
    wq_h = nc.dram_tensor("wq_h", [8, 128, 4096], F32, kind="ExternalInput").ap()
    wo_h = nc.dram_tensor("wo_h", [16, 128, 2048], F32, kind="ExternalInput").ap()
    gw_h = nc.dram_tensor("gw_h", [22, 128, 4096], F32, kind="ExternalInput").ap()
    uw_h = nc.dram_tensor("uw_h", [22, 128, 4096], F32, kind="ExternalInput").ap()
    dw_h = nc.dram_tensor("dw_h", [16, 128, 5632], F32, kind="ExternalInput").ap()
    gs_h = nc.dram_tensor("gs_h", [128, 44], F32, kind="ExternalInput").ap()
    us_h = nc.dram_tensor("us_h", [128, 44], F32, kind="ExternalInput").ap()
    ds_h = nc.dram_tensor("ds_h", [128, 16], F32, kind="ExternalInput").ap()
    ctok_h = nc.dram_tensor("ctok_h", [128, NTILE * 64], F32, kind="ExternalInput").ap()
    mask_h = nc.dram_tensor("mask_h", [128, 2048], F32, kind="ExternalInput").ap()
    sel_h = nc.dram_tensor("sel_h", [1, 8], I32, kind="ExternalInput").ap()
    idb_h = nc.dram_tensor("idb_h", [128, 128], F32, kind="ExternalInput").ap()
    idf_h = nc.dram_tensor("idf_h", [128, 128], F32, kind="ExternalInput").ap()
    pones_h = nc.dram_tensor("pones_h", [1, 128], F32, kind="ExternalInput").ap()
    mones_h = nc.dram_tensor("mones_h", [1, 128], F32, kind="ExternalInput").ap()
    out_t = nc.dram_tensor("out_t", [HID, TOWN], F32, kind="ExternalOutput").ap()

    dbg_t = {}
    dbg_specs = {
        "xnT_own": ([128, 16384], BF16), "ktA": ([128, 4608], BF16),
        "ktB": ([128, 4608], BF16), "vS": ([128, 36 * 260], BF16),
        "mk2row": ([1, 4], F32), "qT": ([128, 16384], BF16),
        "bias_all": ([128, 32], F32), "attn_oT": ([128, 16384], BF16),
        "hdbg": ([128, ITL * 1024], BF16), "qtok": ([128, 16384], BF16),
    }
    for name in dbg:
        shp, dt = dbg_specs[name]
        dbg_t[name] = nc.dram_tensor("dbg_" + name, shp, dt, kind="ExternalOutput").ap()

    P = Prog()
    semh = {}
    ctx = contextlib.ExitStack()
    for s in ("dx", "dw", "dout", "pe", "ve", "se"):
        P.sem(s)
        semh[s] = ctx.enter_context(nc.semaphore(s))

    def sb(name, shape, dt):
        return ctx.enter_context(nc.sbuf_tensor(name, shape, dt))

    arena = sb("arena", [128, ACOLS], BF16)
    bn_sb = sb("bn_sb", [128, 72], F32)
    agg_sb = sb("agg_sb", [128, 6], F32)
    varp_sb = sb("varp_sb", [128, 3], F32)
    negmr_sb = sb("negmr_sb", [128, 3], F32)
    rstd_sb = sb("rstd_sb", [128, 3], F32)
    ksqt_sb = sb("ksqt_sb", [128, 256], BF16)
    kn4_sb = sb("kn4_sb", [128, 4], F32)
    knmax_sb = sb("knmax_sb", [128, 4], F32)
    knm4_sb = sb("knm4_sb", [4, 1], F32)
    mk2_sb = sb("mk2_sb", [1, 4], F32)
    qn_sb = sb("qn_sb", [128, 32], F32)
    qnmax_sb = sb("qnmax_sb", [128, 32], F32)
    qmx32_sb = sb("qmx32_sb", [32, 1], F32)
    qmaxr_sb = sb("qmaxr_sb", [1, 32], F32)
    prodr_sb = sb("prodr_sb", [1, 32], F32)
    sqrow_sb = sb("sqrow_sb", [1, 32], BF16)
    biasa_sb = sb("biasa_sb", [128, 32], F32)
    gs_sb = sb("gs_sb", [128, 44], F32)
    us_sb = sb("us_sb", [128, 44], F32)
    ds_sb = sb("ds_sb", [128, 16], F32)
    sel_sb = sb("sel_sb", [1, 8], I32)
    idb_sb = sb("idb_sb", [128, 128], BF16)
    idf_sb = sb("idf_sb", [128, 128], F32)
    pones_sb = sb("pones_sb", [1, 128], BF16)
    mones_sb = sb("mones_sb", [1, 128], BF16)
    tmp4_sb = sb("tmp4_sb", [128, 1024], F32)

    psT0 = ctx.enter_context(nc.psum_tensor("psT0", [128, 1024], BF16))
    psT1 = ctx.enter_context(nc.psum_tensor("psT1", [128, 1024], BF16))
    psF = [ctx.enter_context(nc.psum_tensor(f"psF{i}", [128, 512], F32))
           for i in range(4)]
    psAV = ctx.enter_context(nc.psum_tensor("psAV", [128, 1024], F32))
    psAVb = psAV[:].bitcast(BF16)

    def av_(c0, n, r0=0, r1=128):
        return arena[r0:r1, c0:c0 + n]

    def avf(c0, nf32, r0=0, r1=128):
        return arena[r0:r1, c0:c0 + 2 * nf32].bitcast(F32)

    probe_ap = arena[64:128, 5:517]
    assert probe_ap.offset == 64 * ACOLS + 5, probe_ap.offset

    xnt_own = av_(XNT_OWN, 16384)
    attn_ot = av_(ATTN_OT, 16384)
    h_sb = av_(H_OFF, ITL * 1024)
    ktA = av_(KTA_O, 4608)
    ktB = av_(KTB_O, 4608)
    vS = av_(VS_O, 36 * 260)
    ctok = av_(CTOK_O, NTILE * 64)
    masks = av_(MASK_O, 2048)
    wkv_sb = av_(WKV_O, 8192)
    xn_v = [av_(XN_O + i * 2048, 2048) for i in range(3)]
    ktok_v = [av_(KTOK_O + i * 256, 256) for i in range(2)]
    xbuf_v = [avf(XBUF_O + i * 4096, 2048) for i in range(3)]

    ev_wkv = P.emit("gpsimd", lambda e: e.dma_start(out=wkv_sb, in_=wkv_h),
                    inc="dw", n=16)
    evc = None
    for dst, srca in ((ctok, ctok_h), (masks, mask_h), (idb_sb[:], idb_h),
                      (pones_sb[:], pones_h), (mones_sb[:], mones_h)):
        evc = P.emit("gpsimd", lambda e, d=dst, s=srca: e.dma_start(out=d, in_=s),
                     inc="dw", n=16)
    for dst, srca in ((sel_sb[:], sel_h), (idf_sb[:], idf_h),
                      (gs_sb[:], gs_h), (us_sb[:], us_h), (ds_sb[:], ds_h)):
        evc2 = P.emit("sync", lambda e, d=dst, s=srca: e.dma_start(out=d, in_=s),
                      inc="dx", n=16)
    ev_consts = emax(evc, evc2)

    ev_knz = P.emit("vector", lambda e: e.memset(knmax_sb[:], 0.0), inc="ve")
    ev_qnz = P.emit("vector", lambda e: e.memset(qnmax_sb[:], 0.0), inc="ve")
    ones_ap = bass.AP(arena, VS_O + 64, [[ACOLS, 128], [260, 36], [65, 4], [1, 1]])
    ev_vones = P.emit("vector", lambda e, a=ones_ap: e.memset(a, 1.0), inc="ve")

    evXL = [None] * NTILE
    evXN = [None] * NTILE
    evS3 = [None] * NTILE
    evS4 = [None] * NTILE
    evS5 = [None] * NTILE
    evKR = [None] * NTILE
    evKN = [None] * NTILE
    evS7 = [None] * NTILE
    evS8 = [None] * NTILE
    evVE = [None] * NTILE

    def xnt_src_slice(t, j):
        s = t // 4
        if s >= 7:
            return av_(XNT_OWN + j * 1024 + (s - 7) * 512 + (t % 4) * 128, 128)
        return av_(XNTT_O + (t % 2) * 2048 + j * 128, 128)

    for step in range(NTILE + 2):
        t = step
        if t < NTILE:
            b = t % 3
            b2p = t % 2
            src_x = xkv[t * 128:(t + 1) * 128, :]
            evXL[t] = P.emit(
                "sync", lambda e, s=src_x, d=xbuf_v[b]: e.dma_start(out=d, in_=s),
                wait=emax(evXN[t - 3] if t >= 3 else None), inc="dx", n=16)
            e1 = None
            for i in range(4):
                e1 = P.emit("vector", lambda e, i=i, b=b: e.bn_stats(
                    bn_sb[:, b * 24 + i * 6:b * 24 + (i + 1) * 6],
                    xbuf_v[b][:, i * 512:(i + 1) * 512]),
                    wait=emax(evXL[t]), inc="ve")
            e2 = P.emit("vector", lambda e, b=b: e.bn_aggr(
                agg_sb[:, 2 * b:2 * b + 2], bn_sb[:, b * 24:b * 24 + 24]),
                wait=emax(e1), inc="ve")
            e3 = P.emit("vector", lambda e, b=b: e.tensor_scalar_add(
                varp_sb[:, b:b + 1], agg_sb[:, 2 * b + 1:2 * b + 2], EPS),
                wait=emax(e2), inc="ve")
            e4 = P.emit("vector", lambda e, b=b: e.reciprocal(
                varp_sb[:, b:b + 1], varp_sb[:, b:b + 1]), wait=emax(e3), inc="ve")
            e5 = P.emit("scalar", lambda e, b=b: e.sqrt(
                rstd_sb[:, b:b + 1], varp_sb[:, b:b + 1]), wait=emax(e4), inc="se")
            e6 = P.emit("vector", lambda e, b=b: e.tensor_scalar(
                out=negmr_sb[:, b:b + 1], in0=agg_sb[:, 2 * b:2 * b + 1],
                scalar1=rstd_sb[:, b:b + 1], scalar2=-1.0,
                op0=ALU.mult, op1=ALU.mult),
                wait=emax(e5), inc="ve")
            evXN[t] = P.emit("scalar", lambda e, b=b: e.activation(
                out=xn_v[b], in_=xbuf_v[b], func=AF.Identity,
                bias=negmr_sb[:, b:b + 1], scale=rstd_sb[:, b:b + 1]),
                wait=emax(e6, evS3[t - 3] if t >= 3 else None), inc="se")
            last = None
            for j in range(16):
                ps = psT0 if j < 8 else psT1
                jj = j % 8
                last = P.emit("tensor", lambda e, b=b, j=j, ps=ps, jj=jj: e.transpose(
                    ps[:, jj * 128:(jj + 1) * 128],
                    xn_v[b][:, j * 128:(j + 1) * 128], idb_sb[:]),
                    wait=emax(evXN[t], evS4[t - 1] if t >= 1 else None,
                              ev_consts if t == 0 else None),
                    inc="pe" if j == 15 else None)
            evS3[t] = last
            s = t // 4
            if s >= 7:
                off = XNT_OWN + (s - 7) * 512 + (t % 4) * 128
                d0 = bass.AP(arena, off, [[ACOLS, 128], [1024, 8], [1, 128]])
                d1 = bass.AP(arena, off + 8 * 1024, [[ACOLS, 128], [1024, 8], [1, 128]])
                s0 = bass.AP(psT0, 0, [[1024, 128], [128, 8], [1, 128]])
                s1 = bass.AP(psT1, 0, [[1024, 128], [128, 8], [1, 128]])
            else:
                d0 = av_(XNTT_O + b2p * 2048, 1024)
                d1 = av_(XNTT_O + b2p * 2048 + 1024, 1024)
                s0, s1 = psT0[:], psT1[:]
            w4 = emax(evS3[t], evS5[t - 2] if (t >= 2 and (t - 2) // 4 < 7) else None)
            P.emit("scalar", lambda e, d=d0, s_=s0: e.copy(d, s_), wait=w4)
            evS4[t] = P.emit("scalar", lambda e, d=d1, s_=s1: e.copy(d, s_),
                             inc="se")
        if step >= 1 and (step - 1) < NTILE:
            t1 = step - 1
            b1 = t1 % 2
            last = None
            for j in range(HT):
                lhsT = xnt_src_slice(t1, j)
                last = P.emit("tensor", lambda e, l=lhsT, j=j, b1=b1: e.matmul(
                    psF[b1][:], l, wkv_sb[:, j * 512:(j + 1) * 512],
                    start=(j == 0), stop=(j == HT - 1)),
                    wait=emax(evS4[t1], ev_wkv,
                              evKR[t1 - 2] if t1 >= 2 else None,
                              evVE[t1 - 2] if t1 >= 2 else None),
                    inc="pe" if j == HT - 1 else None)
            evS5[t1] = last
            vdst = bass.AP(arena, VS_O + t1 * 260, [[ACOLS, 128], [65, 4], [1, 64]])
            vsrc = bass.AP(psF[b1], 256, [[512, 128], [64, 4], [1, 64]])
            evVE[t1] = P.emit("scalar", lambda e, d=vdst, s=vsrc: e.copy(d, s),
                              wait=emax(evS5[t1], ev_vones), inc="se")
            x1 = bass.AP(psF[b1], 0, [[512, 128], [64, 4], [1, 32]])
            x2 = bass.AP(psF[b1], 32, [[512, 128], [64, 4], [1, 32]])
            cosb = bass.AP(arena, CTOK_O + t1 * 64, [[ACOLS, 128], [0, 4], [1, 32]])
            sinb = bass.AP(arena, CTOK_O + t1 * 64 + 32,
                           [[ACOLS, 128], [0, 4], [1, 32]])
            k1 = bass.AP(arena, KTOK_O + b1 * 256, [[ACOLS, 128], [64, 4], [1, 32]])
            k2 = bass.AP(arena, KTOK_O + b1 * 256 + 32,
                         [[ACOLS, 128], [64, 4], [1, 32]])
            rA = bass.AP(arena, RTMPA_O, [[ACOLS, 128], [32, 4], [1, 32]])
            rB = bass.AP(arena, RTMPB_O, [[ACOLS, 128], [32, 4], [1, 32]])
            wr = emax(evS5[t1], evVE[t1], ev_consts,
                      evS7[t1 - 2] if t1 >= 2 else None)
            P.emit("vector", lambda e, o=rA, a=x2, c=sinb: e.tensor_mul(o, a, c),
                   wait=wr)
            P.emit("vector", lambda e, o=rB, a=x1, c=sinb: e.tensor_mul(o, a, c))
            P.emit("vector", lambda e, o=k1, a=x1, c=cosb: e.tensor_mul(o, a, c))
            P.emit("vector", lambda e, o=k1, a=rA: e.tensor_sub(o, o, a))
            P.emit("vector", lambda e, o=k2, a=x2, c=cosb: e.tensor_mul(o, a, c))
            evKR[t1] = P.emit("vector", lambda e, o=k2, a=rB: e.tensor_add(o, o, a),
                              inc="ve")
            eq = P.emit("scalar", lambda e, b1=b1: e.square(ksqt_sb[:], ktok_v[b1]),
                        wait=emax(evKR[t1], evKN[t1 - 1] if t1 >= 1 else None),
                        inc="se")
            er = None
            for r in range(4):
                er = P.emit("vector", lambda e, r=r: e.reduce_sum(
                    kn4_sb[:, r:r + 1], ksqt_sb[:, r * 64:(r + 1) * 64], axis=AX.X),
                    wait=emax(eq), inc="ve")
            evKN[t1] = P.emit("vector", lambda e: e.tensor_max(
                knmax_sb[:], knmax_sb[:], kn4_sb[:]), wait=emax(er, ev_knz), inc="ve")
        if step >= 2 and (step - 2) < NTILE:
            t2 = step - 2
            b2 = t2 % 2
            last = None
            for blk in range(2):
                last = P.emit("tensor", lambda e, b2=b2, blk=blk: e.transpose(
                    psAVb[:, blk * 128:(blk + 1) * 128],
                    ktok_v[b2][:, blk * 128:(blk + 1) * 128], idb_sb[:]),
                    wait=emax(evKR[t2], evS8[t2 - 1] if t2 >= 1 else None),
                    inc="pe" if blk == 1 else None)
            evS7[t2] = last
            P.emit("scalar", lambda e, t2=t2: e.copy(
                av_(KTA_O + t2 * 128, 128), psAVb[:, 0:128]), wait=emax(evS7[t2]))
            evS8[t2] = P.emit("scalar", lambda e, t2=t2: e.copy(
                av_(KTB_O + t2 * 128, 128), psAVb[:, 128:256]), inc="se")

    eT1 = P.emit("tensor", lambda e: e.transpose(
        psF[0][0:4, 0:128], knmax_sb[:], idf_sb[:]),
        wait=emax(evKN[NTILE - 1], evVE[NTILE - 1], evKR[NTILE - 1]), inc="pe")
    eT2 = P.emit("vector", lambda e: e.reduce_max(
        knm4_sb[:], psF[0][0:4, 0:128], axis=AX.X), wait=emax(eT1), inc="ve")
    eT3 = P.emit("tensor", lambda e: e.transpose(
        psF[0][0:1, 0:4], knm4_sb[:], idf_sb[0:4, 0:4]), wait=emax(eT2), inc="pe")
    evMK2 = P.emit("vector", lambda e: e.tensor_copy(mk2_sb[:], psF[0][0:1, 0:4]),
                   wait=emax(eT3), inc="ve")

    dbg_gate = [None]

    def dump_now(name, w):
        if name not in dbg:
            return
        ev = P.emit("sync", lambda e, d=dbg_t[name], s=dmap0[name]: e.dma_start(
            out=d, in_=s), wait=w, inc="dout", n=16)
        dbg_gate[0] = emax(dbg_gate[0], ev)

    dmap0 = {
        "xnT_own": av_(XNT_OWN, 16384), "ktA": ktA, "ktB": ktB, "vS": vS,
        "mk2row": mk2_sb[:], "qT": av_(QT_O, 16384),
        "bias_all": biasa_sb[:], "attn_oT": av_(ATTN_OT, 16384),
        "hdbg": h_sb, "qtok": av_(QTOK_O, 16384),
    }
    for nm in ("xnT_own", "ktA", "ktB", "vS", "mk2row"):
        dump_now(nm, emax(evMK2, evS8[NTILE - 1], evVE[NTILE - 1], evS4[NTILE - 1],
                          evKR[NTILE - 1]))

    qtok_v = [av_(QTOK_O + tt * 2048, 2048) for tt in range(8)]
    wqc_v = [av_(WQC_O + i * 4096, 4096) for i in range(2)]
    att_v = [av_(ATT_O + i * 512, 512) for i in range(4)]
    zero_v = av_(ZERO_O, 512)
    rbff_row = avf(RBFF_O, 1024, 0, 1)
    rbf2_row = arena[0:1, RBF2_O:RBF2_O + 1024]
    rbsb = avf(RBSB_O, 1024, 0, 64)
    atmp = av_(ATMP_O, 1024, 0, 64)

    ev_zero = P.emit("vector", lambda e: e.memset(zero_v, 0.0), inc="ve")

    REGS = {}

    def reg_setup(e):
        st = contextlib.ExitStack()
        REGS["stack"] = st
        REGS["sel"] = [st.enter_context(e.register(f"sel{k}")) for k in range(7)]
        for k in range(7):
            e.reg_load(REGS["sel"][k], sel_sb[0:1, k:k + 1])
        return None

    P.emit("tensor", reg_setup, wait=emax(ev_consts))

    evQMM = {}
    evQE = {}
    for oc in range(8):
        evWQ = P.emit("gpsimd", lambda e, oc=oc: e.dma_start(
            out=wqc_v[oc % 2], in_=wq_h[oc]),
            wait=emax(evQMM.get((oc - 2, 7)),
                      emax(evKR[NTILE - 1], evVE[NTILE - 1])
                      if oc < 2 else None),
            inc="dw", n=16)
        for tt in range(8):
            last = None
            for j in range(HT):
                lhsT = av_(XNT_OWN + j * 1024 + tt * 128, 128)
                last = P.emit("tensor", lambda e, l=lhsT, j=j, tt=tt, oc=oc: e.matmul(
                    psF[2 + tt % 2][:, 0:256], l,
                    wqc_v[oc % 2][:, j * 256:(j + 1) * 256],
                    start=(j == 0), stop=(j == HT - 1)),
                    wait=emax(evWQ, evS4[NTILE - 1],
                              evQE.get((oc, tt - 2)), evQE.get((oc - 1, tt + 6))),
                    inc="pe" if j == HT - 1 else None)
            evQMM[(oc, tt)] = last
            evQE[(oc, tt)] = P.emit("vector", lambda e, oc=oc, tt=tt: e.tensor_copy(
                av_(QTOK_O + tt * 2048 + oc * 256, 256), psF[2 + tt % 2][:, 0:256]),
                wait=emax(evQMM[(oc, tt)]), inc="ve")

    evQR = [None] * 8
    evQN = [None] * 8
    evQT = [None] * 8
    evQTE = [None] * 8
    for tt in range(8):
        base = QTOK_O + tt * 2048
        x1 = bass.AP(arena, base, [[ACOLS, 128], [64, 32], [1, 32]])
        x2 = bass.AP(arena, base + 32, [[ACOLS, 128], [64, 32], [1, 32]])
        cosb = bass.AP(arena, CTOK_O + (28 + tt) * 64, [[ACOLS, 128], [0, 32], [1, 32]])
        sinb = bass.AP(arena, CTOK_O + (28 + tt) * 64 + 32,
                       [[ACOLS, 128], [0, 32], [1, 32]])
        rA = bass.AP(arena, RTMPA_O, [[ACOLS, 128], [32, 32], [1, 32]])
        rB = bass.AP(arena, RTMPB_O, [[ACOLS, 128], [32, 32], [1, 32]])
        w0 = emax(evQE[(7, tt)], ev_consts, evQMM[(7, 7)])
        P.emit("vector", lambda e, o=rA, a=x2, c=sinb: e.tensor_mul(o, a, c), wait=w0)
        P.emit("vector", lambda e, o=rB, a=x1, c=sinb: e.tensor_mul(o, a, c))
        P.emit("vector", lambda e, o=x1, a=x1, c=cosb: e.tensor_mul(o, a, c))
        P.emit("vector", lambda e, o=x1, a=rA: e.tensor_sub(o, o, a))
        P.emit("vector", lambda e, o=x2, a=x2, c=cosb: e.tensor_mul(o, a, c))
        evQR[tt] = P.emit("vector", lambda e, o=x2, a=rB: e.tensor_add(o, o, a),
                          inc="ve")
        eqs = P.emit("scalar", lambda e, tt=tt: e.square(av_(QSQT_O, 2048), qtok_v[tt]),
                     wait=emax(evQR[tt], evQN[tt - 1] if tt >= 1 else None), inc="se")
        er = None
        for hh in range(HQ):
            er = P.emit("vector", lambda e, hh=hh: e.reduce_sum(
                qn_sb[:, hh:hh + 1], av_(QSQT_O + hh * 64, 64), axis=AX.X),
                wait=emax(eqs), inc="ve")
        evQN[tt] = P.emit("vector", lambda e: e.tensor_max(
            qnmax_sb[:], qnmax_sb[:], qn_sb[:]), wait=emax(er, ev_qnz), inc="ve")
        last = None
        for j in range(16):
            ps = psT0 if j < 8 else psT1
            jj = j % 8
            last = P.emit("tensor", lambda e, tt=tt, j=j, ps=ps, jj=jj: e.transpose(
                ps[:, jj * 128:(jj + 1) * 128],
                qtok_v[tt][:, j * 128:(j + 1) * 128], idb_sb[:]),
                wait=emax(evQR[tt], evQTE[tt - 1] if tt >= 1 else None,
                          evS4[NTILE - 1]),
                inc="pe" if j == 15 else None)
        evQT[tt] = last
        d0 = bass.AP(arena, QT_O + tt * 128, [[ACOLS, 128], [1024, 8], [1, 128]])
        d1 = bass.AP(arena, QT_O + 8 * 1024 + tt * 128,
                     [[ACOLS, 128], [1024, 8], [1, 128]])
        s0 = bass.AP(psT0, 0, [[1024, 128], [128, 8], [1, 128]])
        s1 = bass.AP(psT1, 0, [[1024, 128], [128, 8], [1, 128]])
        P.emit("scalar", lambda e, d=d0, s_=s0: e.copy(d, s_),
               wait=emax(evQT[tt]))
        evQTE[tt] = P.emit("scalar", lambda e, d=d1, s_=s1: e.copy(d, s_),
                           inc="se")

    eU1 = P.emit("tensor", lambda e: e.transpose(
        psF[0][0:32, 0:128], qnmax_sb[:], idf_sb[:]),
        wait=emax(evQN[7], evQMM[(7, 7)], evMK2), inc="pe")
    eU2 = P.emit("vector", lambda e: e.reduce_max(
        qmx32_sb[:], psF[0][0:32, 0:128], axis=AX.X), wait=emax(eU1), inc="ve")
    eU3 = P.emit("tensor", lambda e: e.transpose(
        psF[0][0:1, 0:32], qmx32_sb[:], idf_sb[0:32, 0:32]), wait=emax(eU2), inc="pe")
    eU4 = P.emit("vector", lambda e: e.tensor_copy(qmaxr_sb[:], psF[0][0:1, 0:32]),
                 wait=emax(eU3), inc="ve")
    mk2bc = bass.AP(mk2_sb, 0, [[4, 1], [1, 4], [0, 8]])
    pr3 = bass.AP(prodr_sb, 0, [[32, 1], [8, 4], [1, 8]])
    qm3 = bass.AP(qmaxr_sb, 0, [[32, 1], [8, 4], [1, 8]])
    eU5 = P.emit("vector", lambda e, o=pr3, a=qm3, m=mk2bc: e.tensor_mul(o, a, m),
                 wait=emax(eU4, evMK2), inc="ve")
    eU6 = P.emit("vector", lambda e: e.tensor_scalar_mul(
        prodr_sb[:], prodr_sb[:], 1.0 / 64.0), wait=emax(eU5), inc="ve")
    eU7 = P.emit("scalar", lambda e: e.sqrt(sqrow_sb[:], prodr_sb[:]),
                 wait=emax(eU6), inc="se")
    dump_now("qtok", emax(evQR[7], evQN[7], evQTE[7]))
    wk2 = emax(evQTE[7], evQN[7], evS8[NTILE - 1], dbg_gate[0])
    P.emit("scalar", lambda e: e.copy(arena[0:64, KT2A_O:KT2A_O + 4608],
                                      arena[64:128, KTA_O:KTA_O + 4608]), wait=wk2)
    P.emit("scalar", lambda e: e.copy(arena[64:128, KT2A_O:KT2A_O + 4608],
                                      arena[0:64, KTA_O:KTA_O + 4608]))
    P.emit("scalar", lambda e: e.copy(arena[0:64, KT2B_O:KT2B_O + 4608],
                                      arena[64:128, KTB_O:KTB_O + 4608]))
    ev_kt2 = P.emit("scalar", lambda e: e.copy(arena[64:128, KT2B_O:KT2B_O + 4608],
                                               arena[0:64, KTB_O:KTB_O + 4608]),
                    inc="se")

    eB1 = P.emit("tensor", lambda e: e.matmul(
        psF[2][:, 0:32], mones_sb[:], sqrow_sb[:], start=True, stop=True),
        wait=emax(eU7, evQE[(7, 7)]), inc="pe")
    evBIAS = P.emit("vector", lambda e: e.tensor_copy(biasa_sb[:], psF[2][:, 0:32]),
                    wait=emax(eB1), inc="ve")

    dump_now("qT", emax(evQTE[7]))
    dump_now("bias_all", emax(evBIAS))

    blocks = [(7, m) for m in range(4)] + [(8, m) for m in range(4)] + \
             [(k, m) for k in range(7) for m in range(4)]
    NB = len(blocks)
    prev_sc_free = emax(eU2, eU4, evBIAS, eT2, evMK2)
    prev_av_free = emax(evS8[NTILE - 1], evMK2)
    prev_rb_free = emax(evBIAS)
    prev_avlast = None
    prev_atmp = None
    prev_explast = [None, None]
    ev_attn_done_v = None
    ev_attn_done_s = None

    for h in range(HQ):
        kvh = h // 8
        qrow = 64 * (h % 2)
        if (kvh % 2) * 64 == qrow:
            kt_base = KTA_O if kvh < 2 else KTB_O
        else:
            kt_base = KT2A_O if kvh < 2 else KT2B_O
        kt_row = qrow
        qcolbase = QT_O + (h // 2) * 1024 + qrow * ACOLS

        evSC = [None] * NB
        evEXP = [None] * NB
        evMSK = [None] * NB
        evAV = [None] * NB

        def emit_sc(bi):
            slot, m = blocks[bi]
            kb = slot * 4 + m
            lhs = arena[kt_row:kt_row + 64,
                        kt_base + kb * 128:kt_base + (kb + 1) * 128]
            sc = psF[bi % 2]
            w = emax(evEXP[bi - 2] if bi >= 2 else prev_explast[bi % 2],
                     prev_sc_free if bi < 2 else None,
                     emax(evQTE[7], evS8[NTILE - 1], ev_kt2)
                     if (h == 0 and bi == 0) else None)
            if slot >= 7:
                rhs = bass.AP(arena, qcolbase + (512 if slot == 8 else 0),
                              [[ACOLS, 64], [1, 512]])

                def fn(e, l=lhs, r=rhs, o=sc):
                    return e.matmul(o[:], l, r, start=True, stop=True)
            else:
                k = slot
                r0 = bass.AP(arena, qcolbase, [[ACOLS, 64], [1, 512]])
                r1 = bass.AP(arena, qcolbase + 512, [[ACOLS, 64], [1, 512]])

                def fn(e, l=lhs, r0=r0, r1=r1, k=k, o=sc):
                    with e.If_eq(REGS["sel"][k], 0):
                        e.matmul(o[:], l, r0, start=True, stop=True).then_inc(
                            semh["pe"], 1)
                    with e.Else():
                        e.matmul(o[:], l, r1, start=True, stop=True).then_inc(
                            semh["pe"], 1)
                    return None
            evSC[bi] = P.emit("tensor", fn, wait=w, inc="pe")

        def emit_av(bi):
            slot, m = blocks[bi]
            kb = slot * 4 + m
            vp = vS[:, kb * 260 + kvh * 65:kb * 260 + (kvh + 1) * 65]
            ab = att_v[bi % 3]
            st = (slot >= 7 and m == 0)
            w = emax(evMSK[bi], prev_av_free if bi == 0 else None)
            if slot >= 7:
                oap = psAV[0:65, 0:512] if slot == 7 else psAV[0:65, 512:1024]

                def fn(e, v=vp, o=oap, a=ab, st=st):
                    return e.matmul(o, v, a, start=st, stop=False)
            else:
                k = slot
                o0 = psAV[0:65, 0:512]
                o1 = psAV[0:65, 512:1024]

                def fn(e, v=vp, o0=o0, o1=o1, a=ab, k=k):
                    with e.If_eq(REGS["sel"][k], 0):
                        e.matmul(o0, v, a, start=False, stop=False).then_inc(
                            semh["pe"], 1)
                    with e.Else():
                        e.matmul(o1, v, a, start=False, stop=False).then_inc(
                            semh["pe"], 1)
                    return None
            evAV[bi] = P.emit("tensor", fn, wait=w, inc="pe")

        for bi, (slot, m) in enumerate(blocks):
            emit_sc(bi)
            ab = att_v[bi % 3]
            evEXP[bi] = P.emit(
                "scalar", lambda e, o=ab, i=psF[bi % 2], h=h: e.activation(
                    out=o, in_=i[:], func=AF.Exp,
                    bias=biasa_sb[:, h:h + 1], scale=0.125),
                wait=emax(evSC[bi], evBIAS,
                          evAV[bi - 3] if bi >= 3 else prev_avlast),
                inc="se")
            if slot >= 7:
                evMSK[bi] = P.emit(
                    "vector", lambda e, o=ab, m=m: e.tensor_mul(
                        o, o, masks[:, m * 512:(m + 1) * 512]),
                    wait=emax(evEXP[bi]), inc="ve")
            else:
                evMSK[bi] = evEXP[bi]
            if bi >= 1:
                emit_av(bi - 1)
        emit_av(NB - 1)

        eZ1 = P.emit("tensor", lambda e: e.matmul(
            psAV[0:65, 0:512], vS[:, 0:65], zero_v, start=False, stop=True),
            wait=emax(ev_zero, evAV[NB - 1]), inc="pe")
        eZ2 = P.emit("tensor", lambda e: e.matmul(
            psAV[0:65, 512:1024], vS[:, 0:65], zero_v, start=False, stop=True),
            inc="pe")
        eR0 = P.emit("scalar", lambda e: e.copy(rbff_row, psAV[64:65, 0:1024]),
                     wait=emax(eZ2), inc="se")
        eR1 = P.emit("vector", lambda e: e.reciprocal(rbff_row, rbff_row),
                     wait=emax(eR0), inc="ve")
        eR1b = P.emit("vector", lambda e: e.tensor_copy(rbf2_row, rbff_row),
                      wait=emax(eR1), inc="ve")
        eR2 = P.emit("tensor", lambda e: e.matmul(
            psF[2][0:64, :], pones_sb[0:1, 0:64], arena[0:1, RBF2_O:RBF2_O + 512],
            start=True, stop=True),
            wait=emax(eR1b, prev_rb_free), inc="pe")
        eR3 = P.emit("tensor", lambda e: e.matmul(
            psF[3][0:64, :], pones_sb[0:1, 0:64],
            arena[0:1, RBF2_O + 512:RBF2_O + 1024],
            start=True, stop=True), inc="pe")
        P.emit("vector", lambda e: e.tensor_copy(rbsb[:, 0:512], psF[2][0:64, :]),
               wait=emax(eR2, eR3))
        eR4 = P.emit("vector", lambda e: e.tensor_copy(
            rbsb[:, 512:1024], psF[3][0:64, :]), inc="ve")
        col = ATTN_OT + (h // 2) * 1024
        if h % 2 == 0:
            dst = arena[0:64, col:col + 1024]
            evO = P.emit("vector", lambda e, d=dst: e.tensor_mul(
                d, psAV[0:64, 0:1024], rbsb[:]), wait=emax(eR4), inc="ve")
            prev_av_free = emax(evO)
            ev_attn_done_v = evO
        else:
            evO = P.emit("vector", lambda e: e.tensor_mul(
                atmp, psAV[0:64, 0:1024], rbsb[:]),
                wait=emax(eR4, prev_atmp), inc="ve")
            dst = arena[64:128, col:col + 1024]
            evO2 = P.emit("scalar", lambda e, d=dst: e.copy(d, atmp),
                          wait=emax(evO), inc="se")
            prev_av_free = emax(evO, evO2)
            prev_atmp = evO2
            ev_attn_done_v = evO
            ev_attn_done_s = evO2
        prev_sc_free = None
        prev_rb_free = emax(eR4)
        prev_avlast = evAV[NB - 1]
        prev_explast = [evEXP[NB - 2], evEXP[NB - 1]]

    ev_attn_all = emax(ev_attn_done_v, ev_attn_done_s)

    gw_v = [av_(GW_O + i * 4096, 4096) for i in range(2)]
    uw_v = [av_(UW_O + i * 4096, 4096) for i in range(2)]
    gst = av_(GST_O, 1024)
    ust = av_(UST_O, 1024)

    dump_now("attn_oT", ev_attn_all)
    evGW = {}
    evUW = {}
    evGMM = {}
    evUMM = {}
    evHM = {}
    gfree = {0: None, 1: None}
    ufree = {0: None, 1: None}
    last_gmm = None
    last_umm = None
    last_hm = None
    for ic in range(22):
        evGW[ic] = P.emit("gpsimd", lambda e, ic=ic: e.dma_start(
            out=gw_v[ic % 2], in_=gw_h[ic]),
            wait=emax(evGMM.get(ic - 2), ev_attn_all if ic < 2 else None,
                      dbg_gate[0] if ic == 0 else None),
            inc="dw", n=16)
        evUW[ic] = P.emit("gpsimd", lambda e, ic=ic: e.dma_start(
            out=uw_v[ic % 2], in_=uw_h[ic]),
            wait=emax(evUMM.get(ic - 2)), inc="dw", n=16)
        for it2 in range(2):
            itile = 2 * ic + it2
            for tb in range(2):
                pi = (2 * it2 + tb) % 2
                pg = psF[pi]
                pu = psF[2 + pi]
                last = None
                for j in range(HT):
                    lhsT = av_(GW_O + (ic % 2) * 4096 + j * 256 + it2 * 128, 128)
                    last = P.emit("tensor", lambda e, l=lhsT, j=j, p=pg, tb=tb: e.matmul(
                        p[:], l, av_(XNT_OWN + j * 1024 + tb * 512, 512),
                        start=(j == 0), stop=(j == HT - 1)),
                        wait=emax(evGW[ic], gfree[pi],
                                  ev_attn_all if ic == 0 else None),
                        inc="pe" if j == HT - 1 else None)
                last_gmm = evGMM[ic] = last
                last = None
                for j in range(HT):
                    lhsT = av_(UW_O + (ic % 2) * 4096 + j * 256 + it2 * 128, 128)
                    last = P.emit("tensor", lambda e, l=lhsT, j=j, p=pu, tb=tb: e.matmul(
                        p[:], l, av_(XNT_OWN + j * 1024 + tb * 512, 512),
                        start=(j == 0), stop=(j == HT - 1)),
                        wait=emax(evUW[ic], ufree[pi]),
                        inc="pe" if j == HT - 1 else None)
                last_umm = evUMM[ic] = last
                esil = P.emit(
                    "scalar", lambda e, p=pg, tb=tb, itile=itile: e.activation(
                        out=gst[:, tb * 512:(tb + 1) * 512], in_=p[:], func=AF.Silu,
                        scale=gs_sb[:, itile:itile + 1]),
                    wait=emax(last_gmm, evHM.get((itile - 1, tb))),
                    inc="se")
                gfree[pi] = esil
                eue = P.emit(
                    "vector", lambda e, p=pu, tb=tb, itile=itile: e.tensor_scalar_mul(
                        ust[:, tb * 512:(tb + 1) * 512], p[:],
                        us_sb[:, itile:itile + 1]),
                    wait=emax(last_umm, evHM.get((itile - 1, tb))),
                    inc="ve")
                ufree[pi] = eue
                last_hm = evHM[(itile, tb)] = P.emit(
                    "vector", lambda e, tb=tb, itile=itile: e.tensor_mul(
                        av_(H_OFF + itile * 1024 + tb * 512, 512),
                        gst[:, tb * 512:(tb + 1) * 512],
                        ust[:, tb * 512:(tb + 1) * 512]),
                    wait=emax(esil, eue), inc="ve")

    ev_ffn_mm_done = emax(last_gmm, last_umm)

    dump_now("hdbg", emax(last_hm))

    xq_v = [avf(XQ_O + a * 4096, 2048) for a in range(4)]
    ost_v = [avf(OST_O + i * 1024, 512) for i in range(2)]

    evXQ = {}
    evWXMM = {}
    evDNMM = {}
    evOST = {}
    evODMA = {}
    evE1 = {}
    ev_out = None
    for tb in range(2):
        for a in range(4):
            trow = (28 + tb * 4 + a) * 128
            evXQ[(tb, a)] = P.emit("sync", lambda e, a=a, trow=trow: e.dma_start(
                out=xq_v[a], in_=xkv[trow:trow + 128, :]),
                wait=emax(ev_ffn_mm_done, dbg_gate[0],
                          evWXMM.get((tb - 1, 15)) if tb >= 1 else None),
                inc="dx", n=16)
        for H in range(16):
            p2 = H % 2
            evDW = P.emit("gpsimd", lambda e, H=H, p2=p2: e.dma_start(
                out=av_(DW_O + p2 * 5632, 5632), in_=dw_h[H]),
                wait=emax(evE1.get((tb, H - 2)), ev_ffn_mm_done,
                          evDNMM.get((tb - 1, 14 + H)) if tb >= 1 else None),
                inc="dw", n=16)
            evWO = P.emit("gpsimd", lambda e, H=H, p2=p2: e.dma_start(
                out=av_(WO_O + p2 * 2048, 2048), in_=wo_h[H]),
                wait=emax(evWXMM.get((tb, H - 2)),
                          evWXMM.get((tb - 1, 14 + H)) if tb >= 1 else None),
                inc="dw", n=16)
            pw = psF[p2]
            pd = psF[2 + p2]
            last = None
            for j in range(HT):
                lhsT = av_(WO_O + p2 * 2048 + j * 128, 128)
                last = P.emit("tensor", lambda e, l=lhsT, j=j, p=pw, tb=tb: e.matmul(
                    p[:], l, av_(ATTN_OT + j * 1024 + tb * 512, 512),
                    start=(j == 0), stop=False),
                    wait=emax(evWO, ev_attn_all if (tb == 0 and H == 0) else None,
                              evOST.get((tb, H - 2)),
                              evOST.get((tb - 1, 14 + H)) if tb >= 1 else None),
                    inc=None)
            for a in range(4):
                last = P.emit("tensor", lambda e, a=a, H=H, p=pw: e.matmul(
                    p[:, a * 128:(a + 1) * 128],
                    xq_v[a][:, H * 128:(H + 1) * 128], idf_sb[:],
                    is_transpose=True, start=False, stop=(a == 3)),
                    wait=emax(evXQ[(tb, a)]), inc="pe" if a == 3 else None)
            evWXMM[(tb, H)] = last
            last = None
            for r in range(ITL):
                lhsT = av_(DW_O + p2 * 5632 + r * 128, 128)
                last = P.emit("tensor", lambda e, l=lhsT, r=r, p=pd, tb=tb: e.matmul(
                    p[:], l, av_(H_OFF + r * 1024 + tb * 512, 512),
                    start=(r == 0), stop=(r == ITL - 1)),
                    wait=emax(evDW, last_hm if (tb == 0 and H == 0) else None,
                              evE1.get((tb, H - 2)),
                              evE1.get((tb - 1, 14 + H)) if tb >= 1 else None),
                    inc="pe" if r == ITL - 1 else None)
            evDNMM[(tb, H)] = last
            e1 = P.emit("vector", lambda e, p=pd, p2=p2, H=H: e.tensor_scalar_mul(
                tmp4_sb[:, p2 * 512:(p2 + 1) * 512], p[:], ds_sb[:, H:H + 1]),
                wait=emax(evDNMM[(tb, H)]), inc="ve")
            evE1[(tb, H)] = e1
            evOST[(tb, H)] = P.emit("vector", lambda e, p=pw, p2=p2: e.tensor_add(
                ost_v[p2][:], p[:], tmp4_sb[:, p2 * 512:(p2 + 1) * 512]),
                wait=emax(evWXMM[(tb, H)], e1,
                          evODMA.get((tb, H - 2)),
                          evODMA.get((tb - 1, 14 + H)) if tb >= 1 else None),
                inc="ve")
            evODMA[(tb, H)] = P.emit("sync", lambda e, p2=p2, H=H, tb=tb: e.dma_start(
                out=out_t[H * 128:(H + 1) * 128, tb * 512:(tb + 1) * 512],
                in_=ost_v[p2][:]),
                wait=emax(evOST[(tb, H)]), inc="dout", n=16)

    P.emit("sync", None, wait=[("dout", P.cnt["dout"])])

    with nc.Block() as block:

        @block.sync
        def _(e):
            P.replay("sync", e, semh)

        @block.gpsimd
        def _(e):
            P.replay("gpsimd", e, semh)

        @block.tensor
        def _(e):
            P.replay("tensor", e, semh)

        @block.vector
        def _(e):
            P.replay("vector", e, semh)

        @block.scalar
        def _(e):
            P.replay("scalar", e, semh)

    return nc, ctx


def _tile_w(w, rows_per):
    R = w.shape[0]
    return np.ascontiguousarray(
        w.T.reshape(HT, 128, R).transpose(1, 0, 2).reshape(128, HT * R))


def _prep_shared(w_qkv, w_o, gate_w, up_w, down_w, gate_s, up_s, down_s):
    d = {}
    d["wkv_h"] = _tile_w(w_qkv[2048:2560, :], None)
    d["wq_h"] = np.stack([_tile_w(w_qkv[oc * 256:(oc + 1) * 256, :], None)
                          for oc in range(8)])
    d["wo_h"] = np.stack([
        np.ascontiguousarray(w_o[H * 128:(H + 1) * 128, :].T
                             .reshape(HT, 128, 128).transpose(1, 0, 2)
                             .reshape(128, 2048))
        for H in range(16)])
    d["gw_h"] = np.stack([
        np.ascontiguousarray(gate_w[ic * 256:(ic + 1) * 256, :].T
                             .reshape(HT, 128, 256).transpose(1, 0, 2)
                             .reshape(128, 4096))
        for ic in range(22)])
    d["uw_h"] = np.stack([
        np.ascontiguousarray(up_w[ic * 256:(ic + 1) * 256, :].T
                             .reshape(HT, 128, 256).transpose(1, 0, 2)
                             .reshape(128, 4096))
        for ic in range(22)])
    d["dw_h"] = np.stack([
        np.ascontiguousarray(down_w[H * 128:(H + 1) * 128, :].T
                             .reshape(ITL, 128, 128).transpose(1, 0, 2)
                             .reshape(128, 5632))
        for H in range(16)])
    d["gs_h"] = np.ascontiguousarray(gate_s.reshape(ITL, 128).T)
    d["us_h"] = np.ascontiguousarray(up_s.reshape(ITL, 128).T)
    d["ds_h"] = np.ascontiguousarray(down_s.reshape(16, 128).T)
    d["idb_h"] = np.eye(128, dtype=np.float32)
    d["idf_h"] = np.eye(128, dtype=np.float32)
    d["pones_h"] = np.ones((1, 128), dtype=np.float32)
    d["mones_h"] = -np.ones((1, 128), dtype=np.float32)
    mk = np.zeros((128, 4 * 512), dtype=np.float32)
    i = np.arange(512)
    j = np.arange(128)
    for m in range(4):
        mk[:, m * 512:(m + 1) * 512] = (128 * m + j[:, None] <= i[None, :])
    d["mask_h"] = mk
    return d


def _rope_tab(pos):
    inv = 1.0 / (10000.0 ** (np.arange(0, DH, 2, dtype=np.float64) / DH))
    ang = pos[:, None].astype(np.float64) * inv[None, :]
    ntile = len(pos) // 128
    tab = np.empty((128, ntile * 64), dtype=np.float32)
    c = np.cos(ang).astype(np.float32)
    s = np.sin(ang).astype(np.float32)
    for t in range(ntile):
        tab[:, t * 64:t * 64 + 32] = c[t * 128:(t + 1) * 128]
        tab[:, t * 64 + 32:t * 64 + 64] = s[t * 128:(t + 1) * 128]
    return tab


def _core_geometry(c):
    chunks = [(0, j) for j in range(c)] + [(1, j) for j in range(7 - c)] + \
             [(0, c), (1, 7 - c)]
    sel = np.zeros((1, 8), dtype=np.int32)
    for k in range(7):
        sel[0, k] = 0 if k < c else 512
    return chunks, sel


def kernel(L0_attn_hidden_states, L0_attn_w_qkv, L0_attn_w_o,
           L0_ffn_gate_w, L0_ffn_up_w, L0_ffn_down_w,
           L0_ffn_gate_s, L0_ffn_up_s, L0_ffn_down_s):
    x = np.ascontiguousarray(np.asarray(L0_attn_hidden_states, dtype=np.float32)
                             .reshape(2 * SEQ, HID))
    shared = _prep_shared(np.asarray(L0_attn_w_qkv, np.float32),
                          np.asarray(L0_attn_w_o, np.float32),
                          np.asarray(L0_ffn_gate_w, np.float32),
                          np.asarray(L0_ffn_up_w, np.float32),
                          np.asarray(L0_ffn_down_w, np.float32),
                          np.asarray(L0_ffn_gate_s, np.float32),
                          np.asarray(L0_ffn_up_s, np.float32),
                          np.asarray(L0_ffn_down_s, np.float32))

    in_maps = []
    for c in range(NCORE):
        chunks, sel = _core_geometry(c)
        rows = []
        pos = []
        for bt, j in chunks:
            rows.append(x[bt * SEQ + j * 512: bt * SEQ + (j + 1) * 512])
            pos.append(np.arange(j * 512, (j + 1) * 512))
        xkv = np.ascontiguousarray(np.concatenate(rows, 0))
        ctab = _rope_tab(np.concatenate(pos))
        m = dict(shared)
        m["xkv"] = xkv
        m["ctok_h"] = ctab
        m["sel_h"] = sel
        in_maps.append(m)

    nc, ctx = build_program()
    try:
        res = run_bass_kernel_spmd(nc, in_maps, core_ids=list(range(NCORE)))
    finally:
        ctx.close()

    out = np.empty((2 * SEQ, HID), dtype=np.float32)
    for c in range(NCORE):
        ot = np.asarray(res.results[c]["out_t"])
        out[c * 512:(c + 1) * 512] = ot[:, 0:512].T
        out[SEQ + (7 - c) * 512: SEQ + (8 - c) * 512] = ot[:, 512:1024].T
    return out.reshape(2, SEQ, HID)


# version 06
# speedup vs baseline: 1.1416x; 1.1416x over previous
import contextlib

import numpy as np
import ml_dtypes

import concourse.bass as bass
import concourse.mybir as mybir
from concourse.bass_utils import run_bass_kernel_spmd

F32 = mybir.dt.float32
BF16 = mybir.dt.bfloat16
I32 = mybir.dt.int32
AF = mybir.ActivationFunctionType
ALU = mybir.AluOpType
AX = mybir.AxisListType
BF = ml_dtypes.bfloat16

NCORE = 8
SEQ = 4096
HID = 2048
HQ = 32
DH = 64
INTER = 5632
NTILE = 36
KVROWS = NTILE * 128
TOWN = 1024
HT = 16
ITL = 44
EPS = 1e-5

ENG = ("sync", "gpsimd", "tensor", "vector", "scalar")

ACOLS = 101888
XNT_OWN = 0
ATTN_OT = 16384
H_OFF = 32768
KTA_O = 32768
KTB_O = 37376
VS_O = 41984
CTOK_O = 51344
MASK_O = 53648
WKV_O = 55696
XN_O = 63888
XNTT_O = 72080
KTOK_O = 76176
XBUF_O = 76688
QTOK_O = 16384
QT_O = 72080
WQC_O = 88464
RTMPA_O = 98304
RTMPB_O = 99328
QSQT_O = 90512
ATTA_O = 88464
ATTB_O = 90000
RBFF_O = 91536
RBF2_O = 93584
RBSB_O = 94608
ATMP_O = 96656
ZERO_O = 97680
KT2A_O = 55696
KT2B_O = 60304
GW_O = 77824
UW_O = 86016
GST_O = 94208
UST_O = 95232
XQ_O = 0
DW_O = 77824
WO_O = 89088
OST_O = 93184


class Prog:
    def __init__(self):
        self.ops = {e: [] for e in ENG}
        self.cnt = {}

    def sem(self, name):
        self.cnt.setdefault(name, 0)

    def emit(self, eng, fn, wait=(), inc=None, n=1):
        wait = tuple((s, v) for (s, v) in wait if s is not None and v > 0)
        ev = None
        if inc is not None:
            self.cnt[inc] += n
            ev = (inc, self.cnt[inc])
        self.ops[eng].append((fn, wait, inc, n))
        return ev

    def replay(self, eng, e, sems):
        last = {s: 0 for s in self.cnt}
        for fn, wait, inc, n in self.ops[eng]:
            for s, v in wait:
                if last[s] < v:
                    e.wait_ge(sems[s], v)
                    last[s] = v
            ins = fn(e) if fn is not None else None
            if inc is not None and ins is not None:
                ins.then_inc(sems[inc], n)


def emax(*evs):
    best = {}
    for ev in evs:
        if not ev:
            continue
        if isinstance(ev, list):
            for s, v in ev:
                if best.get(s, 0) < v:
                    best[s] = v
        else:
            s, v = ev
            if best.get(s, 0) < v:
                best[s] = v
    return list(best.items())


def build_program(dbg=()):
    nc = bass.Bass("TRN2", target_bir_lowering=False)

    xkv = nc.dram_tensor("xkv", [KVROWS, HID], F32, kind="ExternalInput").ap()
    wkv_h = nc.dram_tensor("wkv_h", [128, 8192], F32, kind="ExternalInput").ap()
    wq_h = nc.dram_tensor("wq_h", [8, 128, 4096], F32, kind="ExternalInput").ap()
    wo_h = nc.dram_tensor("wo_h", [16, 128, 2048], F32, kind="ExternalInput").ap()
    gw_h = nc.dram_tensor("gw_h", [22, 128, 4096], F32, kind="ExternalInput").ap()
    uw_h = nc.dram_tensor("uw_h", [22, 128, 4096], F32, kind="ExternalInput").ap()
    dw_h = nc.dram_tensor("dw_h", [16, 128, 5632], F32, kind="ExternalInput").ap()
    gs_h = nc.dram_tensor("gs_h", [128, 44], F32, kind="ExternalInput").ap()
    us_h = nc.dram_tensor("us_h", [128, 44], F32, kind="ExternalInput").ap()
    ds_h = nc.dram_tensor("ds_h", [128, 16], F32, kind="ExternalInput").ap()
    ctok_h = nc.dram_tensor("ctok_h", [128, NTILE * 64], F32, kind="ExternalInput").ap()
    mask_h = nc.dram_tensor("mask_h", [128, 2048], F32, kind="ExternalInput").ap()
    sel_h = nc.dram_tensor("sel_h", [1, 8], I32, kind="ExternalInput").ap()
    idb_h = nc.dram_tensor("idb_h", [128, 128], F32, kind="ExternalInput").ap()
    idf_h = nc.dram_tensor("idf_h", [128, 128], F32, kind="ExternalInput").ap()
    pones_h = nc.dram_tensor("pones_h", [1, 128], F32, kind="ExternalInput").ap()
    mones_h = nc.dram_tensor("mones_h", [1, 128], F32, kind="ExternalInput").ap()
    out_t = nc.dram_tensor("out_t", [HID, TOWN], F32, kind="ExternalOutput").ap()

    dbg_t = {}
    dbg_specs = {
        "xnT_own": ([128, 16384], BF16), "ktA": ([128, 4608], BF16),
        "ktB": ([128, 4608], BF16), "vS": ([128, 36 * 260], BF16),
        "mk2row": ([1, 4], F32), "qT": ([128, 16384], BF16),
        "bias_all": ([128, 32], F32), "attn_oT": ([128, 16384], BF16),
        "hdbg": ([128, ITL * 1024], BF16), "qtok": ([128, 16384], BF16),
    }
    for name in dbg:
        shp, dt = dbg_specs[name]
        dbg_t[name] = nc.dram_tensor("dbg_" + name, shp, dt, kind="ExternalOutput").ap()

    P = Prog()
    semh = {}
    ctx = contextlib.ExitStack()
    for s in ("dx", "dw", "dout", "pe", "ve", "se"):
        P.sem(s)
        semh[s] = ctx.enter_context(nc.semaphore(s))

    def sb(name, shape, dt):
        return ctx.enter_context(nc.sbuf_tensor(name, shape, dt))

    arena = sb("arena", [128, ACOLS], BF16)
    bn_sb = sb("bn_sb", [128, 96], F32)
    agg_sb = sb("agg_sb", [128, 8], F32)
    varp_sb = sb("varp_sb", [128, 4], F32)
    negmr_sb = sb("negmr_sb", [128, 4], F32)
    rstd_sb = sb("rstd_sb", [128, 4], F32)
    ksqt_sb = sb("ksqt_sb", [128, 256], BF16)
    kn4_sb = sb("kn4_sb", [128, 4], F32)
    knmax_sb = sb("knmax_sb", [128, 4], F32)
    knm4_sb = sb("knm4_sb", [4, 1], F32)
    mk2_sb = sb("mk2_sb", [1, 4], F32)
    qn_sb = sb("qn_sb", [128, 32], F32)
    qnmax_sb = sb("qnmax_sb", [128, 32], F32)
    qmx32_sb = sb("qmx32_sb", [32, 1], F32)
    qmaxr_sb = sb("qmaxr_sb", [1, 32], F32)
    prodr_sb = sb("prodr_sb", [1, 32], F32)
    sqrow_sb = sb("sqrow_sb", [1, 32], BF16)
    biasa_sb = sb("biasa_sb", [128, 32], F32)
    gs_sb = sb("gs_sb", [128, 44], F32)
    us_sb = sb("us_sb", [128, 44], F32)
    ds_sb = sb("ds_sb", [128, 16], F32)
    sel_sb = sb("sel_sb", [1, 8], I32)
    idb_sb = sb("idb_sb", [128, 128], BF16)
    idf_sb = sb("idf_sb", [128, 128], F32)
    pones_sb = sb("pones_sb", [1, 128], BF16)
    mones_sb = sb("mones_sb", [1, 128], BF16)
    tmp4_sb = sb("tmp4_sb", [128, 1024], F32)

    psT0 = ctx.enter_context(nc.psum_tensor("psT0", [128, 1024], BF16))
    psT1 = ctx.enter_context(nc.psum_tensor("psT1", [128, 1024], BF16))
    psF = [ctx.enter_context(nc.psum_tensor(f"psF{i}", [128, 512], F32))
           for i in range(4)]
    psAV = ctx.enter_context(nc.psum_tensor("psAV", [128, 1024], F32))
    psAVb = psAV[:].bitcast(BF16)

    def av_(c0, n, r0=0, r1=128):
        return arena[r0:r1, c0:c0 + n]

    def avf(c0, nf32, r0=0, r1=128):
        return arena[r0:r1, c0:c0 + 2 * nf32].bitcast(F32)

    probe_ap = arena[64:128, 5:517]
    assert probe_ap.offset == 64 * ACOLS + 5, probe_ap.offset

    xnt_own = av_(XNT_OWN, 16384)
    attn_ot = av_(ATTN_OT, 16384)
    h_sb = av_(H_OFF, ITL * 1024)
    ktA = av_(KTA_O, 4608)
    ktB = av_(KTB_O, 4608)
    vS = av_(VS_O, 36 * 260)
    ctok = av_(CTOK_O, NTILE * 64)
    masks = av_(MASK_O, 2048)
    wkv_sb = av_(WKV_O, 8192)
    xn_v = [av_(XN_O + i * 2048, 2048) for i in range(4)]
    ktok_v = [av_(KTOK_O + i * 256, 256) for i in range(2)]
    xbuf_v = [avf(XBUF_O + i * 4096, 2048) for i in range(4)]

    ev_wkv = P.emit("gpsimd", lambda e: e.dma_start(out=wkv_sb, in_=wkv_h),
                    inc="dw", n=16)
    evc = None
    for dst, srca in ((ctok, ctok_h), (masks, mask_h), (idb_sb[:], idb_h),
                      (pones_sb[:], pones_h), (mones_sb[:], mones_h)):
        evc = P.emit("gpsimd", lambda e, d=dst, s=srca: e.dma_start(out=d, in_=s),
                     inc="dw", n=16)
    for dst, srca in ((sel_sb[:], sel_h), (idf_sb[:], idf_h),
                      (gs_sb[:], gs_h), (us_sb[:], us_h), (ds_sb[:], ds_h)):
        evc2 = P.emit("sync", lambda e, d=dst, s=srca: e.dma_start(out=d, in_=s),
                      inc="dx", n=16)
    ev_consts = emax(evc, evc2)

    ev_knz = P.emit("vector", lambda e: e.memset(knmax_sb[:], 0.0), inc="ve")
    ev_qnz = P.emit("vector", lambda e: e.memset(qnmax_sb[:], 0.0), inc="ve")
    ones_ap = bass.AP(arena, VS_O + 64, [[ACOLS, 128], [260, 36], [65, 4], [1, 1]])
    ev_vones = P.emit("vector", lambda e, a=ones_ap: e.memset(a, 1.0), inc="ve")

    evXL = [None] * NTILE
    evXN = [None] * NTILE
    evS3 = [None] * NTILE
    evS4 = [None] * NTILE
    evS5 = [None] * NTILE
    evKR = [None] * NTILE
    evKN = [None] * NTILE
    evS7 = [None] * NTILE
    evS8 = [None] * NTILE
    evVE = [None] * NTILE

    def xnt_src_slice(t, j):
        s = t // 4
        if s >= 7:
            return av_(XNT_OWN + j * 1024 + (s - 7) * 512 + (t % 4) * 128, 128)
        return av_(XNTT_O + (t % 2) * 2048 + j * 128, 128)

    lastT_user = [-1]
    lastA_user = [-1]
    for step in range(NTILE + 2):
        t = step
        if step >= 1 and (step - 1) < NTILE:
            t1 = step - 1
            b1 = t1 % 2
            last = None
            for j in range(HT):
                lhsT = xnt_src_slice(t1, j)
                last = P.emit("tensor", lambda e, l=lhsT, j=j, b1=b1: e.matmul(
                    psF[b1][:], l, wkv_sb[:, j * 512:(j + 1) * 512],
                    start=(j == 0), stop=(j == HT - 1)),
                    wait=emax(evS4[t1], ev_wkv,
                              evKR[t1 - 2] if t1 >= 2 else None,
                              evVE[t1 - 2] if t1 >= 2 else None),
                    inc="pe" if j == HT - 1 else None)
            evS5[t1] = last
            vdst = bass.AP(arena, VS_O + t1 * 260, [[ACOLS, 128], [65, 4], [1, 64]])
            vsrc = bass.AP(psF[b1], 256, [[512, 128], [64, 4], [1, 64]])
            evVE[t1] = P.emit("scalar", lambda e, d=vdst, s=vsrc: e.copy(d, s),
                              wait=emax(evS5[t1], ev_vones), inc="se")
            x1 = bass.AP(psF[b1], 0, [[512, 128], [64, 4], [1, 32]])
            x2 = bass.AP(psF[b1], 32, [[512, 128], [64, 4], [1, 32]])
            cosb = bass.AP(arena, CTOK_O + t1 * 64, [[ACOLS, 128], [0, 4], [1, 32]])
            sinb = bass.AP(arena, CTOK_O + t1 * 64 + 32,
                           [[ACOLS, 128], [0, 4], [1, 32]])
            k1 = bass.AP(arena, KTOK_O + b1 * 256, [[ACOLS, 128], [64, 4], [1, 32]])
            k2 = bass.AP(arena, KTOK_O + b1 * 256 + 32,
                         [[ACOLS, 128], [64, 4], [1, 32]])
            rA = bass.AP(arena, RTMPA_O, [[ACOLS, 128], [32, 4], [1, 32]])
            rB = bass.AP(arena, RTMPB_O, [[ACOLS, 128], [32, 4], [1, 32]])
            wr = emax(evS5[t1], evVE[t1], ev_consts,
                      evS7[t1 - 2] if t1 >= 2 else None)
            P.emit("vector", lambda e, o=rA, a=x2, c=sinb: e.tensor_mul(o, a, c),
                   wait=wr)
            P.emit("vector", lambda e, o=rB, a=x1, c=sinb: e.tensor_mul(o, a, c))
            P.emit("vector", lambda e, o=k1, a=x1, c=cosb: e.tensor_mul(o, a, c))
            P.emit("vector", lambda e, o=k1, a=rA: e.tensor_sub(o, o, a))
            P.emit("vector", lambda e, o=k2, a=x2, c=cosb: e.tensor_mul(o, a, c))
            evKR[t1] = P.emit("vector", lambda e, o=k2, a=rB: e.tensor_add(o, o, a),
                              inc="ve")
            eq = P.emit("scalar", lambda e, b1=b1: e.square(ksqt_sb[:], ktok_v[b1]),
                        wait=emax(evKR[t1], evKN[t1 - 1] if t1 >= 1 else None),
                        inc="se")
            er = None
            for r in range(4):
                er = P.emit("vector", lambda e, r=r: e.reduce_sum(
                    kn4_sb[:, r:r + 1], ksqt_sb[:, r * 64:(r + 1) * 64], axis=AX.X),
                    wait=emax(eq), inc="ve")
            evKN[t1] = P.emit("vector", lambda e: e.tensor_max(
                knmax_sb[:], knmax_sb[:], kn4_sb[:]), wait=emax(er, ev_knz), inc="ve")
        if step >= 2 and (step - 2) < NTILE:
            t2 = step - 2
            b2 = t2 % 2
            psF2b = psF[2][:].bitcast(BF16)
            last = None
            for blk in range(2):
                last = P.emit("tensor", lambda e, b2=b2, blk=blk, p=psF2b: e.transpose(
                    p[:, blk * 128:(blk + 1) * 128],
                    ktok_v[b2][:, blk * 128:(blk + 1) * 128], idb_sb[:]),
                    wait=emax(evKR[t2], evS8[t2 - 1] if t2 >= 1 else None),
                    inc="pe" if blk == 1 else None)
            evS7[t2] = last
            P.emit("scalar", lambda e, t2=t2, p=psF2b: e.copy(
                av_(KTA_O + t2 * 128, 128), p[:, 0:128]), wait=emax(evS7[t2]))
            evS8[t2] = P.emit("scalar", lambda e, t2=t2, p=psF2b: e.copy(
                av_(KTB_O + t2 * 128, 128), p[:, 128:256]), inc="se")
        if t < NTILE:
            b = t % 4
            b2p = t % 2
            src_x = xkv[t * 128:(t + 1) * 128, :]
            evXL[t] = P.emit(
                "sync", lambda e, s=src_x, d=xbuf_v[b]: e.dma_start(out=d, in_=s),
                wait=emax(evXN[t - 4] if t >= 4 else None), inc="dx", n=16)
            e1 = None
            for i in range(4):
                e1 = P.emit("vector", lambda e, i=i, b=b: e.bn_stats(
                    bn_sb[:, b * 24 + i * 6:b * 24 + (i + 1) * 6],
                    xbuf_v[b][:, i * 512:(i + 1) * 512]),
                    wait=emax(evXL[t]), inc="ve")
            e2 = P.emit("vector", lambda e, b=b: e.bn_aggr(
                agg_sb[:, 2 * b:2 * b + 2], bn_sb[:, b * 24:b * 24 + 24]),
                wait=emax(e1), inc="ve")
            e3 = P.emit("vector", lambda e, b=b: e.tensor_scalar_add(
                varp_sb[:, b:b + 1], agg_sb[:, 2 * b + 1:2 * b + 2], EPS),
                wait=emax(e2), inc="ve")
            e4 = P.emit("vector", lambda e, b=b: e.reciprocal(
                varp_sb[:, b:b + 1], varp_sb[:, b:b + 1]), wait=emax(e3), inc="ve")
            e5 = P.emit("scalar", lambda e, b=b: e.sqrt(
                rstd_sb[:, b:b + 1], varp_sb[:, b:b + 1]), wait=emax(e4), inc="se")
            e6 = P.emit("vector", lambda e, b=b: e.tensor_scalar(
                out=negmr_sb[:, b:b + 1], in0=agg_sb[:, 2 * b:2 * b + 1],
                scalar1=rstd_sb[:, b:b + 1], scalar2=-1.0,
                op0=ALU.mult, op1=ALU.mult),
                wait=emax(e5), inc="ve")
            evXN[t] = P.emit("scalar", lambda e, b=b: e.activation(
                out=xn_v[b], in_=xbuf_v[b], func=AF.Identity,
                bias=negmr_sb[:, b:b + 1], scale=rstd_sb[:, b:b + 1]),
                wait=emax(e6, evS3[t - 4] if t >= 4 else None), inc="se")
            s = t // 4
            own = s >= 7
            use_psT = own or (t % 2 == 0)
            tp_wait = emax(evXN[t],
                           evS4[lastT_user[0]] if (use_psT and lastT_user[0] >= 0)
                           else None,
                           evS4[lastA_user[0]] if ((not use_psT) and
                                                   lastA_user[0] >= 0) else None,
                           ev_consts if t == 0 else None)
            last = None
            for j in range(16):
                if use_psT:
                    ps = psT0[:] if j < 8 else psT1[:]
                    jj = j % 8
                else:
                    ps = psAVb
                    jj = j
                last = P.emit("tensor", lambda e, b=b, j=j, ps=ps, jj=jj: e.transpose(
                    ps[:, jj * 128:(jj + 1) * 128],
                    xn_v[b][:, j * 128:(j + 1) * 128], idb_sb[:]),
                    wait=tp_wait if j == 0 else (),
                    inc="pe" if j == 15 else None)
            evS3[t] = last
            if use_psT:
                lastT_user[0] = t
            else:
                lastA_user[0] = t
            if own:
                off = XNT_OWN + (s - 7) * 512 + (t % 4) * 128
                d0 = bass.AP(arena, off, [[ACOLS, 128], [1024, 8], [1, 128]])
                d1 = bass.AP(arena, off + 8 * 1024, [[ACOLS, 128], [1024, 8], [1, 128]])
                s0 = bass.AP(psT0, 0, [[1024, 128], [128, 8], [1, 128]])
                s1 = bass.AP(psT1, 0, [[1024, 128], [128, 8], [1, 128]])
            elif use_psT:
                d0 = av_(XNTT_O + b2p * 2048, 1024)
                d1 = av_(XNTT_O + b2p * 2048 + 1024, 1024)
                s0, s1 = psT0[:], psT1[:]
            else:
                d0 = av_(XNTT_O + b2p * 2048, 1024)
                d1 = av_(XNTT_O + b2p * 2048 + 1024, 1024)
                s0, s1 = psAVb[:, 0:1024], psAVb[:, 1024:2048]
            w4 = emax(evS3[t], evS5[t - 2] if (t >= 2 and (t - 2) // 4 < 7) else None)
            P.emit("scalar", lambda e, d=d0, s_=s0: e.copy(d, s_), wait=w4)
            evS4[t] = P.emit("scalar", lambda e, d=d1, s_=s1: e.copy(d, s_),
                             inc="se")

    eT1 = P.emit("tensor", lambda e: e.transpose(
        psF[0][0:4, 0:128], knmax_sb[:], idf_sb[:]),
        wait=emax(evKN[NTILE - 1], evVE[NTILE - 1], evKR[NTILE - 1]), inc="pe")
    eT2 = P.emit("vector", lambda e: e.reduce_max(
        knm4_sb[:], psF[0][0:4, 0:128], axis=AX.X), wait=emax(eT1), inc="ve")
    eT3 = P.emit("tensor", lambda e: e.transpose(
        psF[0][0:1, 0:4], knm4_sb[:], idf_sb[0:4, 0:4]), wait=emax(eT2), inc="pe")
    evMK2 = P.emit("vector", lambda e: e.tensor_copy(mk2_sb[:], psF[0][0:1, 0:4]),
                   wait=emax(eT3), inc="ve")

    dbg_gate = [None]

    def dump_now(name, w):
        if name not in dbg:
            return
        ev = P.emit("sync", lambda e, d=dbg_t[name], s=dmap0[name]: e.dma_start(
            out=d, in_=s), wait=w, inc="dout", n=16)
        dbg_gate[0] = emax(dbg_gate[0], ev)

    dmap0 = {
        "xnT_own": av_(XNT_OWN, 16384), "ktA": ktA, "ktB": ktB, "vS": vS,
        "mk2row": mk2_sb[:], "qT": av_(QT_O, 16384),
        "bias_all": biasa_sb[:], "attn_oT": av_(ATTN_OT, 16384),
        "hdbg": h_sb, "qtok": av_(QTOK_O, 16384),
    }
    for nm in ("xnT_own", "ktA", "ktB", "vS", "mk2row"):
        dump_now(nm, emax(evMK2, evS8[NTILE - 1], evVE[NTILE - 1], evS4[NTILE - 1],
                          evKR[NTILE - 1]))

    qtok_v = [av_(QTOK_O + tt * 2048, 2048) for tt in range(8)]
    wqc_v = [av_(WQC_O + i * 4096, 4096) for i in range(2)]
    attA_v = [av_(ATTA_O + i * 512, 512) for i in range(3)]
    attB_v = [av_(ATTB_O + i * 512, 512) for i in range(3)]
    psT0f = psT0[:].bitcast(F32)
    psT1f = psT1[:].bitcast(F32)
    zero_v = av_(ZERO_O, 512)
    rbff_row = avf(RBFF_O, 1024, 0, 1)
    rbf2_row = arena[0:1, RBF2_O:RBF2_O + 1024]
    rbsb = avf(RBSB_O, 1024, 0, 64)
    atmp = av_(ATMP_O, 1024, 0, 64)

    ev_zero = P.emit("vector", lambda e: e.memset(zero_v, 0.0), inc="ve")

    REGS = {}

    def reg_setup(e):
        st = contextlib.ExitStack()
        REGS["stack"] = st
        REGS["cid"] = st.enter_context(e.register("cid"))
        e.reg_load(REGS["cid"], sel_sb[0:1, 7:8])
        return None

    P.emit("tensor", reg_setup, wait=emax(ev_consts))

    evQMM = {}
    evQE = {}
    for oc in range(8):
        evWQ = P.emit("gpsimd", lambda e, oc=oc: e.dma_start(
            out=wqc_v[oc % 2], in_=wq_h[oc]),
            wait=emax(evQMM.get((oc - 2, 7)),
                      emax(evKR[NTILE - 1], evVE[NTILE - 1])
                      if oc < 2 else None),
            inc="dw", n=16)
        for tt in range(8):
            last = None
            for j in range(HT):
                lhsT = av_(XNT_OWN + j * 1024 + tt * 128, 128)
                last = P.emit("tensor", lambda e, l=lhsT, j=j, tt=tt, oc=oc: e.matmul(
                    psF[2 + tt % 2][:, 0:256], l,
                    wqc_v[oc % 2][:, j * 256:(j + 1) * 256],
                    start=(j == 0), stop=(j == HT - 1)),
                    wait=emax(evWQ, evS4[NTILE - 1], evS8[NTILE - 1],
                              evQE.get((oc, tt - 2)), evQE.get((oc - 1, tt + 6))),
                    inc="pe" if j == HT - 1 else None)
            evQMM[(oc, tt)] = last
            evQE[(oc, tt)] = P.emit("vector", lambda e, oc=oc, tt=tt: e.tensor_copy(
                av_(QTOK_O + tt * 2048 + oc * 256, 256), psF[2 + tt % 2][:, 0:256]),
                wait=emax(evQMM[(oc, tt)]), inc="ve")

    evQR = [None] * 8
    evQN = [None] * 8
    evQT = [None] * 8
    evQTE = [None] * 8
    for tt in range(8):
        base = QTOK_O + tt * 2048
        x1 = bass.AP(arena, base, [[ACOLS, 128], [64, 32], [1, 32]])
        x2 = bass.AP(arena, base + 32, [[ACOLS, 128], [64, 32], [1, 32]])
        cosb = bass.AP(arena, CTOK_O + (28 + tt) * 64, [[ACOLS, 128], [0, 32], [1, 32]])
        sinb = bass.AP(arena, CTOK_O + (28 + tt) * 64 + 32,
                       [[ACOLS, 128], [0, 32], [1, 32]])
        rA = bass.AP(arena, RTMPA_O, [[ACOLS, 128], [32, 32], [1, 32]])
        rB = bass.AP(arena, RTMPB_O, [[ACOLS, 128], [32, 32], [1, 32]])
        w0 = emax(evQE[(7, tt)], ev_consts, evQMM[(7, 7)])
        P.emit("vector", lambda e, o=rA, a=x2, c=sinb: e.tensor_mul(o, a, c), wait=w0)
        P.emit("vector", lambda e, o=rB, a=x1, c=sinb: e.tensor_mul(o, a, c))
        P.emit("vector", lambda e, o=x1, a=x1, c=cosb: e.tensor_mul(o, a, c))
        P.emit("vector", lambda e, o=x1, a=rA: e.tensor_sub(o, o, a))
        P.emit("vector", lambda e, o=x2, a=x2, c=cosb: e.tensor_mul(o, a, c))
        evQR[tt] = P.emit("vector", lambda e, o=x2, a=rB: e.tensor_add(o, o, a),
                          inc="ve")
        eqs = P.emit("scalar", lambda e, tt=tt: e.square(av_(QSQT_O, 2048), qtok_v[tt]),
                     wait=emax(evQR[tt], evQN[tt - 1] if tt >= 1 else None), inc="se")
        er = None
        for hh in range(HQ):
            er = P.emit("vector", lambda e, hh=hh: e.reduce_sum(
                qn_sb[:, hh:hh + 1], av_(QSQT_O + hh * 64, 64), axis=AX.X),
                wait=emax(eqs), inc="ve")
        evQN[tt] = P.emit("vector", lambda e: e.tensor_max(
            qnmax_sb[:], qnmax_sb[:], qn_sb[:]), wait=emax(er, ev_qnz), inc="ve")
        last = None
        for j in range(16):
            ps = psT0 if j < 8 else psT1
            jj = j % 8
            last = P.emit("tensor", lambda e, tt=tt, j=j, ps=ps, jj=jj: e.transpose(
                ps[:, jj * 128:(jj + 1) * 128],
                qtok_v[tt][:, j * 128:(j + 1) * 128], idb_sb[:]),
                wait=emax(evQR[tt], evQTE[tt - 1] if tt >= 1 else None,
                          evS4[NTILE - 1]),
                inc="pe" if j == 15 else None)
        evQT[tt] = last
        d0 = bass.AP(arena, QT_O + tt * 128, [[ACOLS, 128], [1024, 8], [1, 128]])
        d1 = bass.AP(arena, QT_O + 8 * 1024 + tt * 128,
                     [[ACOLS, 128], [1024, 8], [1, 128]])
        s0 = bass.AP(psT0, 0, [[1024, 128], [128, 8], [1, 128]])
        s1 = bass.AP(psT1, 0, [[1024, 128], [128, 8], [1, 128]])
        P.emit("scalar", lambda e, d=d0, s_=s0: e.copy(d, s_),
               wait=emax(evQT[tt]))
        evQTE[tt] = P.emit("scalar", lambda e, d=d1, s_=s1: e.copy(d, s_),
                           inc="se")

    eU1 = P.emit("tensor", lambda e: e.transpose(
        psF[0][0:32, 0:128], qnmax_sb[:], idf_sb[:]),
        wait=emax(evQN[7], evQMM[(7, 7)], evMK2), inc="pe")
    eU2 = P.emit("vector", lambda e: e.reduce_max(
        qmx32_sb[:], psF[0][0:32, 0:128], axis=AX.X), wait=emax(eU1), inc="ve")
    eU3 = P.emit("tensor", lambda e: e.transpose(
        psF[0][0:1, 0:32], qmx32_sb[:], idf_sb[0:32, 0:32]), wait=emax(eU2), inc="pe")
    eU4 = P.emit("vector", lambda e: e.tensor_copy(qmaxr_sb[:], psF[0][0:1, 0:32]),
                 wait=emax(eU3), inc="ve")
    mk2bc = bass.AP(mk2_sb, 0, [[4, 1], [1, 4], [0, 8]])
    pr3 = bass.AP(prodr_sb, 0, [[32, 1], [8, 4], [1, 8]])
    qm3 = bass.AP(qmaxr_sb, 0, [[32, 1], [8, 4], [1, 8]])
    eU5 = P.emit("vector", lambda e, o=pr3, a=qm3, m=mk2bc: e.tensor_mul(o, a, m),
                 wait=emax(eU4, evMK2), inc="ve")
    eU6 = P.emit("vector", lambda e: e.tensor_scalar_mul(
        prodr_sb[:], prodr_sb[:], 1.0 / 64.0), wait=emax(eU5), inc="ve")
    eU7 = P.emit("scalar", lambda e: e.sqrt(sqrow_sb[:], prodr_sb[:]),
                 wait=emax(eU6), inc="se")
    dump_now("qtok", emax(evQR[7], evQN[7], evQTE[7]))
    wk2 = emax(evQTE[7], evQN[7], evS8[NTILE - 1], dbg_gate[0])
    P.emit("scalar", lambda e: e.copy(arena[0:64, KT2A_O:KT2A_O + 4608],
                                      arena[64:128, KTA_O:KTA_O + 4608]), wait=wk2)
    P.emit("scalar", lambda e: e.copy(arena[64:128, KT2A_O:KT2A_O + 4608],
                                      arena[0:64, KTA_O:KTA_O + 4608]))
    P.emit("scalar", lambda e: e.copy(arena[0:64, KT2B_O:KT2B_O + 4608],
                                      arena[64:128, KTB_O:KTB_O + 4608]))
    ev_kt2 = P.emit("scalar", lambda e: e.copy(arena[64:128, KT2B_O:KT2B_O + 4608],
                                               arena[0:64, KTB_O:KTB_O + 4608]),
                    inc="se")

    eB1 = P.emit("tensor", lambda e: e.matmul(
        psF[2][:, 0:32], mones_sb[:], sqrow_sb[:], start=True, stop=True),
        wait=emax(eU7, evQE[(7, 7)]), inc="pe")
    evBIAS = P.emit("vector", lambda e: e.tensor_copy(biasa_sb[:], psF[2][:, 0:32]),
                    wait=emax(eB1), inc="ve")

    dump_now("qT", emax(evQTE[7]))
    dump_now("bias_all", emax(evBIAS))

    blocks = [(7, m) for m in range(4)] + [(8, m) for m in range(4)] + \
             [(k, m) for k in range(7) for m in range(4)]
    NB = len(blocks)
    pairs = []
    for gkt, alo, bhi in ((KTA_O, 0, 9), (KT2A_O, 8, 1),
                          (KTB_O, 16, 25), (KT2B_O, 24, 17)):
        for i in range(4):
            pairs.append((gkt, alo + 2 * i, bhi + 2 * i))

    pe_ops = [[] for _ in range(8)]

    def pe_emit(fns, wait=()):
        P.cnt["pe"] += 1
        val = P.cnt["pe"]
        w = tuple((s, v) for (s, v) in emax(*wait) if v > 0) if wait else ()
        if not isinstance(fns, list):
            fns = [fns] * 8
        for ci in range(8):
            pe_ops[ci].append((w, fns[ci]))
        return ("pe", val)

    prev_sc_free = [emax(eU2, eU4, evBIAS, eT2, evMK2)] * 4
    prev_av_free = emax(evS8[NTILE - 1], evMK2, evQTE[7], evS4[NTILE - 2], evS4[NTILE - 1])
    prev_rb_free = emax(evBIAS)
    prev_avlast = None
    prev_atmp = None
    ev_attn_done_v = None
    ev_attn_done_s = None

    for pi, (kt_base, hA, hB) in enumerate(pairs):
        qcbA = QT_O + (hA // 2) * 1024
        qcbB = QT_O + (hB // 2) * 1024 + 64 * ACOLS
        kvA = hA // 8
        kvB = hB // 8

        evSCA = [None] * NB
        evSCB = [None] * NB
        evEXA = [None] * NB
        evEXB = [None] * NB
        evMKA = [None] * NB
        evMKB = [None] * NB
        evAVA = [None] * NB
        evAVB = [None] * NB

        def sc_fns(qcb, rowbase, slot, kb, psdst):
            lhs = arena[rowbase:rowbase + 64,
                        kt_base + kb * 128:kt_base + (kb + 1) * 128]
            if slot >= 7:
                r = bass.AP(arena, qcb + (512 if slot == 8 else 0),
                            [[ACOLS, 64], [1, 512]])
                return (lambda e, l=lhs, r=r, o=psdst:
                        e.matmul(o, l, r, start=True, stop=True))
            fns = []
            for ci in range(8):
                sel = 0 if slot < ci else 512
                r = bass.AP(arena, qcb + sel, [[ACOLS, 64], [1, 512]])
                fns.append(lambda e, l=lhs, r=r, o=psdst:
                           e.matmul(o, l, r, start=True, stop=True))
            return fns

        def emit_avs(bi):
            slot, m = blocks[bi]
            kb = slot * 4 + m
            st = (slot >= 7 and m == 0)
            vpA = vS[:, kb * 260 + kvA * 65:kb * 260 + (kvA + 1) * 65]
            vpB = vS[:, kb * 260 + kvB * 65:kb * 260 + (kvB + 1) * 65]
            aA = attA_v[bi % 3]
            aB = attB_v[bi % 3]
            if slot >= 7:
                oA = psAV[0:65, 0:512] if slot == 7 else psAV[0:65, 512:1024]
                fA = (lambda e, v=vpA, o=oA, a=aA, st=st:
                      e.matmul(o, v, a, start=st, stop=False))
                oB = psT0f[0:65, :] if slot == 7 else psT1f[0:65, :]
                fB = (lambda e, v=vpB, o=oB, a=aB, st=st:
                      e.matmul(o, v, a, start=st, stop=False))
            else:
                fA = []
                fB = []
                for ci in range(8):
                    sel = 0 if slot < ci else 512
                    oA = psAV[0:65, sel:sel + 512]
                    fA.append(lambda e, v=vpA, o=oA, a=aA:
                              e.matmul(o, v, a, start=False, stop=False))
                    oB = psT0f[0:65, :] if sel == 0 else psT1f[0:65, :]
                    fB.append(lambda e, v=vpB, o=oB, a=aB:
                              e.matmul(o, v, a, start=False, stop=False))
            evAVA[bi] = pe_emit(fA, wait=[evMKA[bi],
                                          prev_av_free if bi == 0 else None])
            evAVB[bi] = pe_emit(fB, wait=[evMKB[bi],
                                          prev_av_free if bi == 0 else None])

        for bi, (slot, m) in enumerate(blocks):
            kb = slot * 4 + m
            wA = [evEXA[bi - 2] if bi >= 2 else prev_sc_free[bi % 2],
                  emax(evQTE[7], evS8[NTILE - 1], ev_kt2)
                  if (pi == 0 and bi == 0) else None]
            evSCA[bi] = pe_emit(sc_fns(qcbA, 0, slot, kb, psF[bi % 2][:]), wait=wA)
            wB = [evEXB[bi - 2] if bi >= 2 else prev_sc_free[2 + bi % 2]]
            evSCB[bi] = pe_emit(sc_fns(qcbB, 64, slot, kb, psF[2 + bi % 2][:]),
                                wait=wB)
            aA = attA_v[bi % 3]
            aB = attB_v[bi % 3]
            evEXA[bi] = P.emit(
                "scalar", lambda e, o=aA, i=psF[bi % 2], hh=hA: e.activation(
                    out=o, in_=i[:], func=AF.Exp,
                    bias=biasa_sb[:, hh:hh + 1], scale=0.125),
                wait=emax(evSCA[bi], evBIAS,
                          evAVA[bi - 3] if bi >= 3 else prev_avlast),
                inc="se")
            evEXB[bi] = P.emit(
                "scalar", lambda e, o=aB, i=psF[2 + bi % 2], hh=hB: e.activation(
                    out=o, in_=i[:], func=AF.Exp,
                    bias=biasa_sb[:, hh:hh + 1], scale=0.125),
                wait=emax(evSCB[bi],
                          evAVB[bi - 3] if bi >= 3 else prev_avlast),
                inc="se")
            if slot >= 7:
                evMKA[bi] = P.emit(
                    "vector", lambda e, o=aA, m=m: e.tensor_mul(
                        o, o, masks[:, m * 512:(m + 1) * 512]),
                    wait=emax(evEXA[bi]), inc="ve")
                evMKB[bi] = P.emit(
                    "vector", lambda e, o=aB, m=m: e.tensor_mul(
                        o, o, masks[:, m * 512:(m + 1) * 512]),
                    wait=emax(evEXB[bi]), inc="ve")
            else:
                evMKA[bi] = evEXA[bi]
                evMKB[bi] = evEXB[bi]
            if bi >= 1:
                emit_avs(bi - 1)
        emit_avs(NB - 1)

        eZ1 = pe_emit(lambda e: e.matmul(
            psAV[0:65, 0:512], vS[:, 0:65], zero_v, start=False, stop=True),
            wait=[ev_zero, evAVA[NB - 1], evAVB[NB - 1]])
        eZ2 = pe_emit(lambda e: e.matmul(
            psAV[0:65, 512:1024], vS[:, 0:65], zero_v, start=False, stop=True))
        eZ3 = pe_emit(lambda e: e.matmul(
            psT0f[0:65, :], vS[:, 0:65], zero_v, start=False, stop=True))
        eZ4 = pe_emit(lambda e: e.matmul(
            psT1f[0:65, :], vS[:, 0:65], zero_v, start=False, stop=True))

        out_evs = []
        for role, (hh, sums_ops, avsrc) in enumerate((
                (hA, [lambda e: e.copy(rbff_row, psAV[64:65, 0:1024])],
                 psAV[0:64, 0:1024]),
                (hB, [lambda e: e.copy(rbff_row[:, 0:512], psT0f[64:65, :]),
                      lambda e: e.copy(rbff_row[:, 512:1024], psT1f[64:65, :])],
                 None))):
            wz = emax(eZ4, out_evs[-1] if out_evs else None)
            eR0 = None
            for so in sums_ops:
                eR0 = P.emit("scalar", so, wait=wz, inc="se")
            eR1 = P.emit("vector", lambda e: e.reciprocal(rbff_row, rbff_row),
                         wait=emax(eR0), inc="ve")
            eR1b = P.emit("vector", lambda e: e.tensor_copy(rbf2_row, rbff_row),
                          wait=emax(eR1), inc="ve")
            eR2 = pe_emit(lambda e: e.matmul(
                psF[0][0:64, :], pones_sb[0:1, 0:64],
                arena[0:1, RBF2_O:RBF2_O + 512], start=True, stop=True),
                wait=[eR1b, prev_rb_free,
                      emax(evEXA[NB - 2], evEXA[NB - 1]) if role == 0 else None])
            eR3 = pe_emit(lambda e: e.matmul(
                psF[1][0:64, :], pones_sb[0:1, 0:64],
                arena[0:1, RBF2_O + 512:RBF2_O + 1024], start=True, stop=True))
            P.emit("vector", lambda e: e.tensor_copy(rbsb[:, 0:512], psF[0][0:64, :]),
                   wait=emax(eR2, eR3))
            eR4 = P.emit("vector", lambda e: e.tensor_copy(
                rbsb[:, 512:1024], psF[1][0:64, :]), inc="ve")
            prev_rb_free = emax(eR4)
            col = ATTN_OT + (hh // 2) * 1024
            if role == 0:
                def mul_fn(e, d=None):
                    return e.tensor_mul(d, psAV[0:64, 0:1024], rbsb[:])
                srcap = psAV[0:64, 0:1024]
            else:
                srcap = None
            if hh % 2 == 0:
                dst = arena[0:64, col:col + 1024]
                if role == 0:
                    evO = P.emit("vector", lambda e, d=dst: e.tensor_mul(
                        d, psAV[0:64, 0:1024], rbsb[:]), wait=emax(eR4), inc="ve")
                else:
                    evO = P.emit("vector", lambda e, d=dst: e.tensor_mul(
                        d[:, 0:512], psT0f[0:64, :], rbsb[:, 0:512]),
                        wait=emax(eR4), inc="ve")
                    evO = P.emit("vector", lambda e, d=dst: e.tensor_mul(
                        d[:, 512:1024], psT1f[0:64, :], rbsb[:, 512:1024]),
                        inc="ve")
                out_evs.append(evO)
                ev_attn_done_v = evO
            else:
                if role == 0:
                    evO = P.emit("vector", lambda e: e.tensor_mul(
                        atmp, psAV[0:64, 0:1024], rbsb[:]),
                        wait=emax(eR4, prev_atmp), inc="ve")
                else:
                    evO = P.emit("vector", lambda e: e.tensor_mul(
                        atmp[:, 0:512], psT0f[0:64, :], rbsb[:, 0:512]),
                        wait=emax(eR4, prev_atmp), inc="ve")
                    evO = P.emit("vector", lambda e: e.tensor_mul(
                        atmp[:, 512:1024], psT1f[0:64, :], rbsb[:, 512:1024]),
                        inc="ve")
                dst = arena[64:128, col:col + 1024]
                evO2 = P.emit("scalar", lambda e, d=dst: e.copy(d, atmp),
                              wait=emax(evO), inc="se")
                prev_atmp = evO2
                out_evs.append(emax(evO, evO2))
                ev_attn_done_v = evO
                ev_attn_done_s = evO2
        prev_av_free = emax(*out_evs)
        prev_avlast = emax(evAVA[NB - 1], evAVB[NB - 1])
        prev_sc_free = [emax(eR4, out_evs[-1])] * 4

    def attn_pe_block(e):
        for ci in range(8):
            with e.If_eq(REGS["cid"], ci):
                last = {}
                for w, fn in pe_ops[ci]:
                    for s, v in w:
                        if last.get(s, 0) < v:
                            e.wait_ge(semh[s], v)
                            last[s] = v
                    fn(e).then_inc(semh["pe"], 1)
        return None
    P.emit("tensor", attn_pe_block)

    ev_attn_all = emax(ev_attn_done_v, ev_attn_done_s)

    gw_v = [av_(GW_O + i * 4096, 4096) for i in range(2)]
    uw_v = [av_(UW_O + i * 4096, 4096) for i in range(2)]
    gst = av_(GST_O, 1024)
    ust = av_(UST_O, 1024)

    dump_now("attn_oT", ev_attn_all)
    evGW = {}
    evUW = {}
    evGMM = {}
    evUMM = {}
    evHM = {}
    gfree = {0: None, 1: None}
    ufree = {0: None, 1: None}
    last_gmm = None
    last_umm = None
    last_hm = None
    for ic in range(22):
        evGW[ic] = P.emit("gpsimd", lambda e, ic=ic: e.dma_start(
            out=gw_v[ic % 2], in_=gw_h[ic]),
            wait=emax(evGMM.get(ic - 2), ev_attn_all if ic < 2 else None,
                      dbg_gate[0] if ic == 0 else None),
            inc="dw", n=16)
        evUW[ic] = P.emit("gpsimd", lambda e, ic=ic: e.dma_start(
            out=uw_v[ic % 2], in_=uw_h[ic]),
            wait=emax(evUMM.get(ic - 2)), inc="dw", n=16)
        for it2 in range(2):
            itile = 2 * ic + it2
            for tb in range(2):
                pi = (2 * it2 + tb) % 2
                pg = psF[pi]
                pu = psF[2 + pi]
                last = None
                for j in range(HT):
                    lhsT = av_(GW_O + (ic % 2) * 4096 + j * 256 + it2 * 128, 128)
                    last = P.emit("tensor", lambda e, l=lhsT, j=j, p=pg, tb=tb: e.matmul(
                        p[:], l, av_(XNT_OWN + j * 1024 + tb * 512, 512),
                        start=(j == 0), stop=(j == HT - 1)),
                        wait=emax(evGW[ic], gfree[pi],
                                  ev_attn_all if ic == 0 else None),
                        inc="pe" if j == HT - 1 else None)
                last_gmm = evGMM[ic] = last
                last = None
                for j in range(HT):
                    lhsT = av_(UW_O + (ic % 2) * 4096 + j * 256 + it2 * 128, 128)
                    last = P.emit("tensor", lambda e, l=lhsT, j=j, p=pu, tb=tb: e.matmul(
                        p[:], l, av_(XNT_OWN + j * 1024 + tb * 512, 512),
                        start=(j == 0), stop=(j == HT - 1)),
                        wait=emax(evUW[ic], ufree[pi]),
                        inc="pe" if j == HT - 1 else None)
                last_umm = evUMM[ic] = last
                esil = P.emit(
                    "scalar", lambda e, p=pg, tb=tb, itile=itile: e.activation(
                        out=gst[:, tb * 512:(tb + 1) * 512], in_=p[:], func=AF.Silu,
                        scale=gs_sb[:, itile:itile + 1]),
                    wait=emax(last_gmm, evHM.get((itile - 1, tb))),
                    inc="se")
                gfree[pi] = esil
                eue = P.emit(
                    "vector", lambda e, p=pu, tb=tb, itile=itile: e.tensor_scalar_mul(
                        ust[:, tb * 512:(tb + 1) * 512], p[:],
                        us_sb[:, itile:itile + 1]),
                    wait=emax(last_umm, evHM.get((itile - 1, tb))),
                    inc="ve")
                ufree[pi] = eue
                last_hm = evHM[(itile, tb)] = P.emit(
                    "vector", lambda e, tb=tb, itile=itile: e.tensor_mul(
                        av_(H_OFF + itile * 1024 + tb * 512, 512),
                        gst[:, tb * 512:(tb + 1) * 512],
                        ust[:, tb * 512:(tb + 1) * 512]),
                    wait=emax(esil, eue), inc="ve")

    ev_ffn_mm_done = emax(last_gmm, last_umm)

    dump_now("hdbg", emax(last_hm))

    xq_v = [avf(XQ_O + a * 4096, 2048) for a in range(4)]
    ost_v = [avf(OST_O + i * 1024, 512) for i in range(2)]

    evXQ = {}
    evWXMM = {}
    evDNMM = {}
    evOST = {}
    evODMA = {}
    evE1 = {}
    ev_out = None
    for tb in range(2):
        for a in range(4):
            trow = (28 + tb * 4 + a) * 128
            evXQ[(tb, a)] = P.emit("sync", lambda e, a=a, trow=trow: e.dma_start(
                out=xq_v[a], in_=xkv[trow:trow + 128, :]),
                wait=emax(ev_ffn_mm_done, dbg_gate[0],
                          evWXMM.get((tb - 1, 15)) if tb >= 1 else None),
                inc="dx", n=16)
        for H in range(16):
            p2 = H % 2
            evDW = P.emit("gpsimd", lambda e, H=H, p2=p2: e.dma_start(
                out=av_(DW_O + p2 * 5632, 5632), in_=dw_h[H]),
                wait=emax(evE1.get((tb, H - 2)), ev_ffn_mm_done,
                          evDNMM.get((tb - 1, 14 + H)) if tb >= 1 else None),
                inc="dw", n=16)
            evWO = P.emit("gpsimd", lambda e, H=H, p2=p2: e.dma_start(
                out=av_(WO_O + p2 * 2048, 2048), in_=wo_h[H]),
                wait=emax(evWXMM.get((tb, H - 2)),
                          evWXMM.get((tb - 1, 14 + H)) if tb >= 1 else None),
                inc="dw", n=16)
            pw = psF[p2]
            pd = psF[2 + p2]
            last = None
            for j in range(HT):
                lhsT = av_(WO_O + p2 * 2048 + j * 128, 128)
                last = P.emit("tensor", lambda e, l=lhsT, j=j, p=pw, tb=tb: e.matmul(
                    p[:], l, av_(ATTN_OT + j * 1024 + tb * 512, 512),
                    start=(j == 0), stop=False),
                    wait=emax(evWO, ev_attn_all if (tb == 0 and H == 0) else None,
                              evOST.get((tb, H - 2)),
                              evOST.get((tb - 1, 14 + H)) if tb >= 1 else None),
                    inc=None)
            for a in range(4):
                last = P.emit("tensor", lambda e, a=a, H=H, p=pw: e.matmul(
                    p[:, a * 128:(a + 1) * 128],
                    xq_v[a][:, H * 128:(H + 1) * 128], idf_sb[:],
                    is_transpose=True, start=False, stop=(a == 3)),
                    wait=emax(evXQ[(tb, a)]), inc="pe" if a == 3 else None)
            evWXMM[(tb, H)] = last
            last = None
            for r in range(ITL):
                lhsT = av_(DW_O + p2 * 5632 + r * 128, 128)
                last = P.emit("tensor", lambda e, l=lhsT, r=r, p=pd, tb=tb: e.matmul(
                    p[:], l, av_(H_OFF + r * 1024 + tb * 512, 512),
                    start=(r == 0), stop=(r == ITL - 1)),
                    wait=emax(evDW, last_hm if (tb == 0 and H == 0) else None,
                              evE1.get((tb, H - 2)),
                              evE1.get((tb - 1, 14 + H)) if tb >= 1 else None),
                    inc="pe" if r == ITL - 1 else None)
            evDNMM[(tb, H)] = last
            e1 = P.emit("vector", lambda e, p=pd, p2=p2, H=H: e.tensor_scalar_mul(
                tmp4_sb[:, p2 * 512:(p2 + 1) * 512], p[:], ds_sb[:, H:H + 1]),
                wait=emax(evDNMM[(tb, H)]), inc="ve")
            evE1[(tb, H)] = e1
            evOST[(tb, H)] = P.emit("vector", lambda e, p=pw, p2=p2: e.tensor_add(
                ost_v[p2][:], p[:], tmp4_sb[:, p2 * 512:(p2 + 1) * 512]),
                wait=emax(evWXMM[(tb, H)], e1,
                          evODMA.get((tb, H - 2)),
                          evODMA.get((tb - 1, 14 + H)) if tb >= 1 else None),
                inc="ve")
            evODMA[(tb, H)] = P.emit("sync", lambda e, p2=p2, H=H, tb=tb: e.dma_start(
                out=out_t[H * 128:(H + 1) * 128, tb * 512:(tb + 1) * 512],
                in_=ost_v[p2][:]),
                wait=emax(evOST[(tb, H)]), inc="dout", n=16)

    P.emit("sync", None, wait=[("dout", P.cnt["dout"])])

    with nc.Block() as block:

        @block.sync
        def _(e):
            P.replay("sync", e, semh)

        @block.gpsimd
        def _(e):
            P.replay("gpsimd", e, semh)

        @block.tensor
        def _(e):
            P.replay("tensor", e, semh)

        @block.vector
        def _(e):
            P.replay("vector", e, semh)

        @block.scalar
        def _(e):
            P.replay("scalar", e, semh)

    return nc, ctx


def _tile_w(w, rows_per):
    R = w.shape[0]
    return np.ascontiguousarray(
        w.T.reshape(HT, 128, R).transpose(1, 0, 2).reshape(128, HT * R))


def _prep_shared(w_qkv, w_o, gate_w, up_w, down_w, gate_s, up_s, down_s):
    d = {}
    d["wkv_h"] = _tile_w(w_qkv[2048:2560, :], None)
    d["wq_h"] = np.stack([_tile_w(w_qkv[oc * 256:(oc + 1) * 256, :], None)
                          for oc in range(8)])
    d["wo_h"] = np.stack([
        np.ascontiguousarray(w_o[H * 128:(H + 1) * 128, :].T
                             .reshape(HT, 128, 128).transpose(1, 0, 2)
                             .reshape(128, 2048))
        for H in range(16)])
    d["gw_h"] = np.stack([
        np.ascontiguousarray(gate_w[ic * 256:(ic + 1) * 256, :].T
                             .reshape(HT, 128, 256).transpose(1, 0, 2)
                             .reshape(128, 4096))
        for ic in range(22)])
    d["uw_h"] = np.stack([
        np.ascontiguousarray(up_w[ic * 256:(ic + 1) * 256, :].T
                             .reshape(HT, 128, 256).transpose(1, 0, 2)
                             .reshape(128, 4096))
        for ic in range(22)])
    d["dw_h"] = np.stack([
        np.ascontiguousarray(down_w[H * 128:(H + 1) * 128, :].T
                             .reshape(ITL, 128, 128).transpose(1, 0, 2)
                             .reshape(128, 5632))
        for H in range(16)])
    d["gs_h"] = np.ascontiguousarray(gate_s.reshape(ITL, 128).T)
    d["us_h"] = np.ascontiguousarray(up_s.reshape(ITL, 128).T)
    d["ds_h"] = np.ascontiguousarray(down_s.reshape(16, 128).T)
    d["idb_h"] = np.eye(128, dtype=np.float32)
    d["idf_h"] = np.eye(128, dtype=np.float32)
    d["pones_h"] = np.ones((1, 128), dtype=np.float32)
    d["mones_h"] = -np.ones((1, 128), dtype=np.float32)
    mk = np.zeros((128, 4 * 512), dtype=np.float32)
    i = np.arange(512)
    j = np.arange(128)
    for m in range(4):
        mk[:, m * 512:(m + 1) * 512] = (128 * m + j[:, None] <= i[None, :])
    d["mask_h"] = mk
    return d


def _rope_tab(pos):
    inv = 1.0 / (10000.0 ** (np.arange(0, DH, 2, dtype=np.float64) / DH))
    ang = pos[:, None].astype(np.float64) * inv[None, :]
    ntile = len(pos) // 128
    tab = np.empty((128, ntile * 64), dtype=np.float32)
    c = np.cos(ang).astype(np.float32)
    s = np.sin(ang).astype(np.float32)
    for t in range(ntile):
        tab[:, t * 64:t * 64 + 32] = c[t * 128:(t + 1) * 128]
        tab[:, t * 64 + 32:t * 64 + 64] = s[t * 128:(t + 1) * 128]
    return tab


def _core_geometry(c):
    chunks = [(0, j) for j in range(c)] + [(1, j) for j in range(7 - c)] + \
             [(0, c), (1, 7 - c)]
    sel = np.zeros((1, 8), dtype=np.int32)
    for k in range(7):
        sel[0, k] = 0 if k < c else 512
    sel[0, 7] = c
    return chunks, sel


def kernel(L0_attn_hidden_states, L0_attn_w_qkv, L0_attn_w_o,
           L0_ffn_gate_w, L0_ffn_up_w, L0_ffn_down_w,
           L0_ffn_gate_s, L0_ffn_up_s, L0_ffn_down_s):
    x = np.ascontiguousarray(np.asarray(L0_attn_hidden_states, dtype=np.float32)
                             .reshape(2 * SEQ, HID))
    shared = _prep_shared(np.asarray(L0_attn_w_qkv, np.float32),
                          np.asarray(L0_attn_w_o, np.float32),
                          np.asarray(L0_ffn_gate_w, np.float32),
                          np.asarray(L0_ffn_up_w, np.float32),
                          np.asarray(L0_ffn_down_w, np.float32),
                          np.asarray(L0_ffn_gate_s, np.float32),
                          np.asarray(L0_ffn_up_s, np.float32),
                          np.asarray(L0_ffn_down_s, np.float32))

    in_maps = []
    for c in range(NCORE):
        chunks, sel = _core_geometry(c)
        rows = []
        pos = []
        for bt, j in chunks:
            rows.append(x[bt * SEQ + j * 512: bt * SEQ + (j + 1) * 512])
            pos.append(np.arange(j * 512, (j + 1) * 512))
        xkv = np.ascontiguousarray(np.concatenate(rows, 0))
        ctab = _rope_tab(np.concatenate(pos))
        m = dict(shared)
        m["xkv"] = xkv
        m["ctok_h"] = ctab
        m["sel_h"] = sel
        in_maps.append(m)

    nc, ctx = build_program()
    try:
        res = run_bass_kernel_spmd(nc, in_maps, core_ids=list(range(NCORE)))
    finally:
        ctx.close()

    out = np.empty((2 * SEQ, HID), dtype=np.float32)
    for c in range(NCORE):
        ot = np.asarray(res.results[c]["out_t"])
        out[c * 512:(c + 1) * 512] = ot[:, 0:512].T
        out[SEQ + (7 - c) * 512: SEQ + (8 - c) * 512] = ot[:, 512:1024].T
    return out.reshape(2, SEQ, HID)


# version 07
# speedup vs baseline: 1.2027x; 1.0535x over previous
import contextlib

import numpy as np
import ml_dtypes

import concourse.bass as bass
import concourse.mybir as mybir
from concourse.bass_utils import run_bass_kernel_spmd

F32 = mybir.dt.float32
BF16 = mybir.dt.bfloat16
I32 = mybir.dt.int32
AF = mybir.ActivationFunctionType
ALU = mybir.AluOpType
AX = mybir.AxisListType
BF = ml_dtypes.bfloat16

NCORE = 8
SEQ = 4096
HID = 2048
HQ = 32
DH = 64
INTER = 5632
NTILE = 36
KVROWS = NTILE * 128
TOWN = 1024
HT = 16
ITL = 44
EPS = 1e-5

ENG = ("sync", "gpsimd", "tensor", "vector", "scalar")

ACOLS = 101888
XNT_OWN = 0
ATTN_OT = 16384
H_OFF = 32768
KTA_O = 32768
KTB_O = 37376
VS_O = 41984
CTOK_O = 51344
MASK_O = 53648
WKV_O = 55696
XN_O = 63888
XNTT_O = 72080
KTOK_O = 76176
XBUF_O = 76688
QTOK_O = 16384
QT_O = 72080
WQC_O = 88464
RTMPA_O = 98304
RTMPB_O = 99328
QSQT_O = 90512
ATTA_O = 88464
ATTB_O = 90000
RBFF_O = 91536
RBF2_O = 93584
RBSB_O = 94608
ATMP_O = 96656
ZERO_O = 97680
KT2A_O = 55696
KT2B_O = 60304
GW_O = 77824
UW_O = 86016
GST_O = 94208
UST_O = 95232
XQ_O = 0
DW_O = 77824
WO_O = 89088
OST_O = 93184


class Prog:
    def __init__(self):
        self.ops = {e: [] for e in ENG}
        self.cnt = {}

    def sem(self, name):
        self.cnt.setdefault(name, 0)

    def emit(self, eng, fn, wait=(), inc=None, n=1):
        wait = tuple((s, v) for (s, v) in wait if s is not None and v > 0)
        ev = None
        if inc is not None:
            self.cnt[inc] += n
            ev = (inc, self.cnt[inc])
        self.ops[eng].append((fn, wait, inc, n))
        return ev

    def replay(self, eng, e, sems):
        last = {s: 0 for s in self.cnt}
        for fn, wait, inc, n in self.ops[eng]:
            for s, v in wait:
                if last[s] < v:
                    e.wait_ge(sems[s], v)
                    last[s] = v
            ins = fn(e) if fn is not None else None
            if inc is not None and ins is not None:
                ins.then_inc(sems[inc], n)


def emax(*evs):
    best = {}
    for ev in evs:
        if not ev:
            continue
        if isinstance(ev, list):
            for s, v in ev:
                if best.get(s, 0) < v:
                    best[s] = v
        else:
            s, v = ev
            if best.get(s, 0) < v:
                best[s] = v
    return list(best.items())


def build_program(dbg=()):
    nc = bass.Bass("TRN2", target_bir_lowering=False)

    xkv = nc.dram_tensor("xkv", [KVROWS, HID], F32, kind="ExternalInput").ap()
    wkv_h = nc.dram_tensor("wkv_h", [128, 8192], F32, kind="ExternalInput").ap()
    wq_h = nc.dram_tensor("wq_h", [8, 128, 4096], F32, kind="ExternalInput").ap()
    wo_h = nc.dram_tensor("wo_h", [16, 128, 2048], F32, kind="ExternalInput").ap()
    gw_h = nc.dram_tensor("gw_h", [22, 128, 4096], F32, kind="ExternalInput").ap()
    uw_h = nc.dram_tensor("uw_h", [22, 128, 4096], F32, kind="ExternalInput").ap()
    dw_h = nc.dram_tensor("dw_h", [16, 128, 5632], F32, kind="ExternalInput").ap()
    gs_h = nc.dram_tensor("gs_h", [128, 44], F32, kind="ExternalInput").ap()
    us_h = nc.dram_tensor("us_h", [128, 44], F32, kind="ExternalInput").ap()
    ds_h = nc.dram_tensor("ds_h", [128, 16], F32, kind="ExternalInput").ap()
    ctok_h = nc.dram_tensor("ctok_h", [128, NTILE * 64], F32, kind="ExternalInput").ap()
    mask_h = nc.dram_tensor("mask_h", [128, 2048], F32, kind="ExternalInput").ap()
    sel_h = nc.dram_tensor("sel_h", [1, 8], I32, kind="ExternalInput").ap()
    idb_h = nc.dram_tensor("idb_h", [128, 128], F32, kind="ExternalInput").ap()
    idf_h = nc.dram_tensor("idf_h", [128, 128], F32, kind="ExternalInput").ap()
    pones_h = nc.dram_tensor("pones_h", [1, 128], F32, kind="ExternalInput").ap()
    mones_h = nc.dram_tensor("mones_h", [1, 128], F32, kind="ExternalInput").ap()
    out_t = nc.dram_tensor("out_t", [HID, TOWN], F32, kind="ExternalOutput").ap()

    dbg_t = {}
    dbg_specs = {
        "xnT_own": ([128, 16384], BF16), "ktA": ([128, 4608], BF16),
        "ktB": ([128, 4608], BF16), "vS": ([128, 36 * 260], BF16),
        "mk2row": ([1, 4], F32), "qT": ([128, 16384], BF16),
        "bias_all": ([128, 32], F32), "attn_oT": ([128, 16384], BF16),
        "hdbg": ([128, ITL * 1024], BF16), "qtok": ([128, 16384], BF16),
    }
    for name in dbg:
        shp, dt = dbg_specs[name]
        dbg_t[name] = nc.dram_tensor("dbg_" + name, shp, dt, kind="ExternalOutput").ap()

    P = Prog()
    semh = {}
    ctx = contextlib.ExitStack()
    for s in ("dx", "dw", "dout", "pe", "ve", "se"):
        P.sem(s)
        semh[s] = ctx.enter_context(nc.semaphore(s))

    def sb(name, shape, dt):
        return ctx.enter_context(nc.sbuf_tensor(name, shape, dt))

    arena = sb("arena", [128, ACOLS], BF16)
    bn_sb = sb("bn_sb", [128, 96], F32)
    agg_sb = sb("agg_sb", [128, 8], F32)
    varp_sb = sb("varp_sb", [128, 4], F32)
    negmr_sb = sb("negmr_sb", [128, 4], F32)
    rstd_sb = sb("rstd_sb", [128, 4], F32)
    ksqt_sb = sb("ksqt_sb", [128, 256], BF16)
    kn4_sb = sb("kn4_sb", [128, 4], F32)
    knmax_sb = sb("knmax_sb", [128, 4], F32)
    knm4_sb = sb("knm4_sb", [4, 1], F32)
    mk2_sb = sb("mk2_sb", [1, 4], F32)
    qn_sb = sb("qn_sb", [128, 32], F32)
    qnmax_sb = sb("qnmax_sb", [128, 32], F32)
    qmx32_sb = sb("qmx32_sb", [32, 1], F32)
    qmaxr_sb = sb("qmaxr_sb", [1, 32], F32)
    prodr_sb = sb("prodr_sb", [1, 32], F32)
    sqrow_sb = sb("sqrow_sb", [1, 32], BF16)
    biasa_sb = sb("biasa_sb", [128, 32], F32)
    gs_sb = sb("gs_sb", [128, 44], F32)
    us_sb = sb("us_sb", [128, 44], F32)
    ds_sb = sb("ds_sb", [128, 16], F32)
    sel_sb = sb("sel_sb", [1, 8], I32)
    idb_sb = sb("idb_sb", [128, 128], BF16)
    idf_sb = sb("idf_sb", [128, 128], F32)
    pones_sb = sb("pones_sb", [1, 128], BF16)
    mones_sb = sb("mones_sb", [1, 128], BF16)
    tmp4_sb = sb("tmp4_sb", [128, 1024], F32)

    psT0 = ctx.enter_context(nc.psum_tensor("psT0", [128, 1024], BF16))
    psT1 = ctx.enter_context(nc.psum_tensor("psT1", [128, 1024], BF16))
    psF = [ctx.enter_context(nc.psum_tensor(f"psF{i}", [128, 512], F32))
           for i in range(4)]
    psAV = ctx.enter_context(nc.psum_tensor("psAV", [128, 1024], F32))
    psAVb = psAV[:].bitcast(BF16)

    def av_(c0, n, r0=0, r1=128):
        return arena[r0:r1, c0:c0 + n]

    def avf(c0, nf32, r0=0, r1=128):
        return arena[r0:r1, c0:c0 + 2 * nf32].bitcast(F32)

    probe_ap = arena[64:128, 5:517]
    assert probe_ap.offset == 64 * ACOLS + 5, probe_ap.offset

    xnt_own = av_(XNT_OWN, 16384)
    attn_ot = av_(ATTN_OT, 16384)
    h_sb = av_(H_OFF, ITL * 1024)
    ktA = av_(KTA_O, 4608)
    ktB = av_(KTB_O, 4608)
    vS = av_(VS_O, 36 * 260)
    ctok = av_(CTOK_O, NTILE * 64)
    masks = av_(MASK_O, 2048)
    wkv_sb = av_(WKV_O, 8192)
    xn_v = [av_(XN_O + i * 2048, 2048) for i in range(4)]
    ktok_v = [av_(KTOK_O + i * 256, 256) for i in range(2)]
    xbuf_v = [avf(XBUF_O + i * 4096, 2048) for i in range(4)]

    ev_wkv = P.emit("gpsimd", lambda e: e.dma_start(out=wkv_sb, in_=wkv_h),
                    inc="dw", n=16)
    evc = None
    for dst, srca in ((ctok, ctok_h), (masks, mask_h), (idb_sb[:], idb_h),
                      (pones_sb[:], pones_h), (mones_sb[:], mones_h)):
        evc = P.emit("gpsimd", lambda e, d=dst, s=srca: e.dma_start(out=d, in_=s),
                     inc="dw", n=16)
    for dst, srca in ((sel_sb[:], sel_h), (idf_sb[:], idf_h),
                      (gs_sb[:], gs_h), (us_sb[:], us_h), (ds_sb[:], ds_h)):
        evc2 = P.emit("sync", lambda e, d=dst, s=srca: e.dma_start(out=d, in_=s),
                      inc="dx", n=16)
    ev_consts = emax(evc, evc2)

    ev_knz = P.emit("vector", lambda e: e.memset(knmax_sb[:], 0.0), inc="ve")
    ev_qnz = P.emit("vector", lambda e: e.memset(qnmax_sb[:], 0.0), inc="ve")
    ones_ap = bass.AP(arena, VS_O + 64, [[ACOLS, 128], [260, 36], [65, 4], [1, 1]])
    ev_vones = P.emit("vector", lambda e, a=ones_ap: e.memset(a, 1.0), inc="ve")

    evXL = [None] * NTILE
    evXN = [None] * NTILE
    evS3 = [None] * NTILE
    evS4 = [None] * NTILE
    evS5 = [None] * NTILE
    evKR = [None] * NTILE
    evKN = [None] * NTILE
    evS7 = [None] * NTILE
    evS8 = [None] * NTILE
    evVE = [None] * NTILE

    def xnt_src_slice(t, j):
        s = t // 4
        if s >= 7:
            return av_(XNT_OWN + j * 1024 + (s - 7) * 512 + (t % 4) * 128, 128)
        return av_(XNTT_O + (t % 2) * 2048 + j * 128, 128)

    lastT_user = [-1]
    lastA_user = [-1]
    for step in range(NTILE + 2):
        t = step
        if t < NTILE:
            b = t % 4
            b2p = t % 2
            src_x = xkv[t * 128:(t + 1) * 128, :]
            evXL[t] = P.emit(
                "sync", lambda e, s=src_x, d=xbuf_v[b]: e.dma_start(out=d, in_=s),
                wait=emax(evXN[t - 4] if t >= 4 else None), inc="dx", n=16)
            e1 = None
            for i in range(4):
                e1 = P.emit("vector", lambda e, i=i, b=b: e.bn_stats(
                    bn_sb[:, b * 24 + i * 6:b * 24 + (i + 1) * 6],
                    xbuf_v[b][:, i * 512:(i + 1) * 512]),
                    wait=emax(evXL[t]), inc="ve")
            e2 = P.emit("vector", lambda e, b=b: e.bn_aggr(
                agg_sb[:, 2 * b:2 * b + 2], bn_sb[:, b * 24:b * 24 + 24]),
                wait=emax(e1), inc="ve")
            e3 = P.emit("vector", lambda e, b=b: e.tensor_scalar_add(
                varp_sb[:, b:b + 1], agg_sb[:, 2 * b + 1:2 * b + 2], EPS),
                wait=emax(e2), inc="ve")
            e4 = P.emit("vector", lambda e, b=b: e.reciprocal(
                varp_sb[:, b:b + 1], varp_sb[:, b:b + 1]), wait=emax(e3), inc="ve")
            e5 = P.emit("scalar", lambda e, b=b: e.sqrt(
                rstd_sb[:, b:b + 1], varp_sb[:, b:b + 1]), wait=emax(e4), inc="se")
            e6 = P.emit("vector", lambda e, b=b: e.tensor_scalar(
                out=negmr_sb[:, b:b + 1], in0=agg_sb[:, 2 * b:2 * b + 1],
                scalar1=rstd_sb[:, b:b + 1], scalar2=-1.0,
                op0=ALU.mult, op1=ALU.mult),
                wait=emax(e5), inc="ve")
            evXN[t] = P.emit("scalar", lambda e, b=b: e.activation(
                out=xn_v[b], in_=xbuf_v[b], func=AF.Identity,
                bias=negmr_sb[:, b:b + 1], scale=rstd_sb[:, b:b + 1]),
                wait=emax(e6, evS3[t - 4] if t >= 4 else None), inc="se")
            s = t // 4
            own = s >= 7
            use_psT = own or (t % 2 == 0)
            tp_wait = emax(evXN[t],
                           evS4[lastT_user[0]] if (use_psT and lastT_user[0] >= 0)
                           else None,
                           evS4[lastA_user[0]] if ((not use_psT) and
                                                   lastA_user[0] >= 0) else None,
                           ev_consts if t == 0 else None)
            last = None
            for j in range(16):
                if use_psT:
                    ps = psT0[:] if j < 8 else psT1[:]
                    jj = j % 8
                else:
                    ps = psAVb
                    jj = j
                last = P.emit("tensor", lambda e, b=b, j=j, ps=ps, jj=jj: e.transpose(
                    ps[:, jj * 128:(jj + 1) * 128],
                    xn_v[b][:, j * 128:(j + 1) * 128], idb_sb[:]),
                    wait=tp_wait if j == 0 else (),
                    inc="pe" if j == 15 else None)
            evS3[t] = last
            if use_psT:
                lastT_user[0] = t
            else:
                lastA_user[0] = t
            if own:
                off = XNT_OWN + (s - 7) * 512 + (t % 4) * 128
                d0 = bass.AP(arena, off, [[ACOLS, 128], [1024, 8], [1, 128]])
                d1 = bass.AP(arena, off + 8 * 1024, [[ACOLS, 128], [1024, 8], [1, 128]])
                s0 = bass.AP(psT0, 0, [[1024, 128], [128, 8], [1, 128]])
                s1 = bass.AP(psT1, 0, [[1024, 128], [128, 8], [1, 128]])
            elif use_psT:
                d0 = av_(XNTT_O + b2p * 2048, 1024)
                d1 = av_(XNTT_O + b2p * 2048 + 1024, 1024)
                s0, s1 = psT0[:], psT1[:]
            else:
                d0 = av_(XNTT_O + b2p * 2048, 1024)
                d1 = av_(XNTT_O + b2p * 2048 + 1024, 1024)
                s0, s1 = psAVb[:, 0:1024], psAVb[:, 1024:2048]
            w4 = emax(evS3[t], evS5[t - 2] if (t >= 2 and (t - 2) // 4 < 7) else None)
            P.emit("scalar", lambda e, d=d0, s_=s0: e.copy(d, s_), wait=w4)
            evS4[t] = P.emit("scalar", lambda e, d=d1, s_=s1: e.copy(d, s_),
                             inc="se")
        if step >= 1 and (step - 1) < NTILE:
            t1 = step - 1
            b1 = t1 % 2
            last = None
            for j in range(HT):
                lhsT = xnt_src_slice(t1, j)
                last = P.emit("tensor", lambda e, l=lhsT, j=j, b1=b1: e.matmul(
                    psF[b1][:], l, wkv_sb[:, j * 512:(j + 1) * 512],
                    start=(j == 0), stop=(j == HT - 1)),
                    wait=emax(evS4[t1], ev_wkv,
                              evKR[t1 - 2] if t1 >= 2 else None,
                              evVE[t1 - 2] if t1 >= 2 else None),
                    inc="pe" if j == HT - 1 else None)
            evS5[t1] = last
            vdst = bass.AP(arena, VS_O + t1 * 260, [[ACOLS, 128], [65, 4], [1, 64]])
            vsrc = bass.AP(psF[b1], 256, [[512, 128], [64, 4], [1, 64]])
            evVE[t1] = P.emit("scalar", lambda e, d=vdst, s=vsrc: e.copy(d, s),
                              wait=emax(evS5[t1], ev_vones), inc="se")
            x1 = bass.AP(psF[b1], 0, [[512, 128], [64, 4], [1, 32]])
            x2 = bass.AP(psF[b1], 32, [[512, 128], [64, 4], [1, 32]])
            cosb = bass.AP(arena, CTOK_O + t1 * 64, [[ACOLS, 128], [0, 4], [1, 32]])
            sinb = bass.AP(arena, CTOK_O + t1 * 64 + 32,
                           [[ACOLS, 128], [0, 4], [1, 32]])
            k1 = bass.AP(arena, KTOK_O + b1 * 256, [[ACOLS, 128], [64, 4], [1, 32]])
            k2 = bass.AP(arena, KTOK_O + b1 * 256 + 32,
                         [[ACOLS, 128], [64, 4], [1, 32]])
            rA = bass.AP(arena, RTMPA_O, [[ACOLS, 128], [32, 4], [1, 32]])
            rB = bass.AP(arena, RTMPB_O, [[ACOLS, 128], [32, 4], [1, 32]])
            wr = emax(evS5[t1], evVE[t1], ev_consts,
                      evS7[t1 - 2] if t1 >= 2 else None)
            P.emit("vector", lambda e, o=rA, a=x2, c=sinb: e.tensor_mul(o, a, c),
                   wait=wr)
            P.emit("vector", lambda e, o=rB, a=x1, c=sinb: e.tensor_mul(o, a, c))
            P.emit("vector", lambda e, o=k1, a=x1, c=cosb: e.tensor_mul(o, a, c))
            P.emit("vector", lambda e, o=k1, a=rA: e.tensor_sub(o, o, a))
            P.emit("vector", lambda e, o=k2, a=x2, c=cosb: e.tensor_mul(o, a, c))
            evKR[t1] = P.emit("vector", lambda e, o=k2, a=rB: e.tensor_add(o, o, a),
                              inc="ve")
            eq = P.emit("scalar", lambda e, b1=b1: e.square(ksqt_sb[:], ktok_v[b1]),
                        wait=emax(evKR[t1], evKN[t1 - 1] if t1 >= 1 else None),
                        inc="se")
            er = None
            for r in range(4):
                er = P.emit("vector", lambda e, r=r: e.reduce_sum(
                    kn4_sb[:, r:r + 1], ksqt_sb[:, r * 64:(r + 1) * 64], axis=AX.X),
                    wait=emax(eq), inc="ve")
            evKN[t1] = P.emit("vector", lambda e: e.tensor_max(
                knmax_sb[:], knmax_sb[:], kn4_sb[:]), wait=emax(er, ev_knz), inc="ve")
        if step >= 2 and (step - 2) < NTILE:
            t2 = step - 2
            b2 = t2 % 2
            psF2b = psF[2][:].bitcast(BF16)
            last = None
            for blk in range(2):
                last = P.emit("tensor", lambda e, b2=b2, blk=blk, p=psF2b: e.transpose(
                    p[:, blk * 128:(blk + 1) * 128],
                    ktok_v[b2][:, blk * 128:(blk + 1) * 128], idb_sb[:]),
                    wait=emax(evKR[t2], evS8[t2 - 1] if t2 >= 1 else None),
                    inc="pe" if blk == 1 else None)
            evS7[t2] = last
            P.emit("scalar", lambda e, t2=t2, p=psF2b: e.copy(
                av_(KTA_O + t2 * 128, 128), p[:, 0:128]), wait=emax(evS7[t2]))
            evS8[t2] = P.emit("scalar", lambda e, t2=t2, p=psF2b: e.copy(
                av_(KTB_O + t2 * 128, 128), p[:, 128:256]), inc="se")
    eT1 = P.emit("tensor", lambda e: e.transpose(
        psF[0][0:4, 0:128], knmax_sb[:], idf_sb[:]),
        wait=emax(evKN[NTILE - 1], evVE[NTILE - 1], evKR[NTILE - 1]), inc="pe")
    eT2 = P.emit("vector", lambda e: e.reduce_max(
        knm4_sb[:], psF[0][0:4, 0:128], axis=AX.X), wait=emax(eT1), inc="ve")
    eT3 = P.emit("tensor", lambda e: e.transpose(
        psF[0][0:1, 0:4], knm4_sb[:], idf_sb[0:4, 0:4]), wait=emax(eT2), inc="pe")
    evMK2 = P.emit("vector", lambda e: e.tensor_copy(mk2_sb[:], psF[0][0:1, 0:4]),
                   wait=emax(eT3), inc="ve")

    dbg_gate = [None]

    def dump_now(name, w):
        if name not in dbg:
            return
        ev = P.emit("sync", lambda e, d=dbg_t[name], s=dmap0[name]: e.dma_start(
            out=d, in_=s), wait=w, inc="dout", n=16)
        dbg_gate[0] = emax(dbg_gate[0], ev)

    dmap0 = {
        "xnT_own": av_(XNT_OWN, 16384), "ktA": ktA, "ktB": ktB, "vS": vS,
        "mk2row": mk2_sb[:], "qT": av_(QT_O, 16384),
        "bias_all": biasa_sb[:], "attn_oT": av_(ATTN_OT, 16384),
        "hdbg": h_sb, "qtok": av_(QTOK_O, 16384),
    }
    for nm in ("xnT_own", "ktA", "ktB", "vS", "mk2row"):
        dump_now(nm, emax(evMK2, evS8[NTILE - 1], evVE[NTILE - 1], evS4[NTILE - 1],
                          evKR[NTILE - 1]))

    qtok_v = [av_(QTOK_O + tt * 2048, 2048) for tt in range(8)]
    wqc_v = [av_(WQC_O + i * 4096, 4096) for i in range(2)]
    attA_v = [av_(ATTA_O + i * 512, 512) for i in range(3)]
    attB_v = [av_(ATTB_O + i * 512, 512) for i in range(3)]
    psT0f = psT0[:].bitcast(F32)
    psT1f = psT1[:].bitcast(F32)
    zero_v = av_(ZERO_O, 512)
    rbff_row = avf(RBFF_O, 1024, 0, 1)
    rbf2_row = arena[0:1, RBF2_O:RBF2_O + 1024]
    rbsb = avf(RBSB_O, 1024, 0, 64)
    atmp = av_(ATMP_O, 1024, 0, 64)

    ev_zero = P.emit("vector", lambda e: e.memset(zero_v, 0.0), inc="ve")

    REGS = {}

    def reg_setup(e):
        st = contextlib.ExitStack()
        REGS["stack"] = st
        REGS["cid"] = st.enter_context(e.register("cid"))
        e.reg_load(REGS["cid"], sel_sb[0:1, 7:8])
        return None

    P.emit("tensor", reg_setup, wait=emax(ev_consts))

    evQMM = {}
    evQE = {}
    for oc in range(8):
        evWQ = P.emit("gpsimd", lambda e, oc=oc: e.dma_start(
            out=wqc_v[oc % 2], in_=wq_h[oc]),
            wait=emax(evQMM.get((oc - 2, 7)),
                      emax(evKR[NTILE - 1], evVE[NTILE - 1])
                      if oc < 2 else None),
            inc="dw", n=16)
        for tt in range(8):
            last = None
            for j in range(HT):
                lhsT = av_(XNT_OWN + j * 1024 + tt * 128, 128)
                last = P.emit("tensor", lambda e, l=lhsT, j=j, tt=tt, oc=oc: e.matmul(
                    psF[2 + tt % 2][:, 0:256], l,
                    wqc_v[oc % 2][:, j * 256:(j + 1) * 256],
                    start=(j == 0), stop=(j == HT - 1)),
                    wait=emax(evWQ, evS4[NTILE - 1], evS8[NTILE - 1],
                              evQE.get((oc, tt - 2)), evQE.get((oc - 1, tt + 6))),
                    inc="pe" if j == HT - 1 else None)
            evQMM[(oc, tt)] = last
            evQE[(oc, tt)] = P.emit("vector", lambda e, oc=oc, tt=tt: e.tensor_copy(
                av_(QTOK_O + tt * 2048 + oc * 256, 256), psF[2 + tt % 2][:, 0:256]),
                wait=emax(evQMM[(oc, tt)]), inc="ve")

    evQR = [None] * 8
    evQN = [None] * 8
    evQT = [None] * 8
    evQTE = [None] * 8
    for tt in range(8):
        base = QTOK_O + tt * 2048
        x1 = bass.AP(arena, base, [[ACOLS, 128], [64, 32], [1, 32]])
        x2 = bass.AP(arena, base + 32, [[ACOLS, 128], [64, 32], [1, 32]])
        cosb = bass.AP(arena, CTOK_O + (28 + tt) * 64, [[ACOLS, 128], [0, 32], [1, 32]])
        sinb = bass.AP(arena, CTOK_O + (28 + tt) * 64 + 32,
                       [[ACOLS, 128], [0, 32], [1, 32]])
        rA = bass.AP(arena, RTMPA_O, [[ACOLS, 128], [32, 32], [1, 32]])
        rB = bass.AP(arena, RTMPB_O, [[ACOLS, 128], [32, 32], [1, 32]])
        w0 = emax(evQE[(7, tt)], ev_consts, evQMM[(7, 7)])
        P.emit("vector", lambda e, o=rA, a=x2, c=sinb: e.tensor_mul(o, a, c), wait=w0)
        P.emit("vector", lambda e, o=rB, a=x1, c=sinb: e.tensor_mul(o, a, c))
        P.emit("vector", lambda e, o=x1, a=x1, c=cosb: e.tensor_mul(o, a, c))
        P.emit("vector", lambda e, o=x1, a=rA: e.tensor_sub(o, o, a))
        P.emit("vector", lambda e, o=x2, a=x2, c=cosb: e.tensor_mul(o, a, c))
        evQR[tt] = P.emit("vector", lambda e, o=x2, a=rB: e.tensor_add(o, o, a),
                          inc="ve")
        eqs = P.emit("scalar", lambda e, tt=tt: e.square(av_(QSQT_O, 2048), qtok_v[tt]),
                     wait=emax(evQR[tt], evQN[tt - 1] if tt >= 1 else None), inc="se")
        er = None
        for hh in range(HQ):
            er = P.emit("vector", lambda e, hh=hh: e.reduce_sum(
                qn_sb[:, hh:hh + 1], av_(QSQT_O + hh * 64, 64), axis=AX.X),
                wait=emax(eqs), inc="ve")
        evQN[tt] = P.emit("vector", lambda e: e.tensor_max(
            qnmax_sb[:], qnmax_sb[:], qn_sb[:]), wait=emax(er, ev_qnz), inc="ve")
        last = None
        for j in range(16):
            ps = psT0 if j < 8 else psT1
            jj = j % 8
            last = P.emit("tensor", lambda e, tt=tt, j=j, ps=ps, jj=jj: e.transpose(
                ps[:, jj * 128:(jj + 1) * 128],
                qtok_v[tt][:, j * 128:(j + 1) * 128], idb_sb[:]),
                wait=emax(evQR[tt], evQTE[tt - 1] if tt >= 1 else None,
                          evS4[NTILE - 1]),
                inc="pe" if j == 15 else None)
        evQT[tt] = last
        d0 = bass.AP(arena, QT_O + tt * 128, [[ACOLS, 128], [1024, 8], [1, 128]])
        d1 = bass.AP(arena, QT_O + 8 * 1024 + tt * 128,
                     [[ACOLS, 128], [1024, 8], [1, 128]])
        s0 = bass.AP(psT0, 0, [[1024, 128], [128, 8], [1, 128]])
        s1 = bass.AP(psT1, 0, [[1024, 128], [128, 8], [1, 128]])
        P.emit("scalar", lambda e, d=d0, s_=s0: e.copy(d, s_),
               wait=emax(evQT[tt]))
        evQTE[tt] = P.emit("scalar", lambda e, d=d1, s_=s1: e.copy(d, s_),
                           inc="se")

    eU1 = P.emit("tensor", lambda e: e.transpose(
        psF[0][0:32, 0:128], qnmax_sb[:], idf_sb[:]),
        wait=emax(evQN[7], evQMM[(7, 7)], evMK2), inc="pe")
    eU2 = P.emit("vector", lambda e: e.reduce_max(
        qmx32_sb[:], psF[0][0:32, 0:128], axis=AX.X), wait=emax(eU1), inc="ve")
    eU3 = P.emit("tensor", lambda e: e.transpose(
        psF[0][0:1, 0:32], qmx32_sb[:], idf_sb[0:32, 0:32]), wait=emax(eU2), inc="pe")
    eU4 = P.emit("vector", lambda e: e.tensor_copy(qmaxr_sb[:], psF[0][0:1, 0:32]),
                 wait=emax(eU3), inc="ve")
    mk2bc = bass.AP(mk2_sb, 0, [[4, 1], [1, 4], [0, 8]])
    pr3 = bass.AP(prodr_sb, 0, [[32, 1], [8, 4], [1, 8]])
    qm3 = bass.AP(qmaxr_sb, 0, [[32, 1], [8, 4], [1, 8]])
    eU5 = P.emit("vector", lambda e, o=pr3, a=qm3, m=mk2bc: e.tensor_mul(o, a, m),
                 wait=emax(eU4, evMK2), inc="ve")
    eU6 = P.emit("vector", lambda e: e.tensor_scalar_mul(
        prodr_sb[:], prodr_sb[:], 1.0 / 64.0), wait=emax(eU5), inc="ve")
    eU7 = P.emit("scalar", lambda e: e.sqrt(sqrow_sb[:], prodr_sb[:]),
                 wait=emax(eU6), inc="se")
    dump_now("qtok", emax(evQR[7], evQN[7], evQTE[7]))
    wk2 = emax(evQTE[7], evQN[7], evS8[NTILE - 1], dbg_gate[0])
    P.emit("scalar", lambda e: e.copy(arena[0:64, KT2A_O:KT2A_O + 4608],
                                      arena[64:128, KTA_O:KTA_O + 4608]), wait=wk2)
    P.emit("scalar", lambda e: e.copy(arena[64:128, KT2A_O:KT2A_O + 4608],
                                      arena[0:64, KTA_O:KTA_O + 4608]))
    P.emit("scalar", lambda e: e.copy(arena[0:64, KT2B_O:KT2B_O + 4608],
                                      arena[64:128, KTB_O:KTB_O + 4608]))
    ev_kt2 = P.emit("scalar", lambda e: e.copy(arena[64:128, KT2B_O:KT2B_O + 4608],
                                               arena[0:64, KTB_O:KTB_O + 4608]),
                    inc="se")

    eB1 = P.emit("tensor", lambda e: e.matmul(
        psF[2][:, 0:32], mones_sb[:], sqrow_sb[:], start=True, stop=True),
        wait=emax(eU7, evQE[(7, 7)]), inc="pe")
    evBIAS = P.emit("vector", lambda e: e.tensor_copy(biasa_sb[:], psF[2][:, 0:32]),
                    wait=emax(eB1), inc="ve")

    dump_now("qT", emax(evQTE[7]))
    dump_now("bias_all", emax(evBIAS))

    blocks = [(7, m) for m in range(4)] + [(8, m) for m in range(4)] + \
             [(k, m) for k in range(7) for m in range(4)]
    NB = len(blocks)
    pairs = []
    for gkt, alo, bhi in ((KTA_O, 0, 9), (KT2A_O, 8, 1),
                          (KTB_O, 16, 25), (KT2B_O, 24, 17)):
        for i in range(4):
            pairs.append((gkt, alo + 2 * i, bhi + 2 * i))

    pe_ops = [[] for _ in range(8)]

    def pe_emit(fns, wait=()):
        P.cnt["pe"] += 1
        val = P.cnt["pe"]
        w = tuple((s, v) for (s, v) in emax(*wait) if v > 0) if wait else ()
        if not isinstance(fns, list):
            fns = [fns] * 8
        for ci in range(8):
            pe_ops[ci].append((w, fns[ci]))
        return ("pe", val)

    prev_sc_free = [emax(eU2, eU4, evBIAS, eT2, evMK2)] * 4
    prev_av_free = emax(evS8[NTILE - 1], evMK2, evQTE[7], evS4[NTILE - 2], evS4[NTILE - 1])
    prev_rb_free = emax(evBIAS)
    prev_avlast = None
    prev_atmp = None
    ev_attn_done_v = None
    ev_attn_done_s = None

    for pi, (kt_base, hA, hB) in enumerate(pairs):
        qcbA = QT_O + (hA // 2) * 1024
        qcbB = QT_O + (hB // 2) * 1024 + 64 * ACOLS
        kvA = hA // 8
        kvB = hB // 8

        evSCA = [None] * NB
        evSCB = [None] * NB
        evEXA = [None] * NB
        evEXB = [None] * NB
        evMKA = [None] * NB
        evMKB = [None] * NB
        evAVA = [None] * NB
        evAVB = [None] * NB

        def sc_fns(qcb, rowbase, slot, kb, psdst):
            lhs = arena[rowbase:rowbase + 64,
                        kt_base + kb * 128:kt_base + (kb + 1) * 128]
            if slot >= 7:
                r = bass.AP(arena, qcb + (512 if slot == 8 else 0),
                            [[ACOLS, 64], [1, 512]])
                return (lambda e, l=lhs, r=r, o=psdst:
                        e.matmul(o, l, r, start=True, stop=True))
            fns = []
            for ci in range(8):
                sel = 0 if slot < ci else 512
                r = bass.AP(arena, qcb + sel, [[ACOLS, 64], [1, 512]])
                fns.append(lambda e, l=lhs, r=r, o=psdst:
                           e.matmul(o, l, r, start=True, stop=True))
            return fns

        def emit_avs(bi):
            slot, m = blocks[bi]
            kb = slot * 4 + m
            st = (slot >= 7 and m == 0)
            vpA = vS[:, kb * 260 + kvA * 65:kb * 260 + (kvA + 1) * 65]
            vpB = vS[:, kb * 260 + kvB * 65:kb * 260 + (kvB + 1) * 65]
            aA = attA_v[bi % 3]
            aB = attB_v[bi % 3]
            if slot >= 7:
                oA = psAV[0:65, 0:512] if slot == 7 else psAV[0:65, 512:1024]
                fA = (lambda e, v=vpA, o=oA, a=aA, st=st:
                      e.matmul(o, v, a, start=st, stop=False))
                oB = psT0f[0:65, :] if slot == 7 else psT1f[0:65, :]
                fB = (lambda e, v=vpB, o=oB, a=aB, st=st:
                      e.matmul(o, v, a, start=st, stop=False))
            else:
                fA = []
                fB = []
                for ci in range(8):
                    sel = 0 if slot < ci else 512
                    oA = psAV[0:65, sel:sel + 512]
                    fA.append(lambda e, v=vpA, o=oA, a=aA:
                              e.matmul(o, v, a, start=False, stop=False))
                    oB = psT0f[0:65, :] if sel == 0 else psT1f[0:65, :]
                    fB.append(lambda e, v=vpB, o=oB, a=aB:
                              e.matmul(o, v, a, start=False, stop=False))
            evAVA[bi] = pe_emit(fA, wait=[evMKA[bi],
                                          prev_av_free if bi == 0 else None])
            evAVB[bi] = pe_emit(fB, wait=[evMKB[bi],
                                          prev_av_free if bi == 0 else None])

        for bi, (slot, m) in enumerate(blocks):
            kb = slot * 4 + m
            wA = [evEXA[bi - 2] if bi >= 2 else prev_sc_free[bi % 2],
                  emax(evQTE[7], evS8[NTILE - 1], ev_kt2)
                  if (pi == 0 and bi == 0) else None]
            evSCA[bi] = pe_emit(sc_fns(qcbA, 0, slot, kb, psF[bi % 2][:]), wait=wA)
            wB = [evEXB[bi - 2] if bi >= 2 else prev_sc_free[2 + bi % 2]]
            evSCB[bi] = pe_emit(sc_fns(qcbB, 64, slot, kb, psF[2 + bi % 2][:]),
                                wait=wB)
            aA = attA_v[bi % 3]
            aB = attB_v[bi % 3]
            evEXA[bi] = P.emit(
                "scalar", lambda e, o=aA, i=psF[bi % 2], hh=hA: e.activation(
                    out=o, in_=i[:], func=AF.Exp,
                    bias=biasa_sb[:, hh:hh + 1], scale=0.125),
                wait=emax(evSCA[bi], evBIAS,
                          evAVA[bi - 3] if bi >= 3 else prev_avlast),
                inc="se")
            evEXB[bi] = P.emit(
                "scalar", lambda e, o=aB, i=psF[2 + bi % 2], hh=hB: e.activation(
                    out=o, in_=i[:], func=AF.Exp,
                    bias=biasa_sb[:, hh:hh + 1], scale=0.125),
                wait=emax(evSCB[bi],
                          evAVB[bi - 3] if bi >= 3 else prev_avlast),
                inc="se")
            if slot >= 7:
                evMKA[bi] = P.emit(
                    "vector", lambda e, o=aA, m=m: e.tensor_mul(
                        o, o, masks[:, m * 512:(m + 1) * 512]),
                    wait=emax(evEXA[bi]), inc="ve")
                evMKB[bi] = P.emit(
                    "vector", lambda e, o=aB, m=m: e.tensor_mul(
                        o, o, masks[:, m * 512:(m + 1) * 512]),
                    wait=emax(evEXB[bi]), inc="ve")
            else:
                evMKA[bi] = evEXA[bi]
                evMKB[bi] = evEXB[bi]
            if bi >= 1:
                emit_avs(bi - 1)
        emit_avs(NB - 1)

        eZ1 = pe_emit(lambda e: e.matmul(
            psAV[0:65, 0:512], vS[:, 0:65], zero_v, start=False, stop=True),
            wait=[ev_zero, evAVA[NB - 1], evAVB[NB - 1]])
        eZ2 = pe_emit(lambda e: e.matmul(
            psAV[0:65, 512:1024], vS[:, 0:65], zero_v, start=False, stop=True))
        eZ3 = pe_emit(lambda e: e.matmul(
            psT0f[0:65, :], vS[:, 0:65], zero_v, start=False, stop=True))
        eZ4 = pe_emit(lambda e: e.matmul(
            psT1f[0:65, :], vS[:, 0:65], zero_v, start=False, stop=True))

        out_evs = []
        for role, (hh, sums_ops, avsrc) in enumerate((
                (hA, [lambda e: e.copy(rbff_row, psAV[64:65, 0:1024])],
                 psAV[0:64, 0:1024]),
                (hB, [lambda e: e.copy(rbff_row[:, 0:512], psT0f[64:65, :]),
                      lambda e: e.copy(rbff_row[:, 512:1024], psT1f[64:65, :])],
                 None))):
            wz = emax(eZ4, out_evs[-1] if out_evs else None)
            eR0 = None
            for so in sums_ops:
                eR0 = P.emit("scalar", so, wait=wz, inc="se")
            eR1 = P.emit("vector", lambda e: e.reciprocal(rbff_row, rbff_row),
                         wait=emax(eR0), inc="ve")
            eR1b = P.emit("vector", lambda e: e.tensor_copy(rbf2_row, rbff_row),
                          wait=emax(eR1), inc="ve")
            eR2 = pe_emit(lambda e: e.matmul(
                psF[0][0:64, :], pones_sb[0:1, 0:64],
                arena[0:1, RBF2_O:RBF2_O + 512], start=True, stop=True),
                wait=[eR1b, prev_rb_free,
                      emax(evEXA[NB - 2], evEXA[NB - 1]) if role == 0 else None])
            eR3 = pe_emit(lambda e: e.matmul(
                psF[1][0:64, :], pones_sb[0:1, 0:64],
                arena[0:1, RBF2_O + 512:RBF2_O + 1024], start=True, stop=True))
            P.emit("vector", lambda e: e.tensor_copy(rbsb[:, 0:512], psF[0][0:64, :]),
                   wait=emax(eR2, eR3))
            eR4 = P.emit("vector", lambda e: e.tensor_copy(
                rbsb[:, 512:1024], psF[1][0:64, :]), inc="ve")
            prev_rb_free = emax(eR4)
            col = ATTN_OT + (hh // 2) * 1024
            if role == 0:
                def mul_fn(e, d=None):
                    return e.tensor_mul(d, psAV[0:64, 0:1024], rbsb[:])
                srcap = psAV[0:64, 0:1024]
            else:
                srcap = None
            if hh % 2 == 0:
                dst = arena[0:64, col:col + 1024]
                if role == 0:
                    evO = P.emit("vector", lambda e, d=dst: e.tensor_mul(
                        d, psAV[0:64, 0:1024], rbsb[:]), wait=emax(eR4), inc="ve")
                else:
                    evO = P.emit("vector", lambda e, d=dst: e.tensor_mul(
                        d[:, 0:512], psT0f[0:64, :], rbsb[:, 0:512]),
                        wait=emax(eR4), inc="ve")
                    evO = P.emit("vector", lambda e, d=dst: e.tensor_mul(
                        d[:, 512:1024], psT1f[0:64, :], rbsb[:, 512:1024]),
                        inc="ve")
                out_evs.append(evO)
                ev_attn_done_v = evO
            else:
                if role == 0:
                    evO = P.emit("vector", lambda e: e.tensor_mul(
                        atmp, psAV[0:64, 0:1024], rbsb[:]),
                        wait=emax(eR4, prev_atmp), inc="ve")
                else:
                    evO = P.emit("vector", lambda e: e.tensor_mul(
                        atmp[:, 0:512], psT0f[0:64, :], rbsb[:, 0:512]),
                        wait=emax(eR4, prev_atmp), inc="ve")
                    evO = P.emit("vector", lambda e: e.tensor_mul(
                        atmp[:, 512:1024], psT1f[0:64, :], rbsb[:, 512:1024]),
                        inc="ve")
                dst = arena[64:128, col:col + 1024]
                evO2 = P.emit("scalar", lambda e, d=dst: e.copy(d, atmp),
                              wait=emax(evO), inc="se")
                prev_atmp = evO2
                out_evs.append(emax(evO, evO2))
                ev_attn_done_v = evO
                ev_attn_done_s = evO2
        prev_av_free = emax(*out_evs)
        prev_avlast = emax(evAVA[NB - 1], evAVB[NB - 1])
        prev_sc_free = [emax(eR4, out_evs[-1])] * 4

    def attn_pe_block(e):
        for ci in range(8):
            with e.If_eq(REGS["cid"], ci):
                last = {}
                for w, fn in pe_ops[ci]:
                    for s, v in w:
                        if last.get(s, 0) < v:
                            e.wait_ge(semh[s], v)
                            last[s] = v
                    fn(e).then_inc(semh["pe"], 1)
        return None
    P.emit("tensor", attn_pe_block)

    ev_attn_all = emax(ev_attn_done_v, ev_attn_done_s)

    gw_v = [av_(GW_O + i * 4096, 4096) for i in range(2)]
    uw_v = [av_(UW_O + i * 4096, 4096) for i in range(2)]
    gst = av_(GST_O, 1024)
    ust = av_(UST_O, 1024)

    dump_now("attn_oT", ev_attn_all)
    evGW = {}
    evUW = {}
    evGMM = {}
    evUMM = {}
    evHM = {}
    gfree = {0: None, 1: None}
    ufree = {0: None, 1: None}
    last_gmm = None
    last_umm = None
    last_hm = None
    for ic in range(22):
        evGW[ic] = P.emit("gpsimd", lambda e, ic=ic: e.dma_start(
            out=gw_v[ic % 2], in_=gw_h[ic]),
            wait=emax(evGMM.get(ic - 2), ev_attn_all if ic < 2 else None,
                      dbg_gate[0] if ic == 0 else None),
            inc="dw", n=16)
        evUW[ic] = P.emit("gpsimd", lambda e, ic=ic: e.dma_start(
            out=uw_v[ic % 2], in_=uw_h[ic]),
            wait=emax(evUMM.get(ic - 2)), inc="dw", n=16)
        for it2 in range(2):
            itile = 2 * ic + it2
            for tb in range(2):
                pi = (2 * it2 + tb) % 2
                pg = psF[pi]
                pu = psF[2 + pi]
                last = None
                for j in range(HT):
                    lhsT = av_(GW_O + (ic % 2) * 4096 + j * 256 + it2 * 128, 128)
                    last = P.emit("tensor", lambda e, l=lhsT, j=j, p=pg, tb=tb: e.matmul(
                        p[:], l, av_(XNT_OWN + j * 1024 + tb * 512, 512),
                        start=(j == 0), stop=(j == HT - 1)),
                        wait=emax(evGW[ic], gfree[pi],
                                  ev_attn_all if ic == 0 else None),
                        inc="pe" if j == HT - 1 else None)
                last_gmm = evGMM[ic] = last
                last = None
                for j in range(HT):
                    lhsT = av_(UW_O + (ic % 2) * 4096 + j * 256 + it2 * 128, 128)
                    last = P.emit("tensor", lambda e, l=lhsT, j=j, p=pu, tb=tb: e.matmul(
                        p[:], l, av_(XNT_OWN + j * 1024 + tb * 512, 512),
                        start=(j == 0), stop=(j == HT - 1)),
                        wait=emax(evUW[ic], ufree[pi]),
                        inc="pe" if j == HT - 1 else None)
                last_umm = evUMM[ic] = last
                esil = P.emit(
                    "scalar", lambda e, p=pg, tb=tb, itile=itile: e.activation(
                        out=gst[:, tb * 512:(tb + 1) * 512], in_=p[:], func=AF.Silu,
                        scale=gs_sb[:, itile:itile + 1]),
                    wait=emax(last_gmm, evHM.get((itile - 1, tb))),
                    inc="se")
                gfree[pi] = esil
                eue = P.emit(
                    "vector", lambda e, p=pu, tb=tb, itile=itile: e.tensor_scalar_mul(
                        ust[:, tb * 512:(tb + 1) * 512], p[:],
                        us_sb[:, itile:itile + 1]),
                    wait=emax(last_umm, evHM.get((itile - 1, tb))),
                    inc="ve")
                ufree[pi] = eue
                last_hm = evHM[(itile, tb)] = P.emit(
                    "vector", lambda e, tb=tb, itile=itile: e.tensor_mul(
                        av_(H_OFF + itile * 1024 + tb * 512, 512),
                        gst[:, tb * 512:(tb + 1) * 512],
                        ust[:, tb * 512:(tb + 1) * 512]),
                    wait=emax(esil, eue), inc="ve")

    ev_ffn_mm_done = emax(last_gmm, last_umm)

    dump_now("hdbg", emax(last_hm))

    xq_v = [avf(XQ_O + a * 4096, 2048) for a in range(4)]
    ost_v = [avf(OST_O + i * 1024, 512) for i in range(2)]

    evXQ = {}
    evWXMM = {}
    evDNMM = {}
    evOST = {}
    evODMA = {}
    evE1 = {}
    ev_out = None
    for tb in range(2):
        for a in range(4):
            trow = (28 + tb * 4 + a) * 128
            evXQ[(tb, a)] = P.emit("sync", lambda e, a=a, trow=trow: e.dma_start(
                out=xq_v[a], in_=xkv[trow:trow + 128, :]),
                wait=emax(ev_ffn_mm_done, dbg_gate[0],
                          evWXMM.get((tb - 1, 15)) if tb >= 1 else None),
                inc="dx", n=16)
        for H in range(16):
            p2 = H % 2
            evDW = P.emit("gpsimd", lambda e, H=H, p2=p2: e.dma_start(
                out=av_(DW_O + p2 * 5632, 5632), in_=dw_h[H]),
                wait=emax(evE1.get((tb, H - 2)), ev_ffn_mm_done,
                          evDNMM.get((tb - 1, 14 + H)) if tb >= 1 else None),
                inc="dw", n=16)
            evWO = P.emit("gpsimd", lambda e, H=H, p2=p2: e.dma_start(
                out=av_(WO_O + p2 * 2048, 2048), in_=wo_h[H]),
                wait=emax(evWXMM.get((tb, H - 2)),
                          evWXMM.get((tb - 1, 14 + H)) if tb >= 1 else None),
                inc="dw", n=16)
            pw = psF[p2]
            pd = psF[2 + p2]
            last = None
            for j in range(HT):
                lhsT = av_(WO_O + p2 * 2048 + j * 128, 128)
                last = P.emit("tensor", lambda e, l=lhsT, j=j, p=pw, tb=tb: e.matmul(
                    p[:], l, av_(ATTN_OT + j * 1024 + tb * 512, 512),
                    start=(j == 0), stop=False),
                    wait=emax(evWO, ev_attn_all if (tb == 0 and H == 0) else None,
                              evOST.get((tb, H - 2)),
                              evOST.get((tb - 1, 14 + H)) if tb >= 1 else None),
                    inc=None)
            for a in range(4):
                last = P.emit("tensor", lambda e, a=a, H=H, p=pw: e.matmul(
                    p[:, a * 128:(a + 1) * 128],
                    xq_v[a][:, H * 128:(H + 1) * 128], idf_sb[:],
                    is_transpose=True, start=False, stop=(a == 3)),
                    wait=emax(evXQ[(tb, a)]), inc="pe" if a == 3 else None)
            evWXMM[(tb, H)] = last
            last = None
            for r in range(ITL):
                lhsT = av_(DW_O + p2 * 5632 + r * 128, 128)
                last = P.emit("tensor", lambda e, l=lhsT, r=r, p=pd, tb=tb: e.matmul(
                    p[:], l, av_(H_OFF + r * 1024 + tb * 512, 512),
                    start=(r == 0), stop=(r == ITL - 1)),
                    wait=emax(evDW, last_hm if (tb == 0 and H == 0) else None,
                              evE1.get((tb, H - 2)),
                              evE1.get((tb - 1, 14 + H)) if tb >= 1 else None),
                    inc="pe" if r == ITL - 1 else None)
            evDNMM[(tb, H)] = last
            e1 = P.emit("vector", lambda e, p=pd, p2=p2, H=H: e.tensor_scalar_mul(
                tmp4_sb[:, p2 * 512:(p2 + 1) * 512], p[:], ds_sb[:, H:H + 1]),
                wait=emax(evDNMM[(tb, H)]), inc="ve")
            evE1[(tb, H)] = e1
            evOST[(tb, H)] = P.emit("vector", lambda e, p=pw, p2=p2: e.tensor_add(
                ost_v[p2][:], p[:], tmp4_sb[:, p2 * 512:(p2 + 1) * 512]),
                wait=emax(evWXMM[(tb, H)], e1,
                          evODMA.get((tb, H - 2)),
                          evODMA.get((tb - 1, 14 + H)) if tb >= 1 else None),
                inc="ve")
            evODMA[(tb, H)] = P.emit("sync", lambda e, p2=p2, H=H, tb=tb: e.dma_start(
                out=out_t[H * 128:(H + 1) * 128, tb * 512:(tb + 1) * 512],
                in_=ost_v[p2][:]),
                wait=emax(evOST[(tb, H)]), inc="dout", n=16)

    P.emit("sync", None, wait=[("dout", P.cnt["dout"])])

    with nc.Block() as block:

        @block.sync
        def _(e):
            P.replay("sync", e, semh)

        @block.gpsimd
        def _(e):
            P.replay("gpsimd", e, semh)

        @block.tensor
        def _(e):
            P.replay("tensor", e, semh)

        @block.vector
        def _(e):
            P.replay("vector", e, semh)

        @block.scalar
        def _(e):
            P.replay("scalar", e, semh)

    return nc, ctx


def _tile_w(w, rows_per):
    R = w.shape[0]
    return np.ascontiguousarray(
        w.T.reshape(HT, 128, R).transpose(1, 0, 2).reshape(128, HT * R))


def _prep_shared(w_qkv, w_o, gate_w, up_w, down_w, gate_s, up_s, down_s):
    d = {}
    d["wkv_h"] = _tile_w(w_qkv[2048:2560, :], None)
    d["wq_h"] = np.stack([_tile_w(w_qkv[oc * 256:(oc + 1) * 256, :], None)
                          for oc in range(8)])
    d["wo_h"] = np.stack([
        np.ascontiguousarray(w_o[H * 128:(H + 1) * 128, :].T
                             .reshape(HT, 128, 128).transpose(1, 0, 2)
                             .reshape(128, 2048))
        for H in range(16)])
    d["gw_h"] = np.stack([
        np.ascontiguousarray(gate_w[ic * 256:(ic + 1) * 256, :].T
                             .reshape(HT, 128, 256).transpose(1, 0, 2)
                             .reshape(128, 4096))
        for ic in range(22)])
    d["uw_h"] = np.stack([
        np.ascontiguousarray(up_w[ic * 256:(ic + 1) * 256, :].T
                             .reshape(HT, 128, 256).transpose(1, 0, 2)
                             .reshape(128, 4096))
        for ic in range(22)])
    d["dw_h"] = np.stack([
        np.ascontiguousarray(down_w[H * 128:(H + 1) * 128, :].T
                             .reshape(ITL, 128, 128).transpose(1, 0, 2)
                             .reshape(128, 5632))
        for H in range(16)])
    d["gs_h"] = np.ascontiguousarray(gate_s.reshape(ITL, 128).T)
    d["us_h"] = np.ascontiguousarray(up_s.reshape(ITL, 128).T)
    d["ds_h"] = np.ascontiguousarray(down_s.reshape(16, 128).T)
    d["idb_h"] = np.eye(128, dtype=np.float32)
    d["idf_h"] = np.eye(128, dtype=np.float32)
    d["pones_h"] = np.ones((1, 128), dtype=np.float32)
    d["mones_h"] = -np.ones((1, 128), dtype=np.float32)
    mk = np.zeros((128, 4 * 512), dtype=np.float32)
    i = np.arange(512)
    j = np.arange(128)
    for m in range(4):
        mk[:, m * 512:(m + 1) * 512] = (128 * m + j[:, None] <= i[None, :])
    d["mask_h"] = mk
    return d


def _rope_tab(pos):
    inv = 1.0 / (10000.0 ** (np.arange(0, DH, 2, dtype=np.float64) / DH))
    ang = pos[:, None].astype(np.float64) * inv[None, :]
    ntile = len(pos) // 128
    tab = np.empty((128, ntile * 64), dtype=np.float32)
    c = np.cos(ang).astype(np.float32)
    s = np.sin(ang).astype(np.float32)
    for t in range(ntile):
        tab[:, t * 64:t * 64 + 32] = c[t * 128:(t + 1) * 128]
        tab[:, t * 64 + 32:t * 64 + 64] = s[t * 128:(t + 1) * 128]
    return tab


def _core_geometry(c):
    chunks = [(0, j) for j in range(c)] + [(1, j) for j in range(7 - c)] + \
             [(0, c), (1, 7 - c)]
    sel = np.zeros((1, 8), dtype=np.int32)
    for k in range(7):
        sel[0, k] = 0 if k < c else 512
    sel[0, 7] = c
    return chunks, sel


def kernel(L0_attn_hidden_states, L0_attn_w_qkv, L0_attn_w_o,
           L0_ffn_gate_w, L0_ffn_up_w, L0_ffn_down_w,
           L0_ffn_gate_s, L0_ffn_up_s, L0_ffn_down_s):
    x = np.ascontiguousarray(np.asarray(L0_attn_hidden_states, dtype=np.float32)
                             .reshape(2 * SEQ, HID))
    shared = _prep_shared(np.asarray(L0_attn_w_qkv, np.float32),
                          np.asarray(L0_attn_w_o, np.float32),
                          np.asarray(L0_ffn_gate_w, np.float32),
                          np.asarray(L0_ffn_up_w, np.float32),
                          np.asarray(L0_ffn_down_w, np.float32),
                          np.asarray(L0_ffn_gate_s, np.float32),
                          np.asarray(L0_ffn_up_s, np.float32),
                          np.asarray(L0_ffn_down_s, np.float32))

    in_maps = []
    for c in range(NCORE):
        chunks, sel = _core_geometry(c)
        rows = []
        pos = []
        for bt, j in chunks:
            rows.append(x[bt * SEQ + j * 512: bt * SEQ + (j + 1) * 512])
            pos.append(np.arange(j * 512, (j + 1) * 512))
        xkv = np.ascontiguousarray(np.concatenate(rows, 0))
        ctab = _rope_tab(np.concatenate(pos))
        m = dict(shared)
        m["xkv"] = xkv
        m["ctok_h"] = ctab
        m["sel_h"] = sel
        in_maps.append(m)

    nc, ctx = build_program()
    try:
        res = run_bass_kernel_spmd(nc, in_maps, core_ids=list(range(NCORE)))
    finally:
        ctx.close()

    out = np.empty((2 * SEQ, HID), dtype=np.float32)
    for c in range(NCORE):
        ot = np.asarray(res.results[c]["out_t"])
        out[c * 512:(c + 1) * 512] = ot[:, 0:512].T
        out[SEQ + (7 - c) * 512: SEQ + (8 - c) * 512] = ot[:, 512:1024].T
    return out.reshape(2, SEQ, HID)


# version 09
# speedup vs baseline: 1.2160x; 1.0111x over previous
import contextlib

import numpy as np
import ml_dtypes

import concourse.bass as bass
import concourse.mybir as mybir
from concourse.bass_utils import run_bass_kernel_spmd

F32 = mybir.dt.float32
BF16 = mybir.dt.bfloat16
I32 = mybir.dt.int32
AF = mybir.ActivationFunctionType
ALU = mybir.AluOpType
AX = mybir.AxisListType
BF = ml_dtypes.bfloat16

NCORE = 8
SEQ = 4096
HID = 2048
HQ = 32
DH = 64
INTER = 5632
NTILE = 36
KVROWS = NTILE * 128
TOWN = 1024
HT = 16
ITL = 44
EPS = 1e-5

ENG = ("sync", "gpsimd", "tensor", "vector", "scalar")

ACOLS = 101888
XNT_OWN = 0
ATTN_OT = 16384
H_OFF = 32768
KTA_O = 32768
KTB_O = 37376
VS_O = 41984
CTOK_O = 51344
MASK_O = 53648
WKV_O = 55696
XN_O = 63888
XNTT_O = 72080
KTOK_O = 76176
XBUF_O = 76688
QTOK_O = 16384
QT_O = 72080
WQC_O = 88464
RTMPA_O = 98304
RTMPB_O = 99328
QSQT_O = 90512
ATTA_O = 88464
ATTB_O = 90000
RBFF_O = 91536
RBF2_O = 93584
RBSB_O = 94608
ATMP_O = 96656
ZERO_O = 97680
KT2A_O = 55696
KT2B_O = 60304
GW_O = 77824
UW_O = 86016
GST_O = 94208
UST_O = 95232
XQ_O = 0
DW_O = 77824
WO_O = 89088
OST_O = 93184


class Prog:
    def __init__(self):
        self.ops = {e: [] for e in ENG}
        self.cnt = {}

    def sem(self, name):
        self.cnt.setdefault(name, 0)

    def emit(self, eng, fn, wait=(), inc=None, n=1):
        wait = tuple((s, v) for (s, v) in wait if s is not None and v > 0)
        ev = None
        if inc is not None:
            self.cnt[inc] += n
            ev = (inc, self.cnt[inc])
        self.ops[eng].append((fn, wait, inc, n))
        return ev

    def replay(self, eng, e, sems):
        last = {s: 0 for s in self.cnt}
        for fn, wait, inc, n in self.ops[eng]:
            for s, v in wait:
                if last[s] < v:
                    e.wait_ge(sems[s], v)
                    last[s] = v
            ins = fn(e) if fn is not None else None
            if inc is not None and ins is not None:
                ins.then_inc(sems[inc], n)


def emax(*evs):
    best = {}
    for ev in evs:
        if not ev:
            continue
        if isinstance(ev, list):
            for s, v in ev:
                if best.get(s, 0) < v:
                    best[s] = v
        else:
            s, v = ev
            if best.get(s, 0) < v:
                best[s] = v
    return list(best.items())


def build_program(dbg=()):
    nc = bass.Bass("TRN2", target_bir_lowering=False)

    xkv = nc.dram_tensor("xkv", [KVROWS, HID], F32, kind="ExternalInput").ap()
    wkv_h = nc.dram_tensor("wkv_h", [128, 8192], F32, kind="ExternalInput").ap()
    wq_h = nc.dram_tensor("wq_h", [8, 128, 4096], F32, kind="ExternalInput").ap()
    wo_h = nc.dram_tensor("wo_h", [16, 128, 2048], F32, kind="ExternalInput").ap()
    gw_h = nc.dram_tensor("gw_h", [22, 128, 4096], F32, kind="ExternalInput").ap()
    uw_h = nc.dram_tensor("uw_h", [22, 128, 4096], F32, kind="ExternalInput").ap()
    dw_h = nc.dram_tensor("dw_h", [16, 128, 5632], F32, kind="ExternalInput").ap()
    gs_h = nc.dram_tensor("gs_h", [128, 44], F32, kind="ExternalInput").ap()
    us_h = nc.dram_tensor("us_h", [128, 44], F32, kind="ExternalInput").ap()
    ds_h = nc.dram_tensor("ds_h", [128, 16], F32, kind="ExternalInput").ap()
    ctok_h = nc.dram_tensor("ctok_h", [128, NTILE * 64], F32, kind="ExternalInput").ap()
    mask_h = nc.dram_tensor("mask_h", [128, 2048], F32, kind="ExternalInput").ap()
    sel_h = nc.dram_tensor("sel_h", [1, 8], I32, kind="ExternalInput").ap()
    idb_h = nc.dram_tensor("idb_h", [128, 128], F32, kind="ExternalInput").ap()
    idf_h = nc.dram_tensor("idf_h", [128, 128], F32, kind="ExternalInput").ap()
    pones_h = nc.dram_tensor("pones_h", [1, 128], F32, kind="ExternalInput").ap()
    mones_h = nc.dram_tensor("mones_h", [1, 128], F32, kind="ExternalInput").ap()
    out_t = nc.dram_tensor("out_t", [HID, TOWN], F32, kind="ExternalOutput").ap()

    dbg_t = {}
    dbg_specs = {
        "xnT_own": ([128, 16384], BF16), "ktA": ([128, 4608], BF16),
        "ktB": ([128, 4608], BF16), "vS": ([128, 36 * 260], BF16),
        "mk2row": ([1, 4], F32), "qT": ([128, 16384], BF16),
        "bias_all": ([128, 32], F32), "attn_oT": ([128, 16384], BF16),
        "hdbg": ([128, ITL * 1024], BF16), "qtok": ([128, 16384], BF16),
    }
    for name in dbg:
        shp, dt = dbg_specs[name]
        dbg_t[name] = nc.dram_tensor("dbg_" + name, shp, dt, kind="ExternalOutput").ap()

    P = Prog()
    semh = {}
    ctx = contextlib.ExitStack()
    for s in ("dx", "dw", "dout", "pe", "ve", "se"):
        P.sem(s)
        semh[s] = ctx.enter_context(nc.semaphore(s))

    def sb(name, shape, dt):
        return ctx.enter_context(nc.sbuf_tensor(name, shape, dt))

    arena = sb("arena", [128, ACOLS], BF16)
    bn_sb = sb("bn_sb", [128, 96], F32)
    agg_sb = sb("agg_sb", [128, 8], F32)
    varp_sb = sb("varp_sb", [128, 4], F32)
    negmr_sb = sb("negmr_sb", [128, 4], F32)
    rstd_sb = sb("rstd_sb", [128, 4], F32)
    ksqt_sb = sb("ksqt_sb", [128, 256], BF16)
    kn4_sb = sb("kn4_sb", [128, 4], F32)
    knmax_sb = sb("knmax_sb", [128, 4], F32)
    knm4_sb = sb("knm4_sb", [4, 1], F32)
    mk2_sb = sb("mk2_sb", [1, 4], F32)
    qn_sb = sb("qn_sb", [128, 32], F32)
    qnmax_sb = sb("qnmax_sb", [128, 32], F32)
    qmx32_sb = sb("qmx32_sb", [32, 1], F32)
    qmaxr_sb = sb("qmaxr_sb", [1, 32], F32)
    prodr_sb = sb("prodr_sb", [1, 32], F32)
    sqrow_sb = sb("sqrow_sb", [1, 32], BF16)
    biasa_sb = sb("biasa_sb", [128, 32], F32)
    biasp_sb = sb("biasp_sb", [128, 16], F32)
    recip8_sb = sb("recip8_sb", [128, 8], F32)
    gs_sb = sb("gs_sb", [128, 44], F32)
    us_sb = sb("us_sb", [128, 44], F32)
    ds_sb = sb("ds_sb", [128, 16], F32)
    sel_sb = sb("sel_sb", [1, 8], I32)
    idb_sb = sb("idb_sb", [128, 128], BF16)
    idf_sb = sb("idf_sb", [128, 128], F32)
    pones_sb = sb("pones_sb", [1, 128], BF16)
    mones_sb = sb("mones_sb", [1, 128], BF16)
    tmp4_sb = sb("tmp4_sb", [128, 1024], F32)

    psT0 = ctx.enter_context(nc.psum_tensor("psT0", [128, 1024], BF16))
    psT1 = ctx.enter_context(nc.psum_tensor("psT1", [128, 1024], BF16))
    psW = [ctx.enter_context(nc.psum_tensor(f"psW{i}", [128, 1024], F32))
           for i in range(2)]
    psF = [psW[0][:][:, 0:512], psW[0][:][:, 512:1024],
           psW[1][:][:, 0:512], psW[1][:][:, 512:1024]]
    psAV = ctx.enter_context(nc.psum_tensor("psAV", [128, 1024], F32))
    psAVb = psAV[:].bitcast(BF16)

    def av_(c0, n, r0=0, r1=128):
        return arena[r0:r1, c0:c0 + n]

    def avf(c0, nf32, r0=0, r1=128):
        return arena[r0:r1, c0:c0 + 2 * nf32].bitcast(F32)

    probe_ap = arena[64:128, 5:517]
    assert probe_ap.offset == 64 * ACOLS + 5, probe_ap.offset

    xnt_own = av_(XNT_OWN, 16384)
    attn_ot = av_(ATTN_OT, 16384)
    h_sb = av_(H_OFF, ITL * 1024)
    ktA = av_(KTA_O, 4608)
    ktB = av_(KTB_O, 4608)
    vS = av_(VS_O, 36 * 260)
    ctok = av_(CTOK_O, NTILE * 64)
    masks = av_(MASK_O, 2048)
    wkv_sb = av_(WKV_O, 8192)
    xn_v = [av_(XN_O + i * 2048, 2048) for i in range(4)]
    ktok_v = [av_(KTOK_O + i * 256, 256) for i in range(2)]
    xbuf_v = [avf(XBUF_O + i * 4096, 2048) for i in range(4)]

    ev_wkv = P.emit("gpsimd", lambda e: e.dma_start(out=wkv_sb, in_=wkv_h),
                    inc="dw", n=16)
    evc = None
    for dst, srca in ((ctok, ctok_h), (masks, mask_h), (idb_sb[:], idb_h),
                      (pones_sb[:], pones_h), (mones_sb[:], mones_h)):
        evc = P.emit("gpsimd", lambda e, d=dst, s=srca: e.dma_start(out=d, in_=s),
                     inc="dw", n=16)
    for dst, srca in ((sel_sb[:], sel_h), (idf_sb[:], idf_h),
                      (gs_sb[:], gs_h), (us_sb[:], us_h), (ds_sb[:], ds_h)):
        evc2 = P.emit("sync", lambda e, d=dst, s=srca: e.dma_start(out=d, in_=s),
                      inc="dx", n=16)
    ev_consts = emax(evc, evc2)

    ev_knz = P.emit("vector", lambda e: e.memset(knmax_sb[:], 0.0), inc="ve")
    ev_qnz = P.emit("vector", lambda e: e.memset(qnmax_sb[:], 0.0), inc="ve")
    ones_ap = bass.AP(arena, VS_O + 64, [[ACOLS, 128], [260, 36], [65, 4], [1, 1]])
    ev_vones = P.emit("vector", lambda e, a=ones_ap: e.memset(a, 1.0), inc="ve")

    evXL = [None] * NTILE
    evXN = [None] * NTILE
    evS3 = [None] * NTILE
    evS4 = [None] * NTILE
    evS5 = [None] * NTILE
    evKR = [None] * NTILE
    evKN = [None] * NTILE
    evS7 = [None] * NTILE
    evS8 = [None] * NTILE
    evVE = [None] * NTILE

    def xnt_src_slice(t, j):
        s = t // 4
        if s >= 7:
            return av_(XNT_OWN + j * 1024 + (s - 7) * 512 + (t % 4) * 128, 128)
        return av_(XNTT_O + (t % 2) * 2048 + j * 128, 128)

    lastT_user = [-1]
    lastA_user = [-1]
    for step in range(NTILE + 2):
        t = step
        if t < NTILE:
            b = t % 4
            b2p = t % 2
            src_x = xkv[t * 128:(t + 1) * 128, :]
            evXL[t] = P.emit(
                "sync", lambda e, s=src_x, d=xbuf_v[b]: e.dma_start(out=d, in_=s),
                wait=emax(evXN[t - 4] if t >= 4 else None), inc="dx", n=16)
            e1 = None
            for i in range(4):
                e1 = P.emit("vector", lambda e, i=i, b=b: e.bn_stats(
                    bn_sb[:, b * 24 + i * 6:b * 24 + (i + 1) * 6],
                    xbuf_v[b][:, i * 512:(i + 1) * 512]),
                    wait=emax(evXL[t]), inc="ve")
            e2 = P.emit("vector", lambda e, b=b: e.bn_aggr(
                agg_sb[:, 2 * b:2 * b + 2], bn_sb[:, b * 24:b * 24 + 24]),
                wait=emax(e1), inc="ve")
            e3 = P.emit("vector", lambda e, b=b: e.tensor_scalar_add(
                varp_sb[:, b:b + 1], agg_sb[:, 2 * b + 1:2 * b + 2], EPS),
                wait=emax(e2), inc="ve")
            e4 = P.emit("vector", lambda e, b=b: e.reciprocal(
                varp_sb[:, b:b + 1], varp_sb[:, b:b + 1]), wait=emax(e3), inc="ve")
            e5 = P.emit("scalar", lambda e, b=b: e.sqrt(
                rstd_sb[:, b:b + 1], varp_sb[:, b:b + 1]), wait=emax(e4), inc="se")
            e6 = P.emit("vector", lambda e, b=b: e.tensor_scalar(
                out=negmr_sb[:, b:b + 1], in0=agg_sb[:, 2 * b:2 * b + 1],
                scalar1=rstd_sb[:, b:b + 1], scalar2=-1.0,
                op0=ALU.mult, op1=ALU.mult),
                wait=emax(e5), inc="ve")
            evXN[t] = P.emit("scalar", lambda e, b=b: e.activation(
                out=xn_v[b], in_=xbuf_v[b], func=AF.Identity,
                bias=negmr_sb[:, b:b + 1], scale=rstd_sb[:, b:b + 1]),
                wait=emax(e6, evS3[t - 4] if t >= 4 else None), inc="se")
            s = t // 4
            own = s >= 7
            use_psT = own or (t % 2 == 0)
            tp_wait = emax(evXN[t],
                           evS4[lastT_user[0]] if (use_psT and lastT_user[0] >= 0)
                           else None,
                           evS4[lastA_user[0]] if ((not use_psT) and
                                                   lastA_user[0] >= 0) else None,
                           ev_consts if t == 0 else None)
            last = None
            for j in range(16):
                if use_psT:
                    ps = psT0[:] if j < 8 else psT1[:]
                    jj = j % 8
                else:
                    ps = psAVb
                    jj = j
                last = P.emit("tensor", lambda e, b=b, j=j, ps=ps, jj=jj: e.transpose(
                    ps[:, jj * 128:(jj + 1) * 128],
                    xn_v[b][:, j * 128:(j + 1) * 128], idb_sb[:]),
                    wait=tp_wait if j == 0 else (),
                    inc="pe" if j == 15 else None)
            evS3[t] = last
            if use_psT:
                lastT_user[0] = t
            else:
                lastA_user[0] = t
            if own:
                off = XNT_OWN + (s - 7) * 512 + (t % 4) * 128
                d0 = bass.AP(arena, off, [[ACOLS, 128], [1024, 8], [1, 128]])
                d1 = bass.AP(arena, off + 8 * 1024, [[ACOLS, 128], [1024, 8], [1, 128]])
                s0 = bass.AP(psT0, 0, [[1024, 128], [128, 8], [1, 128]])
                s1 = bass.AP(psT1, 0, [[1024, 128], [128, 8], [1, 128]])
            elif use_psT:
                d0 = av_(XNTT_O + b2p * 2048, 1024)
                d1 = av_(XNTT_O + b2p * 2048 + 1024, 1024)
                s0, s1 = psT0[:], psT1[:]
            else:
                d0 = av_(XNTT_O + b2p * 2048, 1024)
                d1 = av_(XNTT_O + b2p * 2048 + 1024, 1024)
                s0, s1 = psAVb[:, 0:1024], psAVb[:, 1024:2048]
            w4 = emax(evS3[t], evS5[t - 2] if (t >= 2 and (t - 2) // 4 < 7) else None)
            P.emit("scalar", lambda e, d=d0, s_=s0: e.copy(d, s_), wait=w4)
            evS4[t] = P.emit("scalar", lambda e, d=d1, s_=s1: e.copy(d, s_),
                             inc="se")
        if step >= 1 and (step - 1) < NTILE:
            t1 = step - 1
            b1 = t1 % 2
            last = None
            for j in range(HT):
                lhsT = xnt_src_slice(t1, j)
                last = P.emit("tensor", lambda e, l=lhsT, j=j, b1=b1: e.matmul(
                    psF[b1], l, wkv_sb[:, j * 512:(j + 1) * 512],
                    start=(j == 0), stop=(j == HT - 1)),
                    wait=emax(evS4[t1], ev_wkv,
                              evKR[t1 - 2] if t1 >= 2 else None,
                              evVE[t1 - 2] if t1 >= 2 else None),
                    inc="pe" if j == HT - 1 else None)
            evS5[t1] = last
            vdst = bass.AP(arena, VS_O + t1 * 260, [[ACOLS, 128], [65, 4], [1, 64]])
            vsrc = bass.AP(psW[0], b1 * 512 + 256, [[1024, 128], [64, 4], [1, 64]])
            evVE[t1] = P.emit("scalar", lambda e, d=vdst, s=vsrc: e.copy(d, s),
                              wait=emax(evS5[t1], ev_vones), inc="se")
            x1 = bass.AP(psW[0], b1 * 512, [[1024, 128], [64, 4], [1, 32]])
            x2 = bass.AP(psW[0], b1 * 512 + 32, [[1024, 128], [64, 4], [1, 32]])
            cosb = bass.AP(arena, CTOK_O + t1 * 64, [[ACOLS, 128], [0, 4], [1, 32]])
            sinb = bass.AP(arena, CTOK_O + t1 * 64 + 32,
                           [[ACOLS, 128], [0, 4], [1, 32]])
            k1 = bass.AP(arena, KTOK_O + b1 * 256, [[ACOLS, 128], [64, 4], [1, 32]])
            k2 = bass.AP(arena, KTOK_O + b1 * 256 + 32,
                         [[ACOLS, 128], [64, 4], [1, 32]])
            rA = bass.AP(arena, RTMPA_O, [[ACOLS, 128], [32, 4], [1, 32]])
            rB = bass.AP(arena, RTMPB_O, [[ACOLS, 128], [32, 4], [1, 32]])
            wr = emax(evS5[t1], evVE[t1], ev_consts,
                      evS7[t1 - 2] if t1 >= 2 else None)
            P.emit("vector", lambda e, o=rA, a=x2, c=sinb: e.tensor_mul(o, a, c),
                   wait=wr)
            P.emit("vector", lambda e, o=rB, a=x1, c=sinb: e.tensor_mul(o, a, c))
            P.emit("vector", lambda e, o=k1, a=x1, c=cosb: e.tensor_mul(o, a, c))
            P.emit("vector", lambda e, o=k1, a=rA: e.tensor_sub(o, o, a))
            P.emit("vector", lambda e, o=k2, a=x2, c=cosb: e.tensor_mul(o, a, c))
            evKR[t1] = P.emit("vector", lambda e, o=k2, a=rB: e.tensor_add(o, o, a),
                              inc="ve")
            eq = P.emit("scalar", lambda e, b1=b1: e.square(ksqt_sb[:], ktok_v[b1]),
                        wait=emax(evKR[t1], evKN[t1 - 1] if t1 >= 1 else None),
                        inc="se")
            er = None
            for r in range(4):
                er = P.emit("vector", lambda e, r=r: e.reduce_sum(
                    kn4_sb[:, r:r + 1], ksqt_sb[:, r * 64:(r + 1) * 64], axis=AX.X),
                    wait=emax(eq), inc="ve")
            evKN[t1] = P.emit("vector", lambda e: e.tensor_max(
                knmax_sb[:], knmax_sb[:], kn4_sb[:]), wait=emax(er, ev_knz), inc="ve")
        if step >= 2 and (step - 2) < NTILE:
            t2 = step - 2
            b2 = t2 % 2
            psF2b = psF[2].bitcast(BF16)
            last = None
            for blk in range(2):
                last = P.emit("tensor", lambda e, b2=b2, blk=blk, p=psF2b: e.transpose(
                    p[:, blk * 128:(blk + 1) * 128],
                    ktok_v[b2][:, blk * 128:(blk + 1) * 128], idb_sb[:]),
                    wait=emax(evKR[t2], evS8[t2 - 1] if t2 >= 1 else None),
                    inc="pe" if blk == 1 else None)
            evS7[t2] = last
            P.emit("scalar", lambda e, t2=t2, p=psF2b: e.copy(
                av_(KTA_O + t2 * 128, 128), p[:, 0:128]), wait=emax(evS7[t2]))
            evS8[t2] = P.emit("scalar", lambda e, t2=t2, p=psF2b: e.copy(
                av_(KTB_O + t2 * 128, 128), p[:, 128:256]), inc="se")
    eT1 = P.emit("tensor", lambda e: e.transpose(
        psF[0][0:4, 0:128], knmax_sb[:], idf_sb[:]),
        wait=emax(evKN[NTILE - 1], evVE[NTILE - 1], evKR[NTILE - 1]), inc="pe")
    eT2 = P.emit("vector", lambda e: e.reduce_max(
        knm4_sb[:], psF[0][0:4, 0:128], axis=AX.X), wait=emax(eT1), inc="ve")
    eT3 = P.emit("tensor", lambda e: e.transpose(
        psF[0][0:1, 0:4], knm4_sb[:], idf_sb[0:4, 0:4]), wait=emax(eT2), inc="pe")
    evMK2 = P.emit("vector", lambda e: e.tensor_copy(mk2_sb[:], psF[0][0:1, 0:4]),
                   wait=emax(eT3), inc="ve")

    dbg_gate = [None]

    def dump_now(name, w):
        if name not in dbg:
            return
        ev = P.emit("sync", lambda e, d=dbg_t[name], s=dmap0[name]: e.dma_start(
            out=d, in_=s), wait=w, inc="dout", n=16)
        dbg_gate[0] = emax(dbg_gate[0], ev)

    dmap0 = {
        "xnT_own": av_(XNT_OWN, 16384), "ktA": ktA, "ktB": ktB, "vS": vS,
        "mk2row": mk2_sb[:], "qT": av_(QT_O, 16384),
        "bias_all": biasa_sb[:], "attn_oT": av_(ATTN_OT, 16384),
        "hdbg": h_sb, "qtok": av_(QTOK_O, 16384),
    }
    for nm in ("xnT_own", "ktA", "ktB", "vS", "mk2row"):
        dump_now(nm, emax(evMK2, evS8[NTILE - 1], evVE[NTILE - 1], evS4[NTILE - 1],
                          evKR[NTILE - 1]))

    qtok_v = [av_(QTOK_O + tt * 2048, 2048) for tt in range(8)]
    wqc_v = [av_(WQC_O + i * 4096, 4096) for i in range(2)]
    attAB_v = [av_(ATTA_O + i * 1024, 1024) for i in range(3)]
    psT0f = psT0[:].bitcast(F32)
    psT1f = psT1[:].bitcast(F32)
    zero_v = av_(ZERO_O, 512)
    rbff_row = avf(RBFF_O, 1024, 0, 1)
    rbf2_row = arena[0:1, RBF2_O:RBF2_O + 1024]
    rbsb = avf(RBSB_O, 1024, 0, 64)
    atmp = av_(ATMP_O, 1024, 0, 64)

    ev_zero = P.emit("vector", lambda e: e.memset(zero_v, 0.0), inc="ve")

    REGS = {}

    def reg_setup(e):
        st = contextlib.ExitStack()
        REGS["stack"] = st
        REGS["cid"] = st.enter_context(e.register("cid"))
        e.reg_load(REGS["cid"], sel_sb[0:1, 7:8])
        return None

    P.emit("tensor", reg_setup, wait=emax(ev_consts))

    evQMM = {}
    evQE = {}
    for oc in range(8):
        evWQ = P.emit("gpsimd", lambda e, oc=oc: e.dma_start(
            out=wqc_v[oc % 2], in_=wq_h[oc]),
            wait=emax(evQMM.get((oc - 2, 7)),
                      emax(evKR[NTILE - 1], evVE[NTILE - 1])
                      if oc < 2 else None),
            inc="dw", n=16)
        for tt in range(8):
            last = None
            for j in range(HT):
                lhsT = av_(XNT_OWN + j * 1024 + tt * 128, 128)
                last = P.emit("tensor", lambda e, l=lhsT, j=j, tt=tt, oc=oc: e.matmul(
                    psF[2 + tt % 2][:, 0:256], l,
                    wqc_v[oc % 2][:, j * 256:(j + 1) * 256],
                    start=(j == 0), stop=(j == HT - 1)),
                    wait=emax(evWQ, evS4[NTILE - 1], evS8[NTILE - 1],
                              evQE.get((oc, tt - 2)), evQE.get((oc - 1, tt + 6))),
                    inc="pe" if j == HT - 1 else None)
            evQMM[(oc, tt)] = last
            evQE[(oc, tt)] = P.emit("vector", lambda e, oc=oc, tt=tt: e.tensor_copy(
                av_(QTOK_O + tt * 2048 + oc * 256, 256), psF[2 + tt % 2][:, 0:256]),
                wait=emax(evQMM[(oc, tt)]), inc="ve")

    evQR = [None] * 8
    evQN = [None] * 8
    evQT = [None] * 8
    evQTE = [None] * 8
    for tt in range(8):
        base = QTOK_O + tt * 2048
        x1 = bass.AP(arena, base, [[ACOLS, 128], [64, 32], [1, 32]])
        x2 = bass.AP(arena, base + 32, [[ACOLS, 128], [64, 32], [1, 32]])
        cosb = bass.AP(arena, CTOK_O + (28 + tt) * 64, [[ACOLS, 128], [0, 32], [1, 32]])
        sinb = bass.AP(arena, CTOK_O + (28 + tt) * 64 + 32,
                       [[ACOLS, 128], [0, 32], [1, 32]])
        rA = bass.AP(arena, RTMPA_O, [[ACOLS, 128], [32, 32], [1, 32]])
        rB = bass.AP(arena, RTMPB_O, [[ACOLS, 128], [32, 32], [1, 32]])
        w0 = emax(evQE[(7, tt)], ev_consts, evQMM[(7, 7)])
        P.emit("vector", lambda e, o=rA, a=x2, c=sinb: e.tensor_mul(o, a, c), wait=w0)
        P.emit("vector", lambda e, o=rB, a=x1, c=sinb: e.tensor_mul(o, a, c))
        P.emit("vector", lambda e, o=x1, a=x1, c=cosb: e.tensor_mul(o, a, c))
        P.emit("vector", lambda e, o=x1, a=rA: e.tensor_sub(o, o, a))
        P.emit("vector", lambda e, o=x2, a=x2, c=cosb: e.tensor_mul(o, a, c))
        evQR[tt] = P.emit("vector", lambda e, o=x2, a=rB: e.tensor_add(o, o, a),
                          inc="ve")
        eqs = P.emit("scalar", lambda e, tt=tt: e.square(av_(QSQT_O, 2048), qtok_v[tt]),
                     wait=emax(evQR[tt], evQN[tt - 1] if tt >= 1 else None), inc="se")
        er = None
        for hh in range(HQ):
            er = P.emit("vector", lambda e, hh=hh: e.reduce_sum(
                qn_sb[:, hh:hh + 1], av_(QSQT_O + hh * 64, 64), axis=AX.X),
                wait=emax(eqs), inc="ve")
        evQN[tt] = P.emit("vector", lambda e: e.tensor_max(
            qnmax_sb[:], qnmax_sb[:], qn_sb[:]), wait=emax(er, ev_qnz), inc="ve")
        last = None
        for j in range(16):
            ps = psT0 if j < 8 else psT1
            jj = j % 8
            last = P.emit("tensor", lambda e, tt=tt, j=j, ps=ps, jj=jj: e.transpose(
                ps[:, jj * 128:(jj + 1) * 128],
                qtok_v[tt][:, j * 128:(j + 1) * 128], idb_sb[:]),
                wait=emax(evQR[tt], evQTE[tt - 1] if tt >= 1 else None,
                          evS4[NTILE - 1]),
                inc="pe" if j == 15 else None)
        evQT[tt] = last
        d0 = bass.AP(arena, QT_O + tt * 128, [[ACOLS, 128], [1024, 8], [1, 128]])
        d1 = bass.AP(arena, QT_O + 8 * 1024 + tt * 128,
                     [[ACOLS, 128], [1024, 8], [1, 128]])
        s0 = bass.AP(psT0, 0, [[1024, 128], [128, 8], [1, 128]])
        s1 = bass.AP(psT1, 0, [[1024, 128], [128, 8], [1, 128]])
        P.emit("scalar", lambda e, d=d0, s_=s0: e.copy(d, s_),
               wait=emax(evQT[tt]))
        evQTE[tt] = P.emit("scalar", lambda e, d=d1, s_=s1: e.copy(d, s_),
                           inc="se")

    eU1 = P.emit("tensor", lambda e: e.transpose(
        psF[0][0:32, 0:128], qnmax_sb[:], idf_sb[:]),
        wait=emax(evQN[7], evQMM[(7, 7)], evMK2), inc="pe")
    eU2 = P.emit("vector", lambda e: e.reduce_max(
        qmx32_sb[:], psF[0][0:32, 0:128], axis=AX.X), wait=emax(eU1), inc="ve")
    eU3 = P.emit("tensor", lambda e: e.transpose(
        psF[0][0:1, 0:32], qmx32_sb[:], idf_sb[0:32, 0:32]), wait=emax(eU2), inc="pe")
    eU4 = P.emit("vector", lambda e: e.tensor_copy(qmaxr_sb[:], psF[0][0:1, 0:32]),
                 wait=emax(eU3), inc="ve")
    mk2bc = bass.AP(mk2_sb, 0, [[4, 1], [1, 4], [0, 8]])
    pr3 = bass.AP(prodr_sb, 0, [[32, 1], [8, 4], [1, 8]])
    qm3 = bass.AP(qmaxr_sb, 0, [[32, 1], [8, 4], [1, 8]])
    eU5 = P.emit("vector", lambda e, o=pr3, a=qm3, m=mk2bc: e.tensor_mul(o, a, m),
                 wait=emax(eU4, evMK2), inc="ve")
    eU6 = P.emit("vector", lambda e: e.tensor_scalar_mul(
        prodr_sb[:], prodr_sb[:], 1.0 / 64.0), wait=emax(eU5), inc="ve")
    eU7 = P.emit("scalar", lambda e: e.sqrt(sqrow_sb[:], prodr_sb[:]),
                 wait=emax(eU6), inc="se")
    dump_now("qtok", emax(evQR[7], evQN[7], evQTE[7]))
    wk2 = emax(evQTE[7], evQN[7], evS8[NTILE - 1], dbg_gate[0])
    P.emit("scalar", lambda e: e.copy(arena[0:64, KT2A_O:KT2A_O + 4608],
                                      arena[64:128, KTA_O:KTA_O + 4608]), wait=wk2)
    P.emit("scalar", lambda e: e.copy(arena[64:128, KT2A_O:KT2A_O + 4608],
                                      arena[0:64, KTA_O:KTA_O + 4608]))
    P.emit("scalar", lambda e: e.copy(arena[0:64, KT2B_O:KT2B_O + 4608],
                                      arena[64:128, KTB_O:KTB_O + 4608]))
    ev_kt2 = P.emit("scalar", lambda e: e.copy(arena[64:128, KT2B_O:KT2B_O + 4608],
                                               arena[0:64, KTB_O:KTB_O + 4608]),
                    inc="se")

    eB1 = P.emit("tensor", lambda e: e.matmul(
        psF[2][:, 0:32], mones_sb[:], sqrow_sb[:], start=True, stop=True),
        wait=emax(eU7, evQE[(7, 7)]), inc="pe")
    evBIAS = P.emit("vector", lambda e: e.tensor_copy(biasa_sb[:], psF[2][:, 0:32]),
                    wait=emax(eB1), inc="ve")

    dump_now("qT", emax(evQTE[7]))
    dump_now("bias_all", emax(evBIAS))

    blocks = [(7, m) for m in range(4)] + [(8, m) for m in range(4)] + \
             [(k, m) for k in range(7) for m in range(4)]
    NB = len(blocks)
    pairs = []
    for gkt, alo, bhi in ((KTA_O, 0, 9), (KT2A_O, 8, 1),
                          (KTB_O, 16, 25), (KT2B_O, 24, 17)):
        for i in range(4):
            pairs.append((gkt, alo + 2 * i, bhi + 2 * i))

    evBP = None
    for pi2, (_, hA2, hB2) in enumerate(pairs):
        evBP = P.emit("vector", lambda e, p=pi2, a=hA2, b=hB2: e.tensor_tensor(
            biasp_sb[:, p:p + 1], biasa_sb[:, a:a + 1], biasa_sb[:, b:b + 1],
            ALU.min), wait=emax(evBIAS), inc="ve")

    pe_ops = [[] for _ in range(8)]

    def pe_emit(fns, wait=()):
        P.cnt["pe"] += 1
        val = P.cnt["pe"]
        w = tuple((s, v) for (s, v) in emax(*wait) if v > 0) if wait else ()
        if not isinstance(fns, list):
            fns = [fns] * 8
        for ci in range(8):
            pe_ops[ci].append((w, fns[ci]))
        return ("pe", val)

    prev_sc_free = emax(eU2, eU4, evBIAS, eT2, evMK2, evBP)
    prev_av_free = emax(evS8[NTILE - 1], evMK2, evQTE[7],
                        evS4[NTILE - 2], evS4[NTILE - 1])
    prev_rb_free = emax(evBIAS)
    prev_avlast = None
    prev_atmp = None
    ev_attn_done_v = None
    ev_attn_done_s = None

    for pi, (kt_base, hA, hB) in enumerate(pairs):
        qcbA = QT_O + (hA // 2) * 1024
        qcbB = QT_O + (hB // 2) * 1024 + 64 * ACOLS
        kvA = hA // 8
        kvB = hB // 8

        evSC = [None] * NB
        evEX = [None] * NB
        evMK = [None] * NB
        evAVA = [None] * NB
        evAVB = [None] * NB

        def sc_fns(qcb, rowbase, slot, kb, psdst):
            lhs = arena[rowbase:rowbase + 64,
                        kt_base + kb * 128:kt_base + (kb + 1) * 128]
            if slot >= 7:
                r = bass.AP(arena, qcb + (512 if slot == 8 else 0),
                            [[ACOLS, 64], [1, 512]])
                return (lambda e, l=lhs, r=r, o=psdst:
                        e.matmul(o, l, r, start=True, stop=True))
            fns = []
            for ci in range(8):
                sel = 0 if slot < ci else 512
                r = bass.AP(arena, qcb + sel, [[ACOLS, 64], [1, 512]])
                fns.append(lambda e, l=lhs, r=r, o=psdst:
                           e.matmul(o, l, r, start=True, stop=True))
            return fns

        def emit_avs(bi):
            slot, m = blocks[bi]
            kb = slot * 4 + m
            st = (slot >= 7 and m == 0)
            vpA = vS[:, kb * 260 + kvA * 65:kb * 260 + (kvA + 1) * 65]
            vpB = vS[:, kb * 260 + kvB * 65:kb * 260 + (kvB + 1) * 65]
            aA = attAB_v[bi % 3][:, 0:512]
            aB = attAB_v[bi % 3][:, 512:1024]
            if slot >= 7:
                oA = psAV[0:65, 0:512] if slot == 7 else psAV[0:65, 512:1024]
                fA = (lambda e, v=vpA, o=oA, a=aA, st=st:
                      e.matmul(o, v, a, start=st, stop=False))
                oB = psT0f[0:65, :] if slot == 7 else psT1f[0:65, :]
                fB = (lambda e, v=vpB, o=oB, a=aB, st=st:
                      e.matmul(o, v, a, start=st, stop=False))
            else:
                fA = []
                fB = []
                for ci in range(8):
                    sel = 0 if slot < ci else 512
                    oA = psAV[0:65, sel:sel + 512]
                    fA.append(lambda e, v=vpA, o=oA, a=aA:
                              e.matmul(o, v, a, start=False, stop=False))
                    oB = psT0f[0:65, :] if sel == 0 else psT1f[0:65, :]
                    fB.append(lambda e, v=vpB, o=oB, a=aB:
                              e.matmul(o, v, a, start=False, stop=False))
            evAVA[bi] = pe_emit(fA, wait=[evMK[bi],
                                          prev_av_free if bi == 0 else None])
            evAVB[bi] = pe_emit(fB, wait=[evMK[bi],
                                          prev_av_free if bi == 0 else None])

        for bi, (slot, m) in enumerate(blocks):
            kb = slot * 4 + m
            ps = psW[bi % 2][:]
            wA = [evEX[bi - 2] if bi >= 2 else prev_sc_free,
                  emax(evQTE[7], evS8[NTILE - 1], ev_kt2)
                  if (pi == 0 and bi == 0) else None]
            evSC[bi] = pe_emit(sc_fns(qcbA, 0, slot, kb, ps[:, 0:512]), wait=wA)
            evSC[bi] = pe_emit(sc_fns(qcbB, 64, slot, kb, ps[:, 512:1024]))
            ab = attAB_v[bi % 3]
            evEX[bi] = P.emit(
                "scalar", lambda e, o=ab, i=ps, p=pi: e.activation(
                    out=o, in_=i, func=AF.Exp,
                    bias=biasp_sb[:, p:p + 1], scale=0.125),
                wait=emax(evSC[bi], evBP,
                          emax(evAVA[bi - 3], evAVB[bi - 3]) if bi >= 3
                          else prev_avlast),
                inc="se")
            if slot >= 7:
                mk = bass.AP(arena, MASK_O + m * 512,
                             [[ACOLS, 128], [0, 2], [1, 512]])
                ab3 = bass.AP(arena, ATTA_O + (bi % 3) * 1024,
                              [[ACOLS, 128], [512, 2], [1, 512]])
                evMK[bi] = P.emit(
                    "vector", lambda e, o=ab3, m_=mk: e.tensor_mul(o, o, m_),
                    wait=emax(evEX[bi]), inc="ve")
            else:
                evMK[bi] = evEX[bi]
            if bi >= 1:
                emit_avs(bi - 1)
        emit_avs(NB - 1)

        eZ1 = pe_emit(lambda e: e.matmul(
            psAV[0:65, 0:512], vS[:, 0:65], zero_v, start=False, stop=True),
            wait=[ev_zero, evAVA[NB - 1], evAVB[NB - 1]])
        eZ2 = pe_emit(lambda e: e.matmul(
            psAV[0:65, 512:1024], vS[:, 0:65], zero_v, start=False, stop=True))
        eZ3 = pe_emit(lambda e: e.matmul(
            psT0f[0:65, :], vS[:, 0:65], zero_v, start=False, stop=True))
        eZ4 = pe_emit(lambda e: e.matmul(
            psT1f[0:65, :], vS[:, 0:65], zero_v, start=False, stop=True))

        out_evs = []
        eR4 = None
        for role, hh in ((0, hA), (1, hB)):
            wz = emax(eZ4, out_evs[-1] if out_evs else None)
            if role == 0:
                eR0 = P.emit("scalar", lambda e: e.copy(
                    rbff_row, psAV[64:65, 0:1024]), wait=wz, inc="se")
            else:
                P.emit("scalar", lambda e: e.copy(
                    rbff_row[:, 0:512], psT0f[64:65, :]), wait=wz)
                eR0 = P.emit("scalar", lambda e: e.copy(
                    rbff_row[:, 512:1024], psT1f[64:65, :]), inc="se")
            wexp = [eR0, prev_rb_free,
                    emax(evEX[NB - 2], evEX[NB - 1]) if role == 0 else None]
            eT_f = None
            for hi in range(8):
                eT_f = pe_emit(lambda e, hi=hi: e.transpose(
                    psW[0][0:128, hi:hi + 1],
                    rbff_row[:, 128 * hi:128 * (hi + 1)], idf_sb[0:1, 0:1]),
                    wait=wexp if hi == 0 else ())
            eR1 = P.emit("vector", lambda e: e.reciprocal(
                recip8_sb[:], psW[0][:, 0:8]), wait=emax(eT_f), inc="ve")
            eT_b = None
            for hi in range(8):
                eT_b = pe_emit(lambda e, hi=hi: e.transpose(
                    psW[0][0:1, 128 * hi:128 * (hi + 1)],
                    recip8_sb[:, hi:hi + 1], idf_sb[:]),
                    wait=[eR1] if hi == 0 else ())
            eR1b = P.emit("scalar", lambda e: e.copy(rbf2_row, psW[0][0:1, 0:1024]),
                          wait=emax(eT_b), inc="se")
            eR2 = pe_emit(lambda e: e.matmul(
                psF[0][0:64, :], pones_sb[0:1, 0:64],
                arena[0:1, RBF2_O:RBF2_O + 512], start=True, stop=True),
                wait=[eR1b])
            eR3 = pe_emit(lambda e: e.matmul(
                psF[1][0:64, :], pones_sb[0:1, 0:64],
                arena[0:1, RBF2_O + 512:RBF2_O + 1024], start=True, stop=True))
            P.emit("vector", lambda e: e.tensor_copy(rbsb[:, 0:512], psF[0][0:64, :]),
                   wait=emax(eR2, eR3))
            eR4 = P.emit("vector", lambda e: e.tensor_copy(
                rbsb[:, 512:1024], psF[1][0:64, :]), inc="ve")
            prev_rb_free = emax(eR4)
            col = ATTN_OT + (hh // 2) * 1024
            if role == 0:
                dst = arena[0:64, col:col + 1024]
                evO = P.emit("vector", lambda e, d=dst: e.tensor_mul(
                    d, psAV[0:64, 0:1024], rbsb[:]), wait=emax(eR4), inc="ve")
                out_evs.append(evO)
                ev_attn_done_v = evO
            else:
                evO = P.emit("vector", lambda e: e.tensor_mul(
                    atmp[:, 0:512], psT0f[0:64, :], rbsb[:, 0:512]),
                    wait=emax(eR4, prev_atmp), inc="ve")
                evO = P.emit("vector", lambda e: e.tensor_mul(
                    atmp[:, 512:1024], psT1f[0:64, :], rbsb[:, 512:1024]),
                    inc="ve")
                dst = arena[64:128, col:col + 1024]
                evO2 = P.emit("scalar", lambda e, d=dst: e.copy(d, atmp),
                              wait=emax(evO), inc="se")
                prev_atmp = evO2
                out_evs.append(emax(evO, evO2))
                ev_attn_done_v = evO
                ev_attn_done_s = evO2
        prev_av_free = emax(*out_evs)
        prev_avlast = emax(evAVA[NB - 1], evAVB[NB - 1])
        prev_sc_free = emax(eR4, out_evs[-1])

    def attn_pe_block(e):
        for ci in range(8):
            with e.If_eq(REGS["cid"], ci):
                last = {}
                for w, fn in pe_ops[ci]:
                    for s, v in w:
                        if last.get(s, 0) < v:
                            e.wait_ge(semh[s], v)
                            last[s] = v
                    fn(e).then_inc(semh["pe"], 1)
        return None
    P.emit("tensor", attn_pe_block)

    ev_attn_all = emax(ev_attn_done_v, ev_attn_done_s)

    gw_v = [av_(GW_O + i * 4096, 4096) for i in range(2)]
    uw_v = [av_(UW_O + i * 4096, 4096) for i in range(2)]
    gst = av_(GST_O, 1024)
    ust = av_(UST_O, 1024)

    dump_now("attn_oT", ev_attn_all)
    evGW = {}
    evUW = {}
    evGMM = {}
    evUMM = {}
    evHM = {}
    gfree = {0: None, 1: None}
    ufree = {0: None, 1: None}
    last_gmm = None
    last_umm = None
    last_hm = None
    for ic in range(22):
        evGW[ic] = P.emit("gpsimd", lambda e, ic=ic: e.dma_start(
            out=gw_v[ic % 2], in_=gw_h[ic]),
            wait=emax(evGMM.get(ic - 2), ev_attn_all if ic < 2 else None,
                      dbg_gate[0] if ic == 0 else None),
            inc="dw", n=16)
        evUW[ic] = P.emit("gpsimd", lambda e, ic=ic: e.dma_start(
            out=uw_v[ic % 2], in_=uw_h[ic]),
            wait=emax(evUMM.get(ic - 2)), inc="dw", n=16)
        for it2 in range(2):
            itile = 2 * ic + it2
            for tb in range(2):
                pi = (2 * it2 + tb) % 2
                pg = psF[pi]
                pu = psF[2 + pi]
                last = None
                for j in range(HT):
                    lhsT = av_(GW_O + (ic % 2) * 4096 + j * 256 + it2 * 128, 128)
                    last = P.emit("tensor", lambda e, l=lhsT, j=j, p=pg, tb=tb: e.matmul(
                        p[:], l, av_(XNT_OWN + j * 1024 + tb * 512, 512),
                        start=(j == 0), stop=(j == HT - 1)),
                        wait=emax(evGW[ic], gfree[pi],
                                  ev_attn_all if ic == 0 else None),
                        inc="pe" if j == HT - 1 else None)
                last_gmm = evGMM[ic] = last
                last = None
                for j in range(HT):
                    lhsT = av_(UW_O + (ic % 2) * 4096 + j * 256 + it2 * 128, 128)
                    last = P.emit("tensor", lambda e, l=lhsT, j=j, p=pu, tb=tb: e.matmul(
                        p[:], l, av_(XNT_OWN + j * 1024 + tb * 512, 512),
                        start=(j == 0), stop=(j == HT - 1)),
                        wait=emax(evUW[ic], ufree[pi]),
                        inc="pe" if j == HT - 1 else None)
                last_umm = evUMM[ic] = last
                esil = P.emit(
                    "scalar", lambda e, p=pg, tb=tb, itile=itile: e.activation(
                        out=gst[:, tb * 512:(tb + 1) * 512], in_=p[:], func=AF.Silu,
                        scale=gs_sb[:, itile:itile + 1]),
                    wait=emax(last_gmm, evHM.get((itile - 1, tb))),
                    inc="se")
                gfree[pi] = esil
                eue = P.emit(
                    "vector", lambda e, p=pu, tb=tb, itile=itile: e.tensor_scalar_mul(
                        ust[:, tb * 512:(tb + 1) * 512], p[:],
                        us_sb[:, itile:itile + 1]),
                    wait=emax(last_umm, evHM.get((itile - 1, tb))),
                    inc="ve")
                ufree[pi] = eue
                last_hm = evHM[(itile, tb)] = P.emit(
                    "vector", lambda e, tb=tb, itile=itile: e.tensor_mul(
                        av_(H_OFF + itile * 1024 + tb * 512, 512),
                        gst[:, tb * 512:(tb + 1) * 512],
                        ust[:, tb * 512:(tb + 1) * 512]),
                    wait=emax(esil, eue), inc="ve")

    ev_ffn_mm_done = emax(last_gmm, last_umm)

    dump_now("hdbg", emax(last_hm))

    xq_v = [avf(XQ_O + a * 4096, 2048) for a in range(4)]
    ost_v = [avf(OST_O + i * 1024, 512) for i in range(2)]

    evXQ = {}
    evWXMM = {}
    evDNMM = {}
    evOST = {}
    evODMA = {}
    evE1 = {}
    ev_out = None
    for tb in range(2):
        for a in range(4):
            trow = (28 + tb * 4 + a) * 128
            evXQ[(tb, a)] = P.emit("sync", lambda e, a=a, trow=trow: e.dma_start(
                out=xq_v[a], in_=xkv[trow:trow + 128, :]),
                wait=emax(ev_ffn_mm_done, dbg_gate[0],
                          evWXMM.get((tb - 1, 15)) if tb >= 1 else None),
                inc="dx", n=16)
        for H in range(16):
            p2 = H % 2
            evDW = P.emit("gpsimd", lambda e, H=H, p2=p2: e.dma_start(
                out=av_(DW_O + p2 * 5632, 5632), in_=dw_h[H]),
                wait=emax(evE1.get((tb, H - 2)), ev_ffn_mm_done,
                          evDNMM.get((tb - 1, 14 + H)) if tb >= 1 else None),
                inc="dw", n=16)
            evWO = P.emit("gpsimd", lambda e, H=H, p2=p2: e.dma_start(
                out=av_(WO_O + p2 * 2048, 2048), in_=wo_h[H]),
                wait=emax(evWXMM.get((tb, H - 2)),
                          evWXMM.get((tb - 1, 14 + H)) if tb >= 1 else None),
                inc="dw", n=16)
            pw = psF[p2]
            pd = psF[2 + p2]
            last = None
            for j in range(HT):
                lhsT = av_(WO_O + p2 * 2048 + j * 128, 128)
                last = P.emit("tensor", lambda e, l=lhsT, j=j, p=pw, tb=tb: e.matmul(
                    p[:], l, av_(ATTN_OT + j * 1024 + tb * 512, 512),
                    start=(j == 0), stop=False),
                    wait=emax(evWO, ev_attn_all if (tb == 0 and H == 0) else None,
                              evOST.get((tb, H - 2)),
                              evOST.get((tb - 1, 14 + H)) if tb >= 1 else None),
                    inc=None)
            for a in range(4):
                last = P.emit("tensor", lambda e, a=a, H=H, p=pw: e.matmul(
                    p[:, a * 128:(a + 1) * 128],
                    xq_v[a][:, H * 128:(H + 1) * 128], idf_sb[:],
                    is_transpose=True, start=False, stop=(a == 3)),
                    wait=emax(evXQ[(tb, a)]), inc="pe" if a == 3 else None)
            evWXMM[(tb, H)] = last
            last = None
            for r in range(ITL):
                lhsT = av_(DW_O + p2 * 5632 + r * 128, 128)
                last = P.emit("tensor", lambda e, l=lhsT, r=r, p=pd, tb=tb: e.matmul(
                    p[:], l, av_(H_OFF + r * 1024 + tb * 512, 512),
                    start=(r == 0), stop=(r == ITL - 1)),
                    wait=emax(evDW, last_hm if (tb == 0 and H == 0) else None,
                              evE1.get((tb, H - 2)),
                              evE1.get((tb - 1, 14 + H)) if tb >= 1 else None),
                    inc="pe" if r == ITL - 1 else None)
            evDNMM[(tb, H)] = last
            e1 = P.emit("vector", lambda e, p=pd, p2=p2, H=H: e.tensor_scalar_mul(
                tmp4_sb[:, p2 * 512:(p2 + 1) * 512], p[:], ds_sb[:, H:H + 1]),
                wait=emax(evDNMM[(tb, H)]), inc="ve")
            evE1[(tb, H)] = e1
            evOST[(tb, H)] = P.emit("vector", lambda e, p=pw, p2=p2: e.tensor_add(
                ost_v[p2][:], p[:], tmp4_sb[:, p2 * 512:(p2 + 1) * 512]),
                wait=emax(evWXMM[(tb, H)], e1,
                          evODMA.get((tb, H - 2)),
                          evODMA.get((tb - 1, 14 + H)) if tb >= 1 else None),
                inc="ve")
            evODMA[(tb, H)] = P.emit("sync", lambda e, p2=p2, H=H, tb=tb: e.dma_start(
                out=out_t[H * 128:(H + 1) * 128, tb * 512:(tb + 1) * 512],
                in_=ost_v[p2][:]),
                wait=emax(evOST[(tb, H)]), inc="dout", n=16)

    P.emit("sync", None, wait=[("dout", P.cnt["dout"])])

    with nc.Block() as block:

        @block.sync
        def _(e):
            P.replay("sync", e, semh)

        @block.gpsimd
        def _(e):
            P.replay("gpsimd", e, semh)

        @block.tensor
        def _(e):
            P.replay("tensor", e, semh)

        @block.vector
        def _(e):
            P.replay("vector", e, semh)

        @block.scalar
        def _(e):
            P.replay("scalar", e, semh)

    return nc, ctx


def _tile_w(w, rows_per):
    R = w.shape[0]
    return np.ascontiguousarray(
        w.T.reshape(HT, 128, R).transpose(1, 0, 2).reshape(128, HT * R))


def _prep_shared(w_qkv, w_o, gate_w, up_w, down_w, gate_s, up_s, down_s):
    d = {}
    d["wkv_h"] = _tile_w(w_qkv[2048:2560, :], None)
    d["wq_h"] = np.stack([_tile_w(w_qkv[oc * 256:(oc + 1) * 256, :], None)
                          for oc in range(8)])
    d["wo_h"] = np.stack([
        np.ascontiguousarray(w_o[H * 128:(H + 1) * 128, :].T
                             .reshape(HT, 128, 128).transpose(1, 0, 2)
                             .reshape(128, 2048))
        for H in range(16)])
    d["gw_h"] = np.stack([
        np.ascontiguousarray(gate_w[ic * 256:(ic + 1) * 256, :].T
                             .reshape(HT, 128, 256).transpose(1, 0, 2)
                             .reshape(128, 4096))
        for ic in range(22)])
    d["uw_h"] = np.stack([
        np.ascontiguousarray(up_w[ic * 256:(ic + 1) * 256, :].T
                             .reshape(HT, 128, 256).transpose(1, 0, 2)
                             .reshape(128, 4096))
        for ic in range(22)])
    d["dw_h"] = np.stack([
        np.ascontiguousarray(down_w[H * 128:(H + 1) * 128, :].T
                             .reshape(ITL, 128, 128).transpose(1, 0, 2)
                             .reshape(128, 5632))
        for H in range(16)])
    d["gs_h"] = np.ascontiguousarray(gate_s.reshape(ITL, 128).T)
    d["us_h"] = np.ascontiguousarray(up_s.reshape(ITL, 128).T)
    d["ds_h"] = np.ascontiguousarray(down_s.reshape(16, 128).T)
    d["idb_h"] = np.eye(128, dtype=np.float32)
    d["idf_h"] = np.eye(128, dtype=np.float32)
    d["pones_h"] = np.ones((1, 128), dtype=np.float32)
    d["mones_h"] = -np.ones((1, 128), dtype=np.float32)
    mk = np.zeros((128, 4 * 512), dtype=np.float32)
    i = np.arange(512)
    j = np.arange(128)
    for m in range(4):
        mk[:, m * 512:(m + 1) * 512] = (128 * m + j[:, None] <= i[None, :])
    d["mask_h"] = mk
    return d


def _rope_tab(pos):
    inv = 1.0 / (10000.0 ** (np.arange(0, DH, 2, dtype=np.float64) / DH))
    ang = pos[:, None].astype(np.float64) * inv[None, :]
    ntile = len(pos) // 128
    tab = np.empty((128, ntile * 64), dtype=np.float32)
    c = np.cos(ang).astype(np.float32)
    s = np.sin(ang).astype(np.float32)
    for t in range(ntile):
        tab[:, t * 64:t * 64 + 32] = c[t * 128:(t + 1) * 128]
        tab[:, t * 64 + 32:t * 64 + 64] = s[t * 128:(t + 1) * 128]
    return tab


def _core_geometry(c):
    chunks = [(0, j) for j in range(c)] + [(1, j) for j in range(7 - c)] + \
             [(0, c), (1, 7 - c)]
    sel = np.zeros((1, 8), dtype=np.int32)
    for k in range(7):
        sel[0, k] = 0 if k < c else 512
    sel[0, 7] = c
    return chunks, sel


def kernel(L0_attn_hidden_states, L0_attn_w_qkv, L0_attn_w_o,
           L0_ffn_gate_w, L0_ffn_up_w, L0_ffn_down_w,
           L0_ffn_gate_s, L0_ffn_up_s, L0_ffn_down_s):
    x = np.ascontiguousarray(np.asarray(L0_attn_hidden_states, dtype=np.float32)
                             .reshape(2 * SEQ, HID))
    shared = _prep_shared(np.asarray(L0_attn_w_qkv, np.float32),
                          np.asarray(L0_attn_w_o, np.float32),
                          np.asarray(L0_ffn_gate_w, np.float32),
                          np.asarray(L0_ffn_up_w, np.float32),
                          np.asarray(L0_ffn_down_w, np.float32),
                          np.asarray(L0_ffn_gate_s, np.float32),
                          np.asarray(L0_ffn_up_s, np.float32),
                          np.asarray(L0_ffn_down_s, np.float32))

    in_maps = []
    for c in range(NCORE):
        chunks, sel = _core_geometry(c)
        rows = []
        pos = []
        for bt, j in chunks:
            rows.append(x[bt * SEQ + j * 512: bt * SEQ + (j + 1) * 512])
            pos.append(np.arange(j * 512, (j + 1) * 512))
        xkv = np.ascontiguousarray(np.concatenate(rows, 0))
        ctab = _rope_tab(np.concatenate(pos))
        m = dict(shared)
        m["xkv"] = xkv
        m["ctok_h"] = ctab
        m["sel_h"] = sel
        in_maps.append(m)

    nc, ctx = build_program()
    try:
        res = run_bass_kernel_spmd(nc, in_maps, core_ids=list(range(NCORE)))
    finally:
        ctx.close()

    out = np.empty((2 * SEQ, HID), dtype=np.float32)
    for c in range(NCORE):
        ot = np.asarray(res.results[c]["out_t"])
        out[c * 512:(c + 1) * 512] = ot[:, 0:512].T
        out[SEQ + (7 - c) * 512: SEQ + (8 - c) * 512] = ot[:, 512:1024].T
    return out.reshape(2, SEQ, HID)


# version 12
# speedup vs baseline: 1.2721x; 1.0461x over previous
import contextlib

import numpy as np
import ml_dtypes

import concourse.bass as bass
import concourse.mybir as mybir
from concourse.bass_utils import run_bass_kernel_spmd

F32 = mybir.dt.float32
BF16 = mybir.dt.bfloat16
I32 = mybir.dt.int32
AF = mybir.ActivationFunctionType
ALU = mybir.AluOpType
AX = mybir.AxisListType
BF = ml_dtypes.bfloat16

NCORE = 8
SEQ = 4096
HID = 2048
HQ = 32
DH = 64
INTER = 5632
NTILE = 36
KVROWS = NTILE * 128
TOWN = 1024
HT = 16
ITL = 44
EPS = 1e-5

ENG = ("sync", "gpsimd", "tensor", "vector", "scalar")

ACOLS = 101888
XNT_OWN = 0
ATTN_OT = 16384
H_OFF = 32768
KTA_O = 32768
KTB_O = 37376
VS_O = 41984
CTOK_O = 51344
MASK_O = 53648
WKV_O = 55696
XN_O = 63888
XNTT_O = 72080
KTOK_O = 76176
XBUF_O = 76688
QTOK_O = 16384
QT_O = 72080
WQC_O = 88464
RTMPA_O = 98304
RTMPB_O = 99328
QSQT_O = 90512
ATTA_O = 88464
ATTB_O = 90000
RBFF_O = 91536
RBF2_O = 93584
RBSB_O = 94608
ATMP_O = 96656
ZERO_O = 97680
KT2A_O = 55696
KT2B_O = 60304
GW_O = 77824
UW_O = 86016
GST_O = 94208
UST_O = 95232
XQ_O = 0
DW_O = 77824
WO_O = 89088
OST_O = 93184


class Prog:
    def __init__(self):
        self.ops = {e: [] for e in ENG}
        self.cnt = {}

    def sem(self, name):
        self.cnt.setdefault(name, 0)

    def emit(self, eng, fn, wait=(), inc=None, n=1):
        wait = tuple((s, v) for (s, v) in wait if s is not None and v > 0)
        ev = None
        if inc is not None:
            self.cnt[inc] += n
            ev = (inc, self.cnt[inc])
        self.ops[eng].append((fn, wait, inc, n))
        return ev

    def replay(self, eng, e, sems):
        last = {s: 0 for s in self.cnt}
        for fn, wait, inc, n in self.ops[eng]:
            for s, v in wait:
                if last[s] < v:
                    e.wait_ge(sems[s], v)
                    last[s] = v
            ins = fn(e) if fn is not None else None
            if inc is not None and ins is not None:
                ins.then_inc(sems[inc], n)


def emax(*evs):
    best = {}
    for ev in evs:
        if not ev:
            continue
        if isinstance(ev, list):
            for s, v in ev:
                if best.get(s, 0) < v:
                    best[s] = v
        else:
            s, v = ev
            if best.get(s, 0) < v:
                best[s] = v
    return list(best.items())


def build_program(dbg=()):
    nc = bass.Bass("TRN2", target_bir_lowering=False)

    xkv = nc.dram_tensor("xkv", [KVROWS, HID], F32, kind="ExternalInput").ap()
    wkv_h = nc.dram_tensor("wkv_h", [128, 8192], F32, kind="ExternalInput").ap()
    wq_h = nc.dram_tensor("wq_h", [8, 128, 4096], F32, kind="ExternalInput").ap()
    wo_h = nc.dram_tensor("wo_h", [16, 128, 2048], F32, kind="ExternalInput").ap()
    gw_h = nc.dram_tensor("gw_h", [22, 128, 4096], F32, kind="ExternalInput").ap()
    uw_h = nc.dram_tensor("uw_h", [22, 128, 4096], F32, kind="ExternalInput").ap()
    dw_h = nc.dram_tensor("dw_h", [16, 128, 5632], F32, kind="ExternalInput").ap()
    gs_h = nc.dram_tensor("gs_h", [128, 44], F32, kind="ExternalInput").ap()
    us_h = nc.dram_tensor("us_h", [128, 44], F32, kind="ExternalInput").ap()
    ds_h = nc.dram_tensor("ds_h", [128, 16], F32, kind="ExternalInput").ap()
    ctok_h = nc.dram_tensor("ctok_h", [128, NTILE * 64], F32, kind="ExternalInput").ap()
    mask_h = nc.dram_tensor("mask_h", [128, 2048], F32, kind="ExternalInput").ap()
    sel_h = nc.dram_tensor("sel_h", [1, 8], I32, kind="ExternalInput").ap()
    idb_h = nc.dram_tensor("idb_h", [128, 128], F32, kind="ExternalInput").ap()
    idf_h = nc.dram_tensor("idf_h", [128, 128], F32, kind="ExternalInput").ap()
    pones_h = nc.dram_tensor("pones_h", [1, 128], F32, kind="ExternalInput").ap()
    mones_h = nc.dram_tensor("mones_h", [1, 128], F32, kind="ExternalInput").ap()
    out_t = nc.dram_tensor("out_t", [HID, TOWN], F32, kind="ExternalOutput").ap()

    dbg_t = {}
    dbg_specs = {
        "xnT_own": ([128, 16384], BF16), "ktA": ([128, 4608], BF16),
        "ktB": ([128, 4608], BF16), "vS": ([128, 36 * 260], BF16),
        "mk2row": ([1, 4], F32), "qT": ([128, 16384], BF16),
        "bias_all": ([128, 32], F32), "attn_oT": ([128, 16384], BF16),
        "hdbg": ([128, ITL * 1024], BF16), "qtok": ([128, 16384], BF16),
    }
    for name in dbg:
        shp, dt = dbg_specs[name]
        dbg_t[name] = nc.dram_tensor("dbg_" + name, shp, dt, kind="ExternalOutput").ap()

    P = Prog()
    semh = {}
    ctx = contextlib.ExitStack()
    for s in ("dx", "dw", "dout", "pe", "ve", "se"):
        P.sem(s)
        semh[s] = ctx.enter_context(nc.semaphore(s))

    def sb(name, shape, dt):
        return ctx.enter_context(nc.sbuf_tensor(name, shape, dt))

    arena = sb("arena", [128, ACOLS], BF16)
    bn_sb = sb("bn_sb", [128, 96], F32)
    agg_sb = sb("agg_sb", [128, 8], F32)
    varp_sb = sb("varp_sb", [128, 4], F32)
    negmr_sb = sb("negmr_sb", [128, 4], F32)
    rstd_sb = sb("rstd_sb", [128, 4], F32)
    ksqt_sb = sb("ksqt_sb", [128, 256], BF16)
    kn4_sb = sb("kn4_sb", [128, 4], F32)
    knmax_sb = sb("knmax_sb", [128, 4], F32)
    knm4_sb = sb("knm4_sb", [4, 1], F32)
    mk2_sb = sb("mk2_sb", [1, 4], F32)
    qn_sb = sb("qn_sb", [128, 32], F32)
    qnmax_sb = sb("qnmax_sb", [128, 32], F32)
    qmx32_sb = sb("qmx32_sb", [32, 1], F32)
    qmaxr_sb = sb("qmaxr_sb", [1, 32], F32)
    prodr_sb = sb("prodr_sb", [1, 32], F32)
    sqrow_sb = sb("sqrow_sb", [1, 32], BF16)
    biasa_sb = sb("biasa_sb", [128, 32], F32)
    biasp_sb = sb("biasp_sb", [128, 16], F32)
    recip8_sb = sb("recip8_sb", [128, 8], F32)
    gs_sb = sb("gs_sb", [128, 44], F32)
    us_sb = sb("us_sb", [128, 44], F32)
    ds_sb = sb("ds_sb", [128, 16], F32)
    sel_sb = sb("sel_sb", [1, 8], I32)
    idb_sb = sb("idb_sb", [128, 128], BF16)
    idf_sb = sb("idf_sb", [128, 128], F32)
    pones_sb = sb("pones_sb", [1, 128], BF16)
    mones_sb = sb("mones_sb", [1, 128], BF16)
    tmp4_sb = sb("tmp4_sb", [128, 1024], F32)

    psT0 = ctx.enter_context(nc.psum_tensor("psT0", [128, 1024], BF16))
    psT1 = ctx.enter_context(nc.psum_tensor("psT1", [128, 1024], BF16))
    psW = [ctx.enter_context(nc.psum_tensor(f"psW{i}", [128, 1024], F32))
           for i in range(2)]
    psF = [psW[0][:][:, 0:512], psW[0][:][:, 512:1024],
           psW[1][:][:, 0:512], psW[1][:][:, 512:1024]]
    psAV = ctx.enter_context(nc.psum_tensor("psAV", [128, 1024], F32))
    psAVb = psAV[:].bitcast(BF16)

    def av_(c0, n, r0=0, r1=128):
        return arena[r0:r1, c0:c0 + n]

    def avf(c0, nf32, r0=0, r1=128):
        return arena[r0:r1, c0:c0 + 2 * nf32].bitcast(F32)

    probe_ap = arena[64:128, 5:517]
    assert probe_ap.offset == 64 * ACOLS + 5, probe_ap.offset

    xnt_own = av_(XNT_OWN, 16384)
    attn_ot = av_(ATTN_OT, 16384)
    h_sb = av_(H_OFF, ITL * 1024)
    ktA = av_(KTA_O, 4608)
    ktB = av_(KTB_O, 4608)
    vS = av_(VS_O, 36 * 260)
    ctok = av_(CTOK_O, NTILE * 64)
    masks = av_(MASK_O, 2048)
    wkv_sb = av_(WKV_O, 8192)
    xn_v = [av_(XN_O + i * 2048, 2048) for i in range(4)]
    ktok_v = [av_(KTOK_O + i * 256, 256) for i in range(2)]
    xbuf_v = [avf(XBUF_O + i * 4096, 2048) for i in range(4)]

    ev_wkv = P.emit("gpsimd", lambda e: e.dma_start(out=wkv_sb, in_=wkv_h),
                    inc="dw", n=16)
    evc = None
    for dst, srca in ((ctok, ctok_h), (masks, mask_h), (idb_sb[:], idb_h),
                      (pones_sb[:], pones_h), (mones_sb[:], mones_h)):
        evc = P.emit("gpsimd", lambda e, d=dst, s=srca: e.dma_start(out=d, in_=s),
                     inc="dw", n=16)
    for dst, srca in ((sel_sb[:], sel_h), (idf_sb[:], idf_h),
                      (gs_sb[:], gs_h), (us_sb[:], us_h), (ds_sb[:], ds_h)):
        evc2 = P.emit("sync", lambda e, d=dst, s=srca: e.dma_start(out=d, in_=s),
                      inc="dx", n=16)
    ev_consts = emax(evc, evc2)

    ev_knz = P.emit("vector", lambda e: e.memset(knmax_sb[:], 0.0), inc="ve")
    ev_qnz = P.emit("vector", lambda e: e.memset(qnmax_sb[:], 0.0), inc="ve")
    ones_ap = bass.AP(arena, VS_O + 64, [[ACOLS, 128], [260, 36], [65, 4], [1, 1]])
    ev_vones = P.emit("vector", lambda e, a=ones_ap: e.memset(a, 1.0), inc="ve")

    evXL = [None] * NTILE
    evXN = [None] * NTILE
    evS3 = [None] * NTILE
    evS4 = [None] * NTILE
    evS5 = [None] * NTILE
    evKR = [None] * NTILE
    evKN = [None] * NTILE
    evS7 = [None] * NTILE
    evS8 = [None] * NTILE
    evVE = [None] * NTILE
    evRC = [None] * NTILE
    evSQT = [None] * NTILE
    evKSQ = [None] * NTILE

    def xnt_src_slice(t, j):
        s = t // 4
        if s >= 7:
            return av_(XNT_OWN + j * 1024 + (s - 7) * 512 + (t % 4) * 128, 128)
        return av_(XNTT_O + (t % 2) * 2048 + j * 128, 128)

    lastT_user = [-1]
    lastA_user = [-1]
    for step in range(NTILE + 3):
        t = step
        if 1 <= step and (step - 1) < NTILE:
            t1 = step - 1
            b1 = t1 % 2
            last = None
            for j in range(HT):
                lhsT = xnt_src_slice(t1, j)
                last = P.emit("tensor", lambda e, l=lhsT, j=j, b1=b1: e.matmul(
                    psF[b1], l, wkv_sb[:, j * 512:(j + 1) * 512],
                    start=(j == 0), stop=(j == HT - 1)),
                    wait=emax(evS4[t1], ev_wkv,
                              evKR[t1 - 2] if t1 >= 2 else None,
                              evVE[t1 - 2] if t1 >= 2 else None),
                    inc="pe" if j == HT - 1 else None)
            evS5[t1] = last
        if 2 <= step and (step - 2) < NTILE:
            t2 = step - 2
            b2 = t2 % 2
            psF2b = psF[2].bitcast(BF16)
            last = None
            for blk in range(2):
                last = P.emit("tensor", lambda e, b2=b2, blk=blk, p=psF2b: e.transpose(
                    p[:, blk * 128:(blk + 1) * 128],
                    ktok_v[b2][:, blk * 128:(blk + 1) * 128], idb_sb[:]),
                    wait=emax(evKR[t2], evS8[t2 - 1] if t2 >= 1 else None),
                    inc="pe" if blk == 1 else None)
            evS7[t2] = last
        if t < NTILE:
            b = t % 4
            b2p = t % 2
            src_x = xkv[t * 128:(t + 1) * 128, :]
            evXL[t] = P.emit(
                "sync", lambda e, s=src_x, d=xbuf_v[b]: e.dma_start(out=d, in_=s),
                wait=emax(evXN[t - 4] if t >= 4 else None), inc="dx", n=16)
        if 1 <= step and (step - 1) < NTILE:
            t1 = step - 1
            b1 = t1 % 2
            vdst = bass.AP(arena, VS_O + t1 * 260, [[ACOLS, 128], [65, 4], [1, 64]])
            vsrc = bass.AP(psW[0], b1 * 512 + 256, [[1024, 128], [64, 4], [1, 64]])
            evVE[t1] = P.emit("scalar", lambda e, d=vdst, s=vsrc: e.copy(d, s),
                              wait=emax(evS5[t1], ev_vones), inc="se")
        if 2 <= step and (step - 2) < NTILE:
            t2 = step - 2
            psF2b = psF[2].bitcast(BF16)
            P.emit("scalar", lambda e, t2=t2, p=psF2b: e.copy(
                av_(KTA_O + t2 * 128, 128), p[:, 0:128]), wait=emax(evS7[t2]))
            evS8[t2] = P.emit("scalar", lambda e, t2=t2, p=psF2b: e.copy(
                av_(KTB_O + t2 * 128, 128), p[:, 128:256]), inc="se")
            evKSQ[t2] = P.emit("scalar", lambda e, b2=t2 % 2: e.square(
                ksqt_sb[:], ktok_v[b2]),
                wait=emax(evKR[t2], evKN[t2 - 1] if t2 >= 1 else None), inc="se")
        if t < NTILE:
            b = t % 4
            e1 = None
            for i in range(4):
                e1 = P.emit("vector", lambda e, i=i, b=b: e.bn_stats(
                    bn_sb[:, b * 24 + i * 6:b * 24 + (i + 1) * 6],
                    xbuf_v[b][:, i * 512:(i + 1) * 512]),
                    wait=emax(evXL[t]), inc="ve")
            e2 = P.emit("vector", lambda e, b=b: e.bn_aggr(
                agg_sb[:, 2 * b:2 * b + 2], bn_sb[:, b * 24:b * 24 + 24]),
                wait=emax(e1), inc="ve")
            e3 = P.emit("vector", lambda e, b=b: e.tensor_scalar_add(
                varp_sb[:, b:b + 1], agg_sb[:, 2 * b + 1:2 * b + 2], EPS),
                wait=emax(e2), inc="ve")
            evRC[t] = P.emit("vector", lambda e, b=b: e.reciprocal(
                varp_sb[:, b:b + 1], varp_sb[:, b:b + 1]), wait=emax(e3), inc="ve")
            evSQT[t] = P.emit("scalar", lambda e, b=b: e.sqrt(
                rstd_sb[:, b:b + 1], varp_sb[:, b:b + 1]),
                wait=emax(evRC[t]), inc="se")
        if 1 <= step and (step - 1) < NTILE:
            t1 = step - 1
            b1 = t1 % 2
            x1 = bass.AP(psW[0], b1 * 512, [[1024, 128], [64, 4], [1, 32]])
            x2 = bass.AP(psW[0], b1 * 512 + 32, [[1024, 128], [64, 4], [1, 32]])
            cosb = bass.AP(arena, CTOK_O + t1 * 64, [[ACOLS, 128], [0, 4], [1, 32]])
            sinb = bass.AP(arena, CTOK_O + t1 * 64 + 32,
                           [[ACOLS, 128], [0, 4], [1, 32]])
            k1 = bass.AP(arena, KTOK_O + b1 * 256, [[ACOLS, 128], [64, 4], [1, 32]])
            k2 = bass.AP(arena, KTOK_O + b1 * 256 + 32,
                         [[ACOLS, 128], [64, 4], [1, 32]])
            rA = bass.AP(arena, RTMPA_O, [[ACOLS, 128], [32, 4], [1, 32]])
            rB = bass.AP(arena, RTMPB_O, [[ACOLS, 128], [32, 4], [1, 32]])
            wr = emax(evS5[t1], evVE[t1], ev_consts,
                      evS7[t1 - 2] if t1 >= 2 else None)
            P.emit("vector", lambda e, o=rA, a=x2, c=sinb: e.tensor_mul(o, a, c),
                   wait=wr)
            P.emit("vector", lambda e, o=rB, a=x1, c=sinb: e.tensor_mul(o, a, c))
            P.emit("vector", lambda e, o=k1, a=x1, c=cosb: e.tensor_mul(o, a, c))
            P.emit("vector", lambda e, o=k1, a=rA: e.tensor_sub(o, o, a))
            P.emit("vector", lambda e, o=k2, a=x2, c=cosb: e.tensor_mul(o, a, c))
            evKR[t1] = P.emit("vector", lambda e, o=k2, a=rB: e.tensor_add(o, o, a),
                              inc="ve")
        if 2 <= step and (step - 2) < NTILE:
            t2 = step - 2
            er = None
            for r in range(4):
                er = P.emit("vector", lambda e, r=r: e.reduce_sum(
                    kn4_sb[:, r:r + 1], ksqt_sb[:, r * 64:(r + 1) * 64], axis=AX.X),
                    wait=emax(evKSQ[t2]), inc="ve")
            evKN[t2] = P.emit("vector", lambda e: e.tensor_max(
                knmax_sb[:], knmax_sb[:], kn4_sb[:]), wait=emax(er, ev_knz), inc="ve")
        if t < NTILE:
            b = t % 4
            evXN[t] = P.emit("vector", lambda e, b=b: e.tensor_scalar(
                out=xn_v[b], in0=xbuf_v[b],
                scalar1=agg_sb[:, 2 * b:2 * b + 1], scalar2=rstd_sb[:, b:b + 1],
                op0=ALU.subtract, op1=ALU.mult),
                wait=emax(evSQT[t], evS3[t - 4] if t >= 4 else None), inc="ve")

        if t < NTILE:
            b = t % 4
            b2p = t % 2
            s = t // 4
            own = s >= 7
            use_psT = own or (t % 2 == 0)
            tp_wait = emax(evXN[t],
                           evS4[lastT_user[0]] if (use_psT and lastT_user[0] >= 0)
                           else None,
                           evS4[lastA_user[0]] if ((not use_psT) and
                                                   lastA_user[0] >= 0) else None,
                           ev_consts if t == 0 else None)
            last = None
            for j in range(16):
                if use_psT:
                    ps = psT0[:] if j < 8 else psT1[:]
                    jj = j % 8
                else:
                    ps = psAVb
                    jj = j
                last = P.emit("tensor", lambda e, b=b, j=j, ps=ps, jj=jj: e.transpose(
                    ps[:, jj * 128:(jj + 1) * 128],
                    xn_v[b][:, j * 128:(j + 1) * 128], idb_sb[:]),
                    wait=tp_wait if j == 0 else (),
                    inc="pe" if j == 15 else None)
            evS3[t] = last
            if use_psT:
                lastT_user[0] = t
            else:
                lastA_user[0] = t
        if t < NTILE:
            b2p = t % 2
            s = t // 4
            own = s >= 7
            use_psT = own or (t % 2 == 0)
            if own:
                off = XNT_OWN + (s - 7) * 512 + (t % 4) * 128
                d0 = bass.AP(arena, off, [[ACOLS, 128], [1024, 8], [1, 128]])
                d1 = bass.AP(arena, off + 8 * 1024, [[ACOLS, 128], [1024, 8], [1, 128]])
                s0 = bass.AP(psT0, 0, [[1024, 128], [128, 8], [1, 128]])
                s1 = bass.AP(psT1, 0, [[1024, 128], [128, 8], [1, 128]])
            elif use_psT:
                d0 = av_(XNTT_O + b2p * 2048, 1024)
                d1 = av_(XNTT_O + b2p * 2048 + 1024, 1024)
                s0, s1 = psT0[:], psT1[:]
            else:
                d0 = av_(XNTT_O + b2p * 2048, 1024)
                d1 = av_(XNTT_O + b2p * 2048 + 1024, 1024)
                s0, s1 = psAVb[:, 0:1024], psAVb[:, 1024:2048]
            w4 = emax(evS3[t], evS5[t - 2] if (t >= 2 and (t - 2) // 4 < 7) else None)
            P.emit("scalar", lambda e, d=d0, s_=s0: e.copy(d, s_), wait=w4)
            evS4[t] = P.emit("scalar", lambda e, d=d1, s_=s1: e.copy(d, s_),
                             inc="se")


    eT1 = P.emit("tensor", lambda e: e.transpose(
        psF[0][0:4, 0:128], knmax_sb[:], idf_sb[:]),
        wait=emax(evKN[NTILE - 1], evVE[NTILE - 1], evKR[NTILE - 1]), inc="pe")
    eT2 = P.emit("vector", lambda e: e.reduce_max(
        knm4_sb[:], psF[0][0:4, 0:128], axis=AX.X), wait=emax(eT1), inc="ve")
    eT3 = P.emit("tensor", lambda e: e.transpose(
        psF[0][0:1, 0:4], knm4_sb[:], idf_sb[0:4, 0:4]), wait=emax(eT2), inc="pe")
    evMK2 = P.emit("vector", lambda e: e.tensor_copy(mk2_sb[:], psF[0][0:1, 0:4]),
                   wait=emax(eT3), inc="ve")

    dbg_gate = [None]

    def dump_now(name, w):
        if name not in dbg:
            return
        ev = P.emit("sync", lambda e, d=dbg_t[name], s=dmap0[name]: e.dma_start(
            out=d, in_=s), wait=w, inc="dout", n=16)
        dbg_gate[0] = emax(dbg_gate[0], ev)

    dmap0 = {
        "xnT_own": av_(XNT_OWN, 16384), "ktA": ktA, "ktB": ktB, "vS": vS,
        "mk2row": mk2_sb[:], "qT": av_(QT_O, 16384),
        "bias_all": biasa_sb[:], "attn_oT": av_(ATTN_OT, 16384),
        "hdbg": h_sb, "qtok": av_(QTOK_O, 16384),
    }
    for nm in ("xnT_own", "ktA", "ktB", "vS", "mk2row"):
        dump_now(nm, emax(evMK2, evS8[NTILE - 1], evVE[NTILE - 1], evS4[NTILE - 1],
                          evKR[NTILE - 1]))

    qtok_v = [av_(QTOK_O + tt * 2048, 2048) for tt in range(8)]
    wqc_v = [av_(WQC_O + i * 4096, 4096) for i in range(2)]
    attAB_v = [av_(ATTA_O + i * 1024, 1024) for i in range(3)]
    psT0f = psT0[:].bitcast(F32)
    psT1f = psT1[:].bitcast(F32)
    zero_v = av_(ZERO_O, 512)
    rbff_row = avf(RBFF_O, 1024, 0, 1)
    rbf2_row = arena[0:1, RBF2_O:RBF2_O + 1024]
    rbsb = avf(RBSB_O, 1024, 0, 64)
    atmp = av_(ATMP_O, 1024, 0, 64)

    ev_zero = P.emit("vector", lambda e: e.memset(zero_v, 0.0), inc="ve")

    REGS = {}

    def reg_setup(e):
        st = contextlib.ExitStack()
        REGS["stack"] = st
        REGS["cid"] = st.enter_context(e.register("cid"))
        e.reg_load(REGS["cid"], sel_sb[0:1, 7:8])
        return None

    P.emit("tensor", reg_setup, wait=emax(ev_consts))

    evQMM = {}
    evQE = {}
    for oc in range(8):
        evWQ = P.emit("gpsimd", lambda e, oc=oc: e.dma_start(
            out=wqc_v[oc % 2], in_=wq_h[oc]),
            wait=emax(evQMM.get((oc - 2, 7)),
                      emax(evKR[NTILE - 1], evVE[NTILE - 1])
                      if oc < 2 else None),
            inc="dw", n=16)
        for tt in range(8):
            last = None
            for j in range(HT):
                lhsT = av_(XNT_OWN + j * 1024 + tt * 128, 128)
                last = P.emit("tensor", lambda e, l=lhsT, j=j, tt=tt, oc=oc: e.matmul(
                    psF[2 + tt % 2][:, 0:256], l,
                    wqc_v[oc % 2][:, j * 256:(j + 1) * 256],
                    start=(j == 0), stop=(j == HT - 1)),
                    wait=emax(evWQ, evS4[NTILE - 1], evS8[NTILE - 1],
                              evQE.get((oc, tt - 2)), evQE.get((oc - 1, tt + 6))),
                    inc="pe" if j == HT - 1 else None)
            evQMM[(oc, tt)] = last
            evQE[(oc, tt)] = P.emit("vector", lambda e, oc=oc, tt=tt: e.tensor_copy(
                av_(QTOK_O + tt * 2048 + oc * 256, 256), psF[2 + tt % 2][:, 0:256]),
                wait=emax(evQMM[(oc, tt)]), inc="ve")

    evQR = [None] * 8
    evQN = [None] * 8
    evQT = [None] * 8
    evQTE = [None] * 8
    for tt in range(8):
        base = QTOK_O + tt * 2048
        x1 = bass.AP(arena, base, [[ACOLS, 128], [64, 32], [1, 32]])
        x2 = bass.AP(arena, base + 32, [[ACOLS, 128], [64, 32], [1, 32]])
        cosb = bass.AP(arena, CTOK_O + (28 + tt) * 64, [[ACOLS, 128], [0, 32], [1, 32]])
        sinb = bass.AP(arena, CTOK_O + (28 + tt) * 64 + 32,
                       [[ACOLS, 128], [0, 32], [1, 32]])
        rA = bass.AP(arena, RTMPA_O, [[ACOLS, 128], [32, 32], [1, 32]])
        rB = bass.AP(arena, RTMPB_O, [[ACOLS, 128], [32, 32], [1, 32]])
        w0 = emax(evQE[(7, tt)], ev_consts, evQMM[(7, 7)])
        P.emit("vector", lambda e, o=rA, a=x2, c=sinb: e.tensor_mul(o, a, c), wait=w0)
        P.emit("vector", lambda e, o=rB, a=x1, c=sinb: e.tensor_mul(o, a, c))
        P.emit("vector", lambda e, o=x1, a=x1, c=cosb: e.tensor_mul(o, a, c))
        P.emit("vector", lambda e, o=x1, a=rA: e.tensor_sub(o, o, a))
        P.emit("vector", lambda e, o=x2, a=x2, c=cosb: e.tensor_mul(o, a, c))
        evQR[tt] = P.emit("vector", lambda e, o=x2, a=rB: e.tensor_add(o, o, a),
                          inc="ve")
        eqs = P.emit("scalar", lambda e, tt=tt: e.square(av_(QSQT_O, 2048), qtok_v[tt]),
                     wait=emax(evQR[tt], evQN[tt - 1] if tt >= 1 else None), inc="se")
        er = None
        for hh in range(HQ):
            er = P.emit("vector", lambda e, hh=hh: e.reduce_sum(
                qn_sb[:, hh:hh + 1], av_(QSQT_O + hh * 64, 64), axis=AX.X),
                wait=emax(eqs), inc="ve")
        evQN[tt] = P.emit("vector", lambda e: e.tensor_max(
            qnmax_sb[:], qnmax_sb[:], qn_sb[:]), wait=emax(er, ev_qnz), inc="ve")
        last = None
        for j in range(16):
            ps = psT0 if j < 8 else psT1
            jj = j % 8
            last = P.emit("tensor", lambda e, tt=tt, j=j, ps=ps, jj=jj: e.transpose(
                ps[:, jj * 128:(jj + 1) * 128],
                qtok_v[tt][:, j * 128:(j + 1) * 128], idb_sb[:]),
                wait=emax(evQR[tt], evQTE[tt - 1] if tt >= 1 else None,
                          evS4[NTILE - 1]),
                inc="pe" if j == 15 else None)
        evQT[tt] = last
        d0 = bass.AP(arena, QT_O + tt * 128, [[ACOLS, 128], [1024, 8], [1, 128]])
        d1 = bass.AP(arena, QT_O + 8 * 1024 + tt * 128,
                     [[ACOLS, 128], [1024, 8], [1, 128]])
        s0 = bass.AP(psT0, 0, [[1024, 128], [128, 8], [1, 128]])
        s1 = bass.AP(psT1, 0, [[1024, 128], [128, 8], [1, 128]])
        P.emit("scalar", lambda e, d=d0, s_=s0: e.copy(d, s_),
               wait=emax(evQT[tt]))
        evQTE[tt] = P.emit("scalar", lambda e, d=d1, s_=s1: e.copy(d, s_),
                           inc="se")

    eU1 = P.emit("tensor", lambda e: e.transpose(
        psF[0][0:32, 0:128], qnmax_sb[:], idf_sb[:]),
        wait=emax(evQN[7], evQMM[(7, 7)], evMK2), inc="pe")
    eU2 = P.emit("vector", lambda e: e.reduce_max(
        qmx32_sb[:], psF[0][0:32, 0:128], axis=AX.X), wait=emax(eU1), inc="ve")
    eU3 = P.emit("tensor", lambda e: e.transpose(
        psF[0][0:1, 0:32], qmx32_sb[:], idf_sb[0:32, 0:32]), wait=emax(eU2), inc="pe")
    eU4 = P.emit("vector", lambda e: e.tensor_copy(qmaxr_sb[:], psF[0][0:1, 0:32]),
                 wait=emax(eU3), inc="ve")
    mk2bc = bass.AP(mk2_sb, 0, [[4, 1], [1, 4], [0, 8]])
    pr3 = bass.AP(prodr_sb, 0, [[32, 1], [8, 4], [1, 8]])
    qm3 = bass.AP(qmaxr_sb, 0, [[32, 1], [8, 4], [1, 8]])
    eU5 = P.emit("vector", lambda e, o=pr3, a=qm3, m=mk2bc: e.tensor_mul(o, a, m),
                 wait=emax(eU4, evMK2), inc="ve")
    eU6 = P.emit("vector", lambda e: e.tensor_scalar_mul(
        prodr_sb[:], prodr_sb[:], 1.0 / 64.0), wait=emax(eU5), inc="ve")
    eU7 = P.emit("scalar", lambda e: e.sqrt(sqrow_sb[:], prodr_sb[:]),
                 wait=emax(eU6), inc="se")
    dump_now("qtok", emax(evQR[7], evQN[7], evQTE[7]))
    wk2 = emax(evQTE[7], evQN[7], evS8[NTILE - 1], dbg_gate[0])
    P.emit("scalar", lambda e: e.copy(arena[0:64, KT2A_O:KT2A_O + 4608],
                                      arena[64:128, KTA_O:KTA_O + 4608]), wait=wk2)
    P.emit("scalar", lambda e: e.copy(arena[64:128, KT2A_O:KT2A_O + 4608],
                                      arena[0:64, KTA_O:KTA_O + 4608]))
    P.emit("scalar", lambda e: e.copy(arena[0:64, KT2B_O:KT2B_O + 4608],
                                      arena[64:128, KTB_O:KTB_O + 4608]))
    ev_kt2 = P.emit("scalar", lambda e: e.copy(arena[64:128, KT2B_O:KT2B_O + 4608],
                                               arena[0:64, KTB_O:KTB_O + 4608]),
                    inc="se")

    eB1 = P.emit("tensor", lambda e: e.matmul(
        psF[2][:, 0:32], mones_sb[:], sqrow_sb[:], start=True, stop=True),
        wait=emax(eU7, evQE[(7, 7)]), inc="pe")
    evBIAS = P.emit("vector", lambda e: e.tensor_copy(biasa_sb[:], psF[2][:, 0:32]),
                    wait=emax(eB1), inc="ve")

    dump_now("qT", emax(evQTE[7]))
    dump_now("bias_all", emax(evBIAS))

    blocks = [(7, m) for m in range(4)] + [(8, m) for m in range(4)] + \
             [(k, m) for k in range(7) for m in range(4)]
    NB = len(blocks)
    pairs = []
    for gkt, alo, bhi in ((KTA_O, 0, 9), (KT2A_O, 8, 1),
                          (KTB_O, 16, 25), (KT2B_O, 24, 17)):
        for i in range(4):
            pairs.append((gkt, alo + 2 * i, bhi + 2 * i))

    evBP = None
    for pi2, (_, hA2, hB2) in enumerate(pairs):
        evBP = P.emit("vector", lambda e, p=pi2, a=hA2, b=hB2: e.tensor_tensor(
            biasp_sb[:, p:p + 1], biasa_sb[:, a:a + 1], biasa_sb[:, b:b + 1],
            ALU.min), wait=emax(evBIAS), inc="ve")

    pe_ops = [[] for _ in range(8)]

    def pe_emit(fns, wait=()):
        P.cnt["pe"] += 1
        val = P.cnt["pe"]
        w = tuple((s, v) for (s, v) in emax(*wait) if v > 0) if wait else ()
        if not isinstance(fns, list):
            fns = [fns] * 8
        for ci in range(8):
            pe_ops[ci].append((w, fns[ci]))
        return ("pe", val)

    prev_sc_free = [emax(eU2, eU4, evBIAS, eT2, evMK2, evBP)] * 2
    prev_av_free = emax(evS8[NTILE - 1], evMK2, evQTE[7],
                        evS4[NTILE - 2], evS4[NTILE - 1])
    prev_rb_free = emax(evBIAS)
    prev_avlast = None
    prev_atmp = None
    ev_attn_done_v = None
    ev_attn_done_s = None

    for pi, (kt_base, hA, hB) in enumerate(pairs):
        qcbA = QT_O + (hA // 2) * 1024
        qcbB = QT_O + (hB // 2) * 1024 + 64 * ACOLS
        kvA = hA // 8
        kvB = hB // 8

        evSC = [None] * NB
        evEX = [None] * NB
        evMK = [None] * NB
        evAVA = [None] * NB
        evAVB = [None] * NB

        def sc_fns(qcb, rowbase, slot, kb, psdst):
            lhs = arena[rowbase:rowbase + 64,
                        kt_base + kb * 128:kt_base + (kb + 1) * 128]
            if slot >= 7:
                r = bass.AP(arena, qcb + (512 if slot == 8 else 0),
                            [[ACOLS, 64], [1, 512]])
                return (lambda e, l=lhs, r=r, o=psdst:
                        e.matmul(o, l, r, start=True, stop=True))
            fns = []
            for ci in range(8):
                sel = 0 if slot < ci else 512
                r = bass.AP(arena, qcb + sel, [[ACOLS, 64], [1, 512]])
                fns.append(lambda e, l=lhs, r=r, o=psdst:
                           e.matmul(o, l, r, start=True, stop=True))
            return fns

        def emit_avs(bi):
            slot, m = blocks[bi]
            kb = slot * 4 + m
            st = (slot >= 7 and m == 0)
            vpA = vS[:, kb * 260 + kvA * 65:kb * 260 + (kvA + 1) * 65]
            vpB = vS[:, kb * 260 + kvB * 65:kb * 260 + (kvB + 1) * 65]
            aA = attAB_v[bi % 3][:, 0:512]
            aB = attAB_v[bi % 3][:, 512:1024]
            if slot >= 7:
                oA = psAV[0:65, 0:512] if slot == 7 else psAV[0:65, 512:1024]
                fA = (lambda e, v=vpA, o=oA, a=aA, st=st:
                      e.matmul(o, v, a, start=st, stop=False))
                oB = psT0f[0:65, :] if slot == 7 else psT1f[0:65, :]
                fB = (lambda e, v=vpB, o=oB, a=aB, st=st:
                      e.matmul(o, v, a, start=st, stop=False))
            else:
                fA = []
                fB = []
                for ci in range(8):
                    sel = 0 if slot < ci else 512
                    oA = psAV[0:65, sel:sel + 512]
                    fA.append(lambda e, v=vpA, o=oA, a=aA:
                              e.matmul(o, v, a, start=False, stop=False))
                    oB = psT0f[0:65, :] if sel == 0 else psT1f[0:65, :]
                    fB.append(lambda e, v=vpB, o=oB, a=aB:
                              e.matmul(o, v, a, start=False, stop=False))
            evAVA[bi] = pe_emit(fA, wait=[evMK[bi],
                                          prev_av_free if bi == 0 else None])
            evAVB[bi] = pe_emit(fB, wait=[evMK[bi],
                                          prev_av_free if bi == 0 else None])

        for bi, (slot, m) in enumerate(blocks):
            kb = slot * 4 + m
            ps = psW[bi % 2][:]
            wA = [evEX[bi - 2] if bi >= 2 else prev_sc_free[bi % 2],
                  emax(evQTE[7], evS8[NTILE - 1], ev_kt2)
                  if (pi == 0 and bi == 0) else None]
            evSC[bi] = pe_emit(sc_fns(qcbA, 0, slot, kb, ps[:, 0:512]), wait=wA)
            evSC[bi] = pe_emit(sc_fns(qcbB, 64, slot, kb, ps[:, 512:1024]))
            ab = attAB_v[bi % 3]
            evEX[bi] = P.emit(
                "scalar", lambda e, o=ab, i=ps, p=pi: e.activation(
                    out=o, in_=i, func=AF.Exp,
                    bias=biasp_sb[:, p:p + 1], scale=0.125),
                wait=emax(evSC[bi], evBP,
                          emax(evAVA[bi - 3], evAVB[bi - 3]) if bi >= 3
                          else prev_avlast),
                inc="se")
            if slot >= 7:
                mk = bass.AP(arena, MASK_O + m * 512,
                             [[ACOLS, 128], [0, 2], [1, 512]])
                ab3 = bass.AP(arena, ATTA_O + (bi % 3) * 1024,
                              [[ACOLS, 128], [512, 2], [1, 512]])
                evMK[bi] = P.emit(
                    "vector", lambda e, o=ab3, m_=mk: e.tensor_mul(o, o, m_),
                    wait=emax(evEX[bi]), inc="ve")
            else:
                evMK[bi] = evEX[bi]
            if bi >= 1:
                emit_avs(bi - 1)
        emit_avs(NB - 1)

        eZ1 = pe_emit(lambda e: e.matmul(
            psAV[0:65, 0:512], vS[:, 0:65], zero_v, start=False, stop=True),
            wait=[ev_zero, evAVA[NB - 1], evAVB[NB - 1]])
        eZ2 = pe_emit(lambda e: e.matmul(
            psAV[0:65, 512:1024], vS[:, 0:65], zero_v, start=False, stop=True))
        eZ3 = pe_emit(lambda e: e.matmul(
            psT0f[0:65, :], vS[:, 0:65], zero_v, start=False, stop=True))
        eZ4 = pe_emit(lambda e: e.matmul(
            psT1f[0:65, :], vS[:, 0:65], zero_v, start=False, stop=True))

        out_evs = []
        eR4 = None
        for role, hh in ((0, hA), (1, hB)):
            wz = emax(eZ4, out_evs[-1] if out_evs else None)
            if role == 0:
                eR0 = P.emit("scalar", lambda e: e.copy(
                    rbff_row, psAV[64:65, 0:1024]), wait=wz, inc="se")
            else:
                P.emit("scalar", lambda e: e.copy(
                    rbff_row[:, 0:512], psT0f[64:65, :]), wait=wz)
                eR0 = P.emit("scalar", lambda e: e.copy(
                    rbff_row[:, 512:1024], psT1f[64:65, :]), inc="se")
            wexp = [eR0, prev_rb_free,
                    emax(evEX[NB - 2], evEX[NB - 1]) if role == 0 else None]
            eT_f = None
            for hi in range(8):
                eT_f = pe_emit(lambda e, hi=hi: e.transpose(
                    psW[1][0:128, hi:hi + 1],
                    rbff_row[:, 128 * hi:128 * (hi + 1)], idf_sb[0:1, 0:1]),
                    wait=wexp if hi == 0 else ())
            eR1 = P.emit("vector", lambda e: e.reciprocal(
                recip8_sb[:], psW[1][:, 0:8]), wait=emax(eT_f), inc="ve")
            eT_b = None
            for hi in range(8):
                eT_b = pe_emit(lambda e, hi=hi: e.transpose(
                    psW[1][0:1, 128 * hi:128 * (hi + 1)],
                    recip8_sb[:, hi:hi + 1], idf_sb[:]),
                    wait=[eR1] if hi == 0 else ())
            eR1b = P.emit("scalar", lambda e: e.copy(rbf2_row, psW[1][0:1, 0:1024]),
                          wait=emax(eT_b), inc="se")
            eR2 = pe_emit(lambda e: e.matmul(
                psF[2][0:64, :], pones_sb[0:1, 0:64],
                arena[0:1, RBF2_O:RBF2_O + 512], start=True, stop=True),
                wait=[eR1b])
            eR3 = pe_emit(lambda e: e.matmul(
                psF[3][0:64, :], pones_sb[0:1, 0:64],
                arena[0:1, RBF2_O + 512:RBF2_O + 1024], start=True, stop=True))
            P.emit("vector", lambda e: e.tensor_copy(rbsb[:, 0:512], psF[2][0:64, :]),
                   wait=emax(eR2, eR3))
            eR4 = P.emit("vector", lambda e: e.tensor_copy(
                rbsb[:, 512:1024], psF[3][0:64, :]), inc="ve")
            prev_rb_free = emax(eR4)
            col = ATTN_OT + (hh // 2) * 1024
            if role == 0:
                dst = arena[0:64, col:col + 1024]
                evO = P.emit("vector", lambda e, d=dst: e.tensor_mul(
                    d, psAV[0:64, 0:1024], rbsb[:]), wait=emax(eR4), inc="ve")
                out_evs.append(evO)
                ev_attn_done_v = evO
            else:
                evO = P.emit("vector", lambda e: e.tensor_mul(
                    atmp[:, 0:512], psT0f[0:64, :], rbsb[:, 0:512]),
                    wait=emax(eR4, prev_atmp), inc="ve")
                evO = P.emit("vector", lambda e: e.tensor_mul(
                    atmp[:, 512:1024], psT1f[0:64, :], rbsb[:, 512:1024]),
                    inc="ve")
                dst = arena[64:128, col:col + 1024]
                evO2 = P.emit("scalar", lambda e, d=dst: e.copy(d, atmp),
                              wait=emax(evO), inc="se")
                prev_atmp = evO2
                out_evs.append(emax(evO, evO2))
                ev_attn_done_v = evO
                ev_attn_done_s = evO2
        prev_av_free = emax(*out_evs)
        prev_avlast = emax(evAVA[NB - 1], evAVB[NB - 1])
        prev_sc_free = [emax(evEX[NB - 2]), emax(eR4, out_evs[-1])]

    def attn_pe_block(e):
        for ci in range(8):
            with e.If_eq(REGS["cid"], ci):
                last = {}
                for w, fn in pe_ops[ci]:
                    for s, v in w:
                        if last.get(s, 0) < v:
                            e.wait_ge(semh[s], v)
                            last[s] = v
                    fn(e).then_inc(semh["pe"], 1)
        return None
    P.emit("tensor", attn_pe_block)

    ev_attn_all = emax(ev_attn_done_v, ev_attn_done_s)

    gw_v = [av_(GW_O + i * 4096, 4096) for i in range(2)]
    uw_v = [av_(UW_O + i * 4096, 4096) for i in range(2)]
    gst = av_(GST_O, 1024)
    ust = av_(UST_O, 1024)

    dump_now("attn_oT", ev_attn_all)
    evGW = {}
    evUW = {}
    evGMM = {}
    evUMM = {}
    evHM = {}
    gfree = {0: None, 1: None}
    ufree = {0: None, 1: None}
    last_gmm = None
    last_umm = None
    last_hm = None
    for ic in range(22):
        evGW[ic] = P.emit("gpsimd", lambda e, ic=ic: e.dma_start(
            out=gw_v[ic % 2], in_=gw_h[ic]),
            wait=emax(evGMM.get(ic - 2), ev_attn_all if ic < 2 else None,
                      dbg_gate[0] if ic == 0 else None),
            inc="dw", n=16)
        evUW[ic] = P.emit("gpsimd", lambda e, ic=ic: e.dma_start(
            out=uw_v[ic % 2], in_=uw_h[ic]),
            wait=emax(evUMM.get(ic - 2)), inc="dw", n=16)
        for it2 in range(2):
            itile = 2 * ic + it2
            for tb in range(2):
                pi = (2 * it2 + tb) % 2
                pg = psF[pi]
                pu = psF[2 + pi]
                last = None
                for j in range(HT):
                    lhsT = av_(GW_O + (ic % 2) * 4096 + j * 256 + it2 * 128, 128)
                    last = P.emit("tensor", lambda e, l=lhsT, j=j, p=pg, tb=tb: e.matmul(
                        p[:], l, av_(XNT_OWN + j * 1024 + tb * 512, 512),
                        start=(j == 0), stop=(j == HT - 1)),
                        wait=emax(evGW[ic], gfree[pi],
                                  ev_attn_all if ic == 0 else None),
                        inc="pe" if j == HT - 1 else None)
                last_gmm = evGMM[ic] = last
                last = None
                for j in range(HT):
                    lhsT = av_(UW_O + (ic % 2) * 4096 + j * 256 + it2 * 128, 128)
                    last = P.emit("tensor", lambda e, l=lhsT, j=j, p=pu, tb=tb: e.matmul(
                        p[:], l, av_(XNT_OWN + j * 1024 + tb * 512, 512),
                        start=(j == 0), stop=(j == HT - 1)),
                        wait=emax(evUW[ic], ufree[pi]),
                        inc="pe" if j == HT - 1 else None)
                last_umm = evUMM[ic] = last
                esil = P.emit(
                    "scalar", lambda e, p=pg, tb=tb, itile=itile: e.activation(
                        out=gst[:, tb * 512:(tb + 1) * 512], in_=p[:], func=AF.Silu,
                        scale=gs_sb[:, itile:itile + 1]),
                    wait=emax(last_gmm, evHM.get((itile - 1, tb))),
                    inc="se")
                gfree[pi] = esil
                eue = P.emit(
                    "vector", lambda e, p=pu, tb=tb, itile=itile: e.tensor_scalar_mul(
                        ust[:, tb * 512:(tb + 1) * 512], p[:],
                        us_sb[:, itile:itile + 1]),
                    wait=emax(last_umm, evHM.get((itile - 1, tb))),
                    inc="ve")
                ufree[pi] = eue
                last_hm = evHM[(itile, tb)] = P.emit(
                    "vector", lambda e, tb=tb, itile=itile: e.tensor_mul(
                        av_(H_OFF + itile * 1024 + tb * 512, 512),
                        gst[:, tb * 512:(tb + 1) * 512],
                        ust[:, tb * 512:(tb + 1) * 512]),
                    wait=emax(esil, eue), inc="ve")

    ev_ffn_mm_done = emax(last_gmm, last_umm)

    dump_now("hdbg", emax(last_hm))

    xq_v = [avf(XQ_O + a * 4096, 2048) for a in range(4)]
    ost_v = [avf(OST_O + i * 1024, 512) for i in range(2)]

    evXQ = {}
    evWXMM = {}
    evDNMM = {}
    evOST = {}
    evODMA = {}
    evE1 = {}
    ev_out = None
    for tb in range(2):
        for a in range(4):
            trow = (28 + tb * 4 + a) * 128
            evXQ[(tb, a)] = P.emit("sync", lambda e, a=a, trow=trow: e.dma_start(
                out=xq_v[a], in_=xkv[trow:trow + 128, :]),
                wait=emax(ev_ffn_mm_done, dbg_gate[0],
                          evWXMM.get((tb - 1, 15)) if tb >= 1 else None),
                inc="dx", n=16)
        for H in range(16):
            p2 = H % 2
            evDW = P.emit("gpsimd", lambda e, H=H, p2=p2: e.dma_start(
                out=av_(DW_O + p2 * 5632, 5632), in_=dw_h[H]),
                wait=emax(evE1.get((tb, H - 2)), ev_ffn_mm_done,
                          evDNMM.get((tb - 1, 14 + H)) if tb >= 1 else None),
                inc="dw", n=16)
            evWO = P.emit("gpsimd", lambda e, H=H, p2=p2: e.dma_start(
                out=av_(WO_O + p2 * 2048, 2048), in_=wo_h[H]),
                wait=emax(evWXMM.get((tb, H - 2)),
                          evWXMM.get((tb - 1, 14 + H)) if tb >= 1 else None),
                inc="dw", n=16)
            pw = psF[p2]
            pd = psF[2 + p2]
            last = None
            for j in range(HT):
                lhsT = av_(WO_O + p2 * 2048 + j * 128, 128)
                last = P.emit("tensor", lambda e, l=lhsT, j=j, p=pw, tb=tb: e.matmul(
                    p[:], l, av_(ATTN_OT + j * 1024 + tb * 512, 512),
                    start=(j == 0), stop=False),
                    wait=emax(evWO, ev_attn_all if (tb == 0 and H == 0) else None,
                              evOST.get((tb, H - 2)),
                              evOST.get((tb - 1, 14 + H)) if tb >= 1 else None),
                    inc=None)
            for a in range(4):
                last = P.emit("tensor", lambda e, a=a, H=H, p=pw: e.matmul(
                    p[:, a * 128:(a + 1) * 128],
                    xq_v[a][:, H * 128:(H + 1) * 128], idf_sb[:],
                    is_transpose=True, start=False, stop=(a == 3)),
                    wait=emax(evXQ[(tb, a)]), inc="pe" if a == 3 else None)
            evWXMM[(tb, H)] = last
            last = None
            for r in range(ITL):
                lhsT = av_(DW_O + p2 * 5632 + r * 128, 128)
                last = P.emit("tensor", lambda e, l=lhsT, r=r, p=pd, tb=tb: e.matmul(
                    p[:], l, av_(H_OFF + r * 1024 + tb * 512, 512),
                    start=(r == 0), stop=(r == ITL - 1)),
                    wait=emax(evDW, last_hm if (tb == 0 and H == 0) else None,
                              evE1.get((tb, H - 2)),
                              evE1.get((tb - 1, 14 + H)) if tb >= 1 else None),
                    inc="pe" if r == ITL - 1 else None)
            evDNMM[(tb, H)] = last
            e1 = P.emit("vector", lambda e, p=pd, p2=p2, H=H: e.tensor_scalar_mul(
                tmp4_sb[:, p2 * 512:(p2 + 1) * 512], p[:], ds_sb[:, H:H + 1]),
                wait=emax(evDNMM[(tb, H)]), inc="ve")
            evE1[(tb, H)] = e1
            evOST[(tb, H)] = P.emit("vector", lambda e, p=pw, p2=p2: e.tensor_add(
                ost_v[p2][:], p[:], tmp4_sb[:, p2 * 512:(p2 + 1) * 512]),
                wait=emax(evWXMM[(tb, H)], e1,
                          evODMA.get((tb, H - 2)),
                          evODMA.get((tb - 1, 14 + H)) if tb >= 1 else None),
                inc="ve")
            evODMA[(tb, H)] = P.emit("sync", lambda e, p2=p2, H=H, tb=tb: e.dma_start(
                out=out_t[H * 128:(H + 1) * 128, tb * 512:(tb + 1) * 512],
                in_=ost_v[p2][:]),
                wait=emax(evOST[(tb, H)]), inc="dout", n=16)

    P.emit("sync", None, wait=[("dout", P.cnt["dout"])])

    with nc.Block() as block:

        @block.sync
        def _(e):
            P.replay("sync", e, semh)

        @block.gpsimd
        def _(e):
            P.replay("gpsimd", e, semh)

        @block.tensor
        def _(e):
            P.replay("tensor", e, semh)

        @block.vector
        def _(e):
            P.replay("vector", e, semh)

        @block.scalar
        def _(e):
            P.replay("scalar", e, semh)

    return nc, ctx


def _tile_w(w, rows_per):
    R = w.shape[0]
    return np.ascontiguousarray(
        w.T.reshape(HT, 128, R).transpose(1, 0, 2).reshape(128, HT * R))


def _prep_shared(w_qkv, w_o, gate_w, up_w, down_w, gate_s, up_s, down_s):
    d = {}
    d["wkv_h"] = _tile_w(w_qkv[2048:2560, :], None)
    d["wq_h"] = np.stack([_tile_w(w_qkv[oc * 256:(oc + 1) * 256, :], None)
                          for oc in range(8)])
    d["wo_h"] = np.stack([
        np.ascontiguousarray(w_o[H * 128:(H + 1) * 128, :].T
                             .reshape(HT, 128, 128).transpose(1, 0, 2)
                             .reshape(128, 2048))
        for H in range(16)])
    d["gw_h"] = np.stack([
        np.ascontiguousarray(gate_w[ic * 256:(ic + 1) * 256, :].T
                             .reshape(HT, 128, 256).transpose(1, 0, 2)
                             .reshape(128, 4096))
        for ic in range(22)])
    d["uw_h"] = np.stack([
        np.ascontiguousarray(up_w[ic * 256:(ic + 1) * 256, :].T
                             .reshape(HT, 128, 256).transpose(1, 0, 2)
                             .reshape(128, 4096))
        for ic in range(22)])
    d["dw_h"] = np.stack([
        np.ascontiguousarray(down_w[H * 128:(H + 1) * 128, :].T
                             .reshape(ITL, 128, 128).transpose(1, 0, 2)
                             .reshape(128, 5632))
        for H in range(16)])
    d["gs_h"] = np.ascontiguousarray(gate_s.reshape(ITL, 128).T)
    d["us_h"] = np.ascontiguousarray(up_s.reshape(ITL, 128).T)
    d["ds_h"] = np.ascontiguousarray(down_s.reshape(16, 128).T)
    d["idb_h"] = np.eye(128, dtype=np.float32)
    d["idf_h"] = np.eye(128, dtype=np.float32)
    d["pones_h"] = np.ones((1, 128), dtype=np.float32)
    d["mones_h"] = -np.ones((1, 128), dtype=np.float32)
    mk = np.zeros((128, 4 * 512), dtype=np.float32)
    i = np.arange(512)
    j = np.arange(128)
    for m in range(4):
        mk[:, m * 512:(m + 1) * 512] = (128 * m + j[:, None] <= i[None, :])
    d["mask_h"] = mk
    return d


def _rope_tab(pos):
    inv = 1.0 / (10000.0 ** (np.arange(0, DH, 2, dtype=np.float64) / DH))
    ang = pos[:, None].astype(np.float64) * inv[None, :]
    ntile = len(pos) // 128
    tab = np.empty((128, ntile * 64), dtype=np.float32)
    c = np.cos(ang).astype(np.float32)
    s = np.sin(ang).astype(np.float32)
    for t in range(ntile):
        tab[:, t * 64:t * 64 + 32] = c[t * 128:(t + 1) * 128]
        tab[:, t * 64 + 32:t * 64 + 64] = s[t * 128:(t + 1) * 128]
    return tab


def _core_geometry(c):
    chunks = [(0, j) for j in range(c)] + [(1, j) for j in range(7 - c)] + \
             [(0, c), (1, 7 - c)]
    sel = np.zeros((1, 8), dtype=np.int32)
    for k in range(7):
        sel[0, k] = 0 if k < c else 512
    sel[0, 7] = c
    return chunks, sel


def kernel(L0_attn_hidden_states, L0_attn_w_qkv, L0_attn_w_o,
           L0_ffn_gate_w, L0_ffn_up_w, L0_ffn_down_w,
           L0_ffn_gate_s, L0_ffn_up_s, L0_ffn_down_s):
    x = np.ascontiguousarray(np.asarray(L0_attn_hidden_states, dtype=np.float32)
                             .reshape(2 * SEQ, HID))
    shared = _prep_shared(np.asarray(L0_attn_w_qkv, np.float32),
                          np.asarray(L0_attn_w_o, np.float32),
                          np.asarray(L0_ffn_gate_w, np.float32),
                          np.asarray(L0_ffn_up_w, np.float32),
                          np.asarray(L0_ffn_down_w, np.float32),
                          np.asarray(L0_ffn_gate_s, np.float32),
                          np.asarray(L0_ffn_up_s, np.float32),
                          np.asarray(L0_ffn_down_s, np.float32))

    in_maps = []
    for c in range(NCORE):
        chunks, sel = _core_geometry(c)
        rows = []
        pos = []
        for bt, j in chunks:
            rows.append(x[bt * SEQ + j * 512: bt * SEQ + (j + 1) * 512])
            pos.append(np.arange(j * 512, (j + 1) * 512))
        xkv = np.ascontiguousarray(np.concatenate(rows, 0))
        ctab = _rope_tab(np.concatenate(pos))
        m = dict(shared)
        m["xkv"] = xkv
        m["ctok_h"] = ctab
        m["sel_h"] = sel
        in_maps.append(m)

    nc, ctx = build_program()
    try:
        res = run_bass_kernel_spmd(nc, in_maps, core_ids=list(range(NCORE)))
    finally:
        ctx.close()

    out = np.empty((2 * SEQ, HID), dtype=np.float32)
    for c in range(NCORE):
        ot = np.asarray(res.results[c]["out_t"])
        out[c * 512:(c + 1) * 512] = ot[:, 0:512].T
        out[SEQ + (7 - c) * 512: SEQ + (8 - c) * 512] = ot[:, 512:1024].T
    return out.reshape(2, SEQ, HID)
